# version 02
# baseline (speedup 1.0000x reference)
import contextlib
import numpy as np
import ml_dtypes
import concourse.bass as bass
import concourse.mybir as mybir
from concourse.bass_utils import run_bass_kernel_spmd

F32 = mybir.dt.float32
BF16 = mybir.dt.bfloat16
AF = mybir.ActivationFunctionType
ALU = mybir.AluOpType
AX = mybir.AxisListType
BF = ml_dtypes.bfloat16

L = 16384
DM = 2048
NH = 16
ITR = 8192
NCORE = 8
TC = L // NCORE
HALO_B = 1024
EXTP = 3584
NEC = EXTP // 128
SEQ_SHIFTS = [0, 1, -1, 3, -3, 7, -7, 20, -20, 53, -53, 141, -141, 380, -380, 1024]
OFFS = sorted(-s for s in SEQ_SHIFTS)
COFFS = [-8, -3, -2, -1, 0, 1, 2, 3]
NCO = len(COFFS)
CHAN = [0, 1, 2, 4, 8]
SCALE = float(128) ** -0.5
RMS_EPS = 1e-6
LN_EPS = 1e-5
NLT = TC // 128
RING = 13
NHG = 4
BSHIFT = 40.0

DEBUG = set()
PHASES = ("P1", "P2K", "P2V", "P3", "P4", "P5", "P6", "P7")


NDMASLOT = 8


class _Node:
    __slots__ = ("eng", "sem", "emit", "reads", "writes", "deps", "idx",
                 "marked", "val", "tag", "is_dma", "extra", "waits")

    def __init__(self, eng, sem, emit, reads, writes, idx, tag, is_dma):
        self.eng = eng
        self.sem = sem
        self.emit = emit
        self.reads = tuple(reads)
        self.writes = tuple(writes)
        self.idx = idx
        self.deps = ()
        self.marked = False
        self.val = 0
        self.tag = tag
        self.is_dma = is_dma
        self.extra = ()


class Prog:
    ENGS = ("g", "sy", "t", "v", "s")
    DMA_SEMS = tuple(f"dg{k}" for k in range(NDMASLOT)) + \
        tuple(f"dsy{k}" for k in range(NDMASLOT))
    BAR_SEMS = ("bg", "bsy", "bt", "bv", "bs")

    def __init__(self):
        self.nodes = []
        self._bn = 0
        self._dmacnt = {"dg": 0, "dsy": 0}
        self._dmahist = {"dg": [], "dsy": []}
        self._lasteng = {}

    def add(self, eng, emit, reads=(), writes=(), dma=False, tag="", sem=None):
        reads = tuple(reads)
        if dma:
            q = "dg" if eng == "g" else "dsy"
            sem = f"{q}{self._dmacnt[q] % NDMASLOT}"
            self._dmacnt[q] += 1
            reads = reads + (("qtail", q),)
        elif sem is None:
            sem = eng
        n = _Node(eng, sem, emit, reads, writes, len(self.nodes), tag, dma)
        if dma:
            hist = self._dmahist[q]
            if len(hist) >= NDMASLOT:
                n.extra = (hist[-NDMASLOT],)
            hist.append(n)
        if emit is not None:
            self._lasteng[eng] = n
        self.nodes.append(n)
        self._dirty = True
        return n

    def barrier(self):
        if not getattr(self, "_dirty", False):
            return
        self._dirty = False
        bn = self._bn
        self._bn += 1
        flags = []
        for e in self.ENGS:
            w = [("bflag", bn, e)]
            if e == "g":
                w.append(("qtail", "dg"))
            if e == "sy":
                w.append(("qtail", "dsy"))
            n = self.add(e, None, writes=w, tag=f"bar1_{bn}_{e}", sem=f"b{e}")
            if e in self._lasteng:
                n.extra = n.extra + (self._lasteng[e],)
            flags.append(("bflag", bn, e))
        for e in self.ENGS:
            self.add(e, None, reads=flags, tag=f"bar2_{bn}_{e}", sem=f"b{e}")

    def resolve(self):
        last_w = {}
        readers = {}
        for n in self.nodes:
            deps = set(n.extra)
            for r in n.reads:
                w = last_w.get(r)
                if w is not None:
                    deps.add(w)
            for r in n.writes:
                w = last_w.get(r)
                if w is not None:
                    deps.add(w)
                deps.update(readers.get(r, ()))
            deps.discard(n)
            n.deps = tuple(deps)
            for r in n.reads:
                readers.setdefault(r, set()).add(n)
            for r in n.writes:
                last_w[r] = n
                readers[r] = {n}
        for n in self.nodes:
            if n.emit is not None:
                n.marked = True
            for d in n.deps:
                d.marked = True
        cum = {}
        for n in self.nodes:
            if n.marked:
                inc = 16 if n.is_dma else 1
                cum[n.sem] = cum.get(n.sem, 0) + inc
                n.val = cum[n.sem]
        vc = [None] * len(self.nodes)
        know = {e: {} for e in self.ENGS}
        for n in self.nodes:
            k = know[n.eng]
            waits = []
            for d in sorted(n.deps, key=lambda d_: (d_.sem, d_.val)):
                if k.get(d.sem, 0) >= d.val:
                    continue
                waits.append((d.sem, d.val))
                dv = vc[d.idx]
                if dv:
                    for s_, v_ in dv.items():
                        if k.get(s_, 0) < v_:
                            k[s_] = v_
                if k.get(d.sem, 0) < d.val:
                    k[d.sem] = d.val
            n.waits = waits
            if n.marked:
                k2 = dict(k)
                k2[n.sem] = n.val
                vc[n.idx] = k2
        per_eng = {e: [] for e in self.ENGS}
        for n in self.nodes:
            per_eng[n.eng].append(n)
        ptr = {e: 0 for e in per_eng}
        done = [False] * len(self.nodes)
        progress = True
        ndone = 0
        while progress:
            progress = False
            for e, lst in per_eng.items():
                while ptr[e] < len(lst):
                    n = lst[ptr[e]]
                    if all(done[d.idx] for d in n.deps):
                        done[n.idx] = True
                        ptr[e] += 1
                        ndone += 1
                        progress = True
                    else:
                        break
        if ndone != len(self.nodes):
            stuck = [n for n in self.nodes if not done[n.idx]][:5]
            info = "; ".join(
                f"{n.eng}#{n.idx}:{n.tag} waits " +
                ",".join(f"{d.eng}#{d.idx}:{d.tag}" for d in n.deps
                         if not done[d.idx]) for n in stuck)
            raise RuntimeError(f"deadlock in schedule: {info}")
        return per_eng

    def emit(self, nc, sems):
        per_eng = self.resolve()

        def run_engine(eng_obj, ename):
            for n in per_eng[ename]:
                for s_, v_ in n.waits:
                    eng_obj.wait_ge(sems[s_], v_)
                ins = n.emit(eng_obj) if n.emit is not None else None
                if n.marked:
                    if ins is None:
                        ins = eng_obj.nop()
                    ins.then_inc(sems[n.sem], 16 if n.is_dma else 1)
        return run_engine


def _hadamard(n):
    h = np.array([[1.0]])
    while h.shape[0] < n:
        h = np.block([[h, h], [h, -h]])
    return h


def make_consts():
    c = {}
    c["h128b_in"] = _hadamard(128).astype(BF)
    c["hfoldup_in"] = np.kron(_hadamard(16), np.eye(8)).astype(BF)
    c["hfolddn_in"] = np.kron(_hadamard(64), np.eye(2)).astype(BF)
    pcs = np.zeros((128, 5, 128), np.float32)
    f = np.arange(128)
    for ci, cs in enumerate(CHAN):
        pcs[f, ci, (f - cs) % 128] = 1.0
    c["pcs_in"] = pcs.astype(BF)
    offs = set(OFFS)
    m = np.zeros((128, NCO, 128), np.float32)
    for ci, co in enumerate(COFFS):
        for p in range(128):
            for q in range(128):
                if co * 128 + p - q in offs:
                    m[p, ci, q] = 1.0
    c["masks_in"] = m.astype(BF)
    c["identf_in"] = np.eye(128, dtype=np.float32)
    c["ones_cb_in"] = np.ones((128, 1), BF)
    c["ones_rf_in"] = np.ones((1, 128), np.float32)
    c["epsv_in"] = np.array([[RMS_EPS, LN_EPS]], np.float32)
    return c


COL = {}
_o = 0
for _n, _w in [("bq", 16), ("bk", 16), ("gq", 16), ("gk", 16), ("bo", 16),
               ("su_ug", 16), ("wsd", 16), ("su_dp", 64), ("wsu", 64), ("wsg", 64)]:
    COL[_n] = _o
    _o += _w
NCOLS = _o


def build(debug=(), phases=PHASES):
    debug = set(debug)
    phases = set(phases)
    nc = bass.Bass("TRN2", target_bir_lowering=False)

    def dram(name, shape, dtype, kind="ExternalInput"):
        return nc.dram_tensor(name, shape, dtype, kind=kind).ap()

    def scr(name, shape, dtype):
        kind = "ExternalOutput" if name in debug else "Internal"
        return nc.dram_tensor(name, shape, dtype, kind=kind).ap()

    xT = dram("xT", [DM, EXTP], F32)
    wq_p = dram("wq_p", [16, 16, 128, 128], F32)
    wk_p = dram("wk_p", [16, 16, 128, 128], F32)
    wv_p = dram("wv_p", [16, 128, DM], F32)
    wo_p = dram("wo_p", [16, 16, 128, 128], F32)
    fup_p = dram("fup_p", [16, 128, ITR], F32)
    fgt_p = dram("fgt_p", [16, 128, ITR], F32)
    fdn_p = dram("fdn_p", [64, 128, DM], F32)
    colpk = dram("colpk", [128, NCOLS], F32)
    bv_row = dram("bv_row", [1, DM], F32)
    wmix_in = dram("wmix_in", [1, 5], F32)
    consts = make_consts()
    CIN = {k: dram(k, list(v.shape), BF16 if v.dtype == BF else F32)
           for k, v in consts.items()}
    out_t = dram("out", [TC, DM], F32, kind="ExternalOutput")

    qt_s = scr("qt_s", [16, 128, TC], BF16)
    kt_s = scr("kt_s", [NEC, 128, NH, 128], BF16)
    v_s = scr("v_s", [NEC, 128, NH, 128], BF16)
    y1_s = scr("y1_s", [16, 128, TC], F32)
    w1_s = scr("w1_s", [64, 128, TC], BF16)
    gup_s = scr("gup_s", [16, 128, ITR], BF16)
    ggt_s = scr("ggt_s", [16, 128, ITR], BF16)
    gdn_s = scr("gdn_s", [64, 128, DM], BF16)
    ao_s = scr("ao_s", [NH, 128, TC], BF16)
    dbg_ao = "ao_s" in debug

    P = Prog()
    xTv = xT.rearrange("(c p) n -> p c n", p=128)

    with contextlib.ExitStack() as ctx:
        SB = lambda n_, s_, d_: ctx.enter_context(nc.sbuf_tensor(n_, s_, d_))
        sem_h = {k: ctx.enter_context(nc.semaphore(k))
                 for k in ("g", "sy", "t", "v", "s") + Prog.DMA_SEMS + Prog.BAR_SEMS}

        cols = SB("cols", [128, NCOLS], F32)
        mixh = SB("mixh", [128, NH, 128], BF16)
        h128b = SB("h128b", [128, 128], BF16)
        masks = SB("masks", [128, NCO, 128], BF16)
        Bcols = SB("Bcols", [128, 16], F32)
        maxq2 = SB("maxq2", [1, 16], F32)
        maxk2 = SB("maxk2", [1, 16], F32)
        ones_cb = SB("ones_cb", [128, 1], BF16)
        ones_rf = SB("ones_rf", [1, 128], F32)
        epst = SB("epst", [1, 2], F32)
        srow = SB("srow", [1, 512], F32)
        srow2 = SB("srow2", [1, 512], F32)
        rtmp = SB("rtmp", [1, 8], F32)

        hfoldup = SB("hfoldup", [128, 128], BF16)
        hfolddn = SB("hfolddn", [128, 128], BF16)
        identf = SB("identf", [128, 128], F32)
        bvrep = SB("bvrep", [128, DM], F32)
        p0ctx = contextlib.ExitStack()
        wmix = p0ctx.enter_context(nc.sbuf_tensor("wmix", [1, 8], F32))
        pcs_t = p0ctx.enter_context(nc.sbuf_tensor("pcs_t", [128, 5, 128], BF16))
        wrep = p0ctx.enter_context(nc.sbuf_tensor("wrep", [128, 8], F32))
        mbase = p0ctx.enter_context(nc.sbuf_tensor("mbase", [128, 128], BF16))
        bvrow_t = p0ctx.enter_context(nc.sbuf_tensor("bvrow_t", [1, DM], F32))
        ps_gen = p0ctx.enter_context(nc.psum_tensor("ps_gen", [128, 1024], F32))

        for dst, src, rs in [
                (cols[:, :], colpk, "cols"), (wmix[:, 0:5], wmix_in, "wmix"),
                (pcs_t[:, :, :], CIN["pcs_in"], "pcs"),
                (ones_cb[:, :], CIN["ones_cb_in"], "ones_cb"),
                (ones_rf[:, :], CIN["ones_rf_in"], "ones_rf"),
                (epst[:, :], CIN["epsv_in"], "epst"),
                (h128b[:, :], CIN["h128b_in"], "h128b"),
                (masks[:, :, :], CIN["masks_in"], "masks"),
                (hfoldup[:, :], CIN["hfoldup_in"], "hfoldup"),
                (hfolddn[:, :], CIN["hfolddn_in"], "hfolddn"),
                (identf[:, :], CIN["identf_in"], "identf"),
                (bvrow_t[:, :], bv_row, "bvrow")]:
            P.add("g", (lambda d_, s_: lambda g: g.dma_start(out=d_, in_=s_))(dst, src),
                  writes=[rs], dma=True, tag=f"ld_{rs}")
        P.add("s", lambda s: s.mul(wmix[:, 0:5], wmix[:, 0:5], SCALE),
              reads=["wmix"], writes=["wmix"], tag="wmix_scale")
        P.add("t", lambda t: t.matmul(ps_gen[:, 0:5], ones_rf[:, :], wmix[:, 0:5],
                                      start=True, stop=True),
              reads=["wmix", "ones_rf"], writes=["psg0"], tag="wrep_mm")
        P.add("t", lambda t: t.matmul(ps_gen[:, 512:1024], ones_rf[:, :],
                                      bvrow_t[:, 0:512], start=True, stop=True),
              reads=["bvrow", "ones_rf"], writes=["psg1"], tag="bvrep_mm0")

        P.add("v", lambda v: v.tensor_copy(wrep[:, 0:5], ps_gen[:, 0:5]),
              reads=["psg0"], writes=["wrep"], tag="wrep_cp")
        P.add("v", lambda v: v.tensor_scalar(mbase[:, :], pcs_t[:, 0, :], wrep[:, 0:1],
                                             None, ALU.mult),
              reads=["wrep", "pcs"], writes=["mbase"], tag="mb0")
        for ci in range(1, 5):
            P.add("v", (lambda ci_: lambda v: v.scalar_tensor_tensor(
                mbase[:, :], pcs_t[:, ci_, :], wrep[:, ci_:ci_ + 1], mbase[:, :],
                ALU.mult, ALU.add))(ci),
                reads=["wrep", "pcs", "mbase"], writes=["mbase"], tag=f"mb{ci}")
        for h in range(NH):
            P.add("v", (lambda h_: lambda v: v.tensor_scalar(
                mixh[:, h_, :], mbase[:, :],
                cols[:, COL["gk"] + h_:COL["gk"] + h_ + 1], None, ALU.mult))(h),
                reads=["mbase", "cols"], writes=["mixh"], tag=f"mixh{h}")
        P.add("v", lambda v: v.memset(maxq2[:, :], 0.0), writes=["maxq2"], tag="z_mq")
        P.add("v", lambda v: v.memset(maxk2[:, :], 0.0), writes=["maxk2"], tag="z_mk")
        for q in range(4):
            if q > 0:
                P.add("t", (lambda q_: lambda t: t.matmul(
                    ps_gen[:, 512:1024], ones_rf[:, :], bvrow_t[:, q_ * 512:(q_ + 1) * 512],
                    start=True, stop=True))(q),
                    reads=["bvrow"], writes=["psg1"], tag=f"bvrep_mm{q}")
            P.add("v", (lambda q_: lambda v: v.tensor_copy(
                bvrep[:, q_ * 512:(q_ + 1) * 512], ps_gen[:, 512:1024]))(q),
                reads=["psg1"], writes=["bvrep"], tag=f"bvrep_cp{q}")
        P.barrier()
        p0ctx.close()


        if "P1" in phases:
            with contextlib.ExitStack() as px:
                PSB = lambda n_, s_, d_: px.enter_context(nc.sbuf_tensor(n_, s_, d_))
                PPS = lambda n_, s_: px.enter_context(nc.psum_tensor(n_, s_, F32))
                wqt = PSB("wqt", [128, 16, 16, 128], BF16)
                xtl = PSB("xtl", [128, 2, 16, 512], BF16)
                qraw = PSB("qraw", [128, 16, 512], F32)
                sqb = PSB("sqb", [128, 2, 512], BF16)
                qtt = PSB("qtt", [128, 2, 16, 512], BF16)
                pq = PPS("pq", [128, 1024])
                pstat = PPS("pstat", [1, 1024])
                prep = PPS("prep", [128, 512])

                for m in range(16):
                    P.add("g", (lambda m_: lambda g: g.dma_start(
                        out=wqt[:, m_, :, :], in_=wq_p[m_].transpose([1, 0, 2])))(m),
                        writes=[("wqt", m)], dma=True, tag=f"wq{m}")
                for tt in range(4):
                    c0 = HALO_B + tt * 512
                    P.add("g", (lambda tt_, c_: lambda g: g.dma_start(
                        out=xtl[:, tt_ % 2, :, :], in_=xTv[:, :, c_:c_ + 512]))(tt, c0),
                        writes=[("xtl", tt % 2)], dma=True, tag=f"x{tt}")
                    for m in range(16):
                        def qmm(tt_, m_):
                            def f(t):
                                sl = pq[:, (m_ % 2) * 512:(m_ % 2 + 1) * 512]
                                for k in range(16):
                                    ins = t.matmul(sl, wqt[:, m_, k, :], xtl[:, tt_ % 2, k, :],
                                                   start=(k == 0), stop=(k == 15))
                                return ins
                            return f
                        P.add("t", qmm(tt, m),
                              reads=[("wqt", m), ("xtl", tt % 2)],
                              writes=[("pq", m % 2)], tag=f"q{tt}_{m}")
                        def qev(tt_, m_):
                            def f(s):
                                sl = pq[:, (m_ % 2) * 512:(m_ % 2 + 1) * 512]
                                bcol = cols[:, COL["bq"] + m_:COL["bq"] + m_ + 1]
                                s.activation(qraw[:, m_, :], sl, AF.Identity, bias=bcol)
                                return s.activation(sqb[:, m_ % 2, :], sl, AF.Square, bias=bcol)
                            return f
                        P.add("s", qev(tt, m), reads=[("pq", m % 2), "cols"],
                              writes=[("qraw", m), ("sqb", m % 2)], tag=f"qev{tt}_{m}")
                        P.add("t", (lambda m_: lambda t: t.matmul(
                            pstat[:, 0:512], ones_cb[:, :], sqb[:, m_ % 2, :],
                            start=(m_ == 0), stop=(m_ == 15)))(m),
                            reads=[("sqb", m % 2), "ones_cb"], writes=["pstat0"],
                            tag=f"qssq{tt}_{m}")
                    P.add("s", lambda s: s.activation(srow[:, :], pstat[:, 0:512], AF.Sqrt,
                                                      bias=epst[:, 0:1], scale=1.0 / DM),
                          reads=["pstat0", "epst"], writes=["srow"], tag=f"sq_sqrt{tt}")
                    P.add("v", lambda v: v.reciprocal(srow[:, :], srow[:, :]),
                          reads=["srow"], writes=["srow"], tag=f"sq_rcp{tt}")
                    P.add("t", lambda t: t.matmul(prep[:, :], ones_rf[:, :], srow[:, :],
                                                  start=True, stop=True),
                          reads=["srow", "ones_rf"], writes=["prep"], tag=f"sq_rep{tt}")
                    for m in range(16):
                        P.add("v", (lambda tt_, m_: lambda v: v.scalar_tensor_tensor(
                            qtt[:, tt_ % 2, m_, :], qraw[:, m_, :],
                            cols[:, COL["gq"] + m_:COL["gq"] + m_ + 1], prep[:, :],
                            ALU.mult, ALU.mult))(tt, m),
                            reads=[("qraw", m), "prep", "cols"],
                            writes=[("qtt", tt % 2, m)], tag=f"qt{tt}_{m}")
                        P.add("s", (lambda tt_, m_: lambda s: s.activation(
                            sqb[:, m_ % 2, :], qtt[:, tt_ % 2, m_, :], AF.Square))(tt, m),
                            reads=[("qtt", tt % 2, m)], writes=[("sqb", m % 2)],
                            tag=f"nq_sq{tt}_{m}")
                        P.add("t", (lambda m_: lambda t: t.matmul(
                            pstat[:, 512:1024], ones_cb[:, :], sqb[:, m_ % 2, :],
                            start=True, stop=True))(m),
                            reads=[("sqb", m % 2), "ones_cb"], writes=["pstat1"],
                            tag=f"nq_mm{tt}_{m}")
                        P.add("v", lambda v: v.reduce_max(rtmp[:, 0:1],
                                                          pstat[:, 512:1024], AX.X),
                              reads=["pstat1"], writes=["rtmpq"], tag=f"nq_rm{tt}_{m}")
                        P.add("v", (lambda m_: lambda v: v.tensor_max(
                            maxq2[:, m_:m_ + 1], maxq2[:, m_:m_ + 1], rtmp[:, 0:1]))(m),
                            reads=["rtmpq", "maxq2"], writes=["maxq2"],
                            tag=f"nq_mx{tt}_{m}")
                    P.add("sy", (lambda tt_: lambda sy: sy.dma_start(
                        out=qt_s[:, :, tt_ * 512:(tt_ + 1) * 512].transpose([1, 0, 2]),
                        in_=qtt[:, tt_ % 2, :, :]))(tt),
                        reads=[("qtt", tt % 2, m) for m in range(16)],
                        writes=["qt_s"], dma=True, tag=f"qspill{tt}")

        P.barrier()
        if "P2K" in phases:
            with contextlib.ExitStack() as px:
                PSB = lambda n_, s_, d_: px.enter_context(nc.sbuf_tensor(n_, s_, d_))
                PPS = lambda n_, s_: px.enter_context(nc.psum_tensor(n_, s_, F32))
                wkt = PSB("wkt", [128, 16, 16, 128], BF16)
                xtl2 = PSB("xtl2", [128, 2, 16, 512], BF16)
                kraw = PSB("kraw", [128, 16, 512], BF16)
                kbb = PSB("kbb", [128, 2, 16, 512], BF16)
                sqb2 = PSB("sqb2", [128, 2, 512], BF16)
                kts = PSB("kts", [128, 2, 16, 512], BF16)
                pk = PPS("pk", [128, 1024])
                pstat2 = PPS("pstat2", [1, 1024])
                prep2 = PPS("prep2", [128, 512])
                pkt = PPS("pkt", [128, 1024])

                for m in range(16):
                    P.add("g", (lambda m_: lambda g: g.dma_start(
                        out=wkt[:, m_, :, :], in_=wk_p[m_].transpose([1, 0, 2])))(m),
                        writes=[("wkt", m)], dma=True, tag=f"wk{m}")
                for tt in range(7):
                    c0 = tt * 512
                    P.add("g", (lambda tt_, c_: lambda g: g.dma_start(
                        out=xtl2[:, tt_ % 2, :, :], in_=xTv[:, :, c_:c_ + 512]))(tt, c0),
                        writes=[("xtl2", tt % 2)], dma=True, tag=f"kx{tt}")
                    for m in range(16):
                        def kmm(tt_, m_):
                            def f(t):
                                sl = pk[:, (m_ % 2) * 512:(m_ % 2 + 1) * 512]
                                for k in range(16):
                                    ins = t.matmul(sl, wkt[:, m_, k, :], xtl2[:, tt_ % 2, k, :],
                                                   start=(k == 0), stop=(k == 15))
                                return ins
                            return f
                        P.add("t", kmm(tt, m), reads=[("wkt", m), ("xtl2", tt % 2)],
                              writes=[("pk", m % 2)], tag=f"k{tt}_{m}")
                        def kev(tt_, m_):
                            def f(s):
                                sl = pk[:, (m_ % 2) * 512:(m_ % 2 + 1) * 512]
                                bcol = cols[:, COL["bk"] + m_:COL["bk"] + m_ + 1]
                                s.activation(kraw[:, m_, :], sl, AF.Identity, bias=bcol)
                                return s.activation(sqb2[:, m_ % 2, :], sl, AF.Square, bias=bcol)
                            return f
                        P.add("s", kev(tt, m), reads=[("pk", m % 2), "cols"],
                              writes=[("kraw", m), ("sqb2", m % 2)], tag=f"kev{tt}_{m}")
                        P.add("t", (lambda m_: lambda t: t.matmul(
                            pstat2[:, 0:512], ones_cb[:, :], sqb2[:, m_ % 2, :],
                            start=(m_ == 0), stop=(m_ == 15)))(m),
                            reads=[("sqb2", m % 2), "ones_cb"], writes=["pstat20"],
                            tag=f"kssq{tt}_{m}")
                    P.add("s", lambda s: s.activation(srow2[:, :], pstat2[:, 0:512], AF.Sqrt,
                                                      bias=epst[:, 0:1], scale=1.0 / DM),
                          reads=["pstat20", "epst"], writes=["srow2"], tag=f"sk_sqrt{tt}")
                    P.add("v", lambda v: v.reciprocal(srow2[:, :], srow2[:, :]),
                          reads=["srow2"], writes=["srow2"], tag=f"sk_rcp{tt}")
                    P.add("t", lambda t: t.matmul(prep2[:, :], ones_rf[:, :], srow2[:, :],
                                                  start=True, stop=True),
                          reads=["srow2", "ones_rf"], writes=["prep2"], tag=f"sk_rep{tt}")
                    for m in range(16):
                        P.add("v", (lambda tt_, m_: lambda v: v.tensor_mul(
                            kbb[:, tt_ % 2, m_, :], kraw[:, m_, :], prep2[:, :]))(tt, m),
                            reads=[("kraw", m), "prep2"], writes=[("kbb", tt % 2, m)],
                            tag=f"kb{tt}_{m}")
                        def ktmm(tt_, m_):
                            def f(t):
                                sl = pkt[:, (m_ % 2) * 512:(m_ % 2 + 1) * 512]
                                return t.matmul(sl, mixh[:, m_, :], kbb[:, tt_ % 2, m_, :],
                                                start=True, stop=True)
                            return f
                        P.add("t", ktmm(tt, m), reads=[("kbb", tt % 2, m), "mixh"],
                              writes=[("pkt", m % 2)], tag=f"ktm{tt}_{m}")
                        def ktev(tt_, m_):
                            def f(s):
                                sl = pkt[:, (m_ % 2) * 512:(m_ % 2 + 1) * 512]
                                s.activation(kts[:, tt_ % 2, m_, :], sl, AF.Copy)
                                return s.activation(sqb2[:, m_ % 2, :], sl, AF.Square)
                            return f
                        P.add("s", ktev(tt, m), reads=[("pkt", m % 2)],
                              writes=[("kts", tt % 2, m), ("sqb2", m % 2)], tag=f"ktev{tt}_{m}")
                        P.add("t", (lambda m_: lambda t: t.matmul(
                            pstat2[:, 512:1024], ones_cb[:, :], sqb2[:, m_ % 2, :],
                            start=True, stop=True))(m),
                            reads=[("sqb2", m % 2), "ones_cb"], writes=["pstat21"],
                            tag=f"nk_mm{tt}_{m}")
                        P.add("v", lambda v: v.reduce_max(rtmp[:, 1:2],
                                                          pstat2[:, 512:1024], AX.X),
                              reads=["pstat21"], writes=["rtmpk"], tag=f"nk_rm{tt}_{m}")
                        P.add("v", (lambda m_: lambda v: v.tensor_max(
                            maxk2[:, m_:m_ + 1], maxk2[:, m_:m_ + 1], rtmp[:, 1:2]))(m),
                            reads=["rtmpk", "maxk2"], writes=["maxk2"],
                            tag=f"nk_mx{tt}_{m}")
                    for ei in range(4):
                        P.add("sy", (lambda tt_, ei_: lambda sy: sy.dma_start(
                            out=kt_s[tt_ * 4 + ei_],
                            in_=kts[:, tt_ % 2, :, ei_ * 128:(ei_ + 1) * 128]))(tt, ei),
                            reads=[("kts", tt % 2, m) for m in range(16)],
                            writes=["kt_s"], dma=True, tag=f"kspill{tt}_{ei}")

        P.barrier()
        if "P2V" in phases:
            with contextlib.ExitStack() as px:
                PSB = lambda n_, s_, d_: px.enter_context(nc.sbuf_tensor(n_, s_, d_))
                PPS = lambda n_, s_: px.enter_context(nc.psum_tensor(n_, s_, F32))
                wvt = PSB("wvt", [128, 16, DM], BF16)
                xtl3 = PSB("xtl3", [128, 2, 16, 512], BF16)
                vts = PSB("vts", [128, 2, 4, DM], BF16)
                pv = PPS("pv", [128, 2048])

                for k in range(16):
                    P.add("g", (lambda k_: lambda g: g.dma_start(
                        out=wvt[:, k_, :], in_=wv_p[k_]))(k),
                        writes=[("wvt", k)], dma=True, tag=f"wv{k}")
                for tt in range(7):
                    c0 = tt * 512
                    P.add("g", (lambda tt_, c_: lambda g: g.dma_start(
                        out=xtl3[:, tt_ % 2, :, :], in_=xTv[:, :, c_:c_ + 512]))(tt, c0),
                        writes=[("xtl3", tt % 2)], dma=True, tag=f"vx{tt}")
                    for ns in range(4):
                        def vmm(tt_, ns_):
                            def f(t):
                                for jb in range(4):
                                    sl = pv[:, jb * 512:(jb + 1) * 512]
                                    for k in range(16):
                                        ins = t.matmul(
                                            sl, xtl3[:, tt_ % 2, k, ns_ * 128:(ns_ + 1) * 128],
                                            wvt[:, k, jb * 512:(jb + 1) * 512],
                                            start=(k == 0), stop=(k == 15))
                                return ins
                            return f
                        P.add("t", vmm(tt, ns),
                              reads=[("xtl3", tt % 2)] + [("wvt", k) for k in range(16)],
                              writes=["pv"], tag=f"v{tt}_{ns}")
                        P.add("v", (lambda tt_, ns_: lambda v: v.tensor_add(
                            vts[:, tt_ % 2, ns_, :], pv[:, :], bvrep[:, :]))(tt, ns),
                            reads=["pv", "bvrep"], writes=[("vts", tt % 2, ns)],
                            tag=f"vev{tt}_{ns}")
                    for ei in range(4):
                        P.add("sy", (lambda tt_, ei_: lambda sy: sy.dma_start(
                            out=v_s[tt_ * 4 + ei_],
                            in_=vts[:, tt_ % 2, ei_, :].rearrange("p (h d) -> p h d", h=NH)))(tt, ei),
                            reads=[("vts", tt % 2, ei)],
                            writes=["v_s"], dma=True, tag=f"vspill{tt}_{ei}")

        P.barrier()
        if "P3" in phases:
            with contextlib.ExitStack() as aoctx:
                attnout = aoctx.enter_context(
                    nc.sbuf_tensor("attnout", [128, NH, NLT, 128], BF16))
                p3ctx = contextlib.ExitStack()
                P3SB = lambda n_, s_, d_: p3ctx.enter_context(nc.sbuf_tensor(n_, s_, d_))
                P3PS = lambda n_, s_: p3ctx.enter_context(nc.psum_tensor(n_, s_, F32))
                ringk = P3SB("ringk", [128, RING, NHG, 128], BF16)
                ringv = P3SB("ringv", [128, RING, NHG, 128], BF16)
                qtb = P3SB("qtb", [128, 2, NHG, 128], BF16)
                expb = P3SB("expb", [128, 2, NCO, 128], BF16)
                pband = P3PS("pband", [128, 2048])
                patt = P3PS("patt", [128, 1024])
                pden = P3PS("pden", [1, 512])
                prep3 = P3PS("prep3", [128, 512])
                recrow = P3SB("recrow", [1, 512], F32)
                reprow = P3SB("reprow", [128, 512], F32)

                P.add("v", lambda v: v.tensor_mul(srow[:, 0:16], maxq2[:, :], maxk2[:, :]),
                      reads=["maxq2", "maxk2"], writes=["srow"], tag="b_mul")
                P.add("s", lambda s: s.activation(srow[:, 0:16], srow[:, 0:16], AF.Sqrt),
                      reads=["srow"], writes=["srow"], tag="b_sqrt")
                P.add("v", lambda v: v.tensor_scalar(srow[:, 0:16], srow[:, 0:16], -1.0,
                                                     BSHIFT, ALU.mult, ALU.add),
                      reads=["srow"], writes=["srow"], tag="b_neg")
                P.add("t", lambda t: t.matmul(prep3[:, 0:16], ones_rf[:, :], srow[:, 0:16],
                                              start=True, stop=True),
                      reads=["srow", "ones_rf"], writes=["prep3"], tag="b_rep")
                P.add("v", lambda v: v.tensor_copy(Bcols[:, :], prep3[:, 0:16]),
                      reads=["prep3"], writes=["Bcols"], tag="b_cp")

                for hg in range(NH // NHG):
                    h0 = hg * NHG
                    loaded = set()

                    def ring_load(e, h0_=h0, hg_=hg):
                        P.add("sy", (lambda e_: lambda sy: sy.dma_start(
                            out=ringk[:, e_ % RING, :, :],
                            in_=kt_s[e_, :, h0_:h0_ + NHG, :]))(e),
                            reads=["kt_s"], writes=[("ringk", e % RING)],
                            dma=True, tag=f"rk{hg_}_{e}")
                        P.add("g", (lambda e_: lambda g: g.dma_start(
                            out=ringv[:, e_ % RING, :, :],
                            in_=v_s[e_, :, h0_:h0_ + NHG, :]))(e),
                            reads=["v_s"], writes=[("ringv", e % RING)],
                            dma=True, tag=f"rv{hg_}_{e}")
                        loaded.add(e)

                    for e in [0, 5, 6, 7, 8, 9, 10, 11]:
                        ring_load(e)
                    for lt in range(NLT):
                        for e in (lt, lt + 11):
                            if e not in loaded and 0 <= e < NEC:
                                ring_load(e)
                        P.add("sy", (lambda h0_, lt_: lambda sy: sy.dma_start(
                            out=qtb[:, lt_ % 2, :, :],
                            in_=qt_s[h0_:h0_ + NHG, :, lt_ * 128:(lt_ + 1) * 128]
                            .transpose([1, 0, 2])))(h0, lt),
                            reads=["qt_s"], writes=[("qtb", lt % 2)], dma=True,
                            tag=f"q3{hg}_{lt}")
                        for hh in range(NHG):
                            h = h0 + hh
                            uf = hh % 2
                            ecs = [lt + 8 + co for co in COFFS]
                            def smm(hh_, uf_, lt_, ecs_):
                                def f(t):
                                    for ci, e in enumerate(ecs_):
                                        sl = pband[:, uf_ * 1024 + ci * 128:
                                                   uf_ * 1024 + (ci + 1) * 128]
                                        ins = t.matmul(sl, ringk[:, e % RING, hh_, :],
                                                       qtb[:, lt_ % 2, hh_, :],
                                                       start=True, stop=True)
                                    return ins
                                return f
                            P.add("t", smm(hh, uf, lt, ecs),
                                  reads=[("ringk", e % RING) for e in ecs] +
                                        [("qtb", lt % 2)],
                                  writes=[("pband", uf)], tag=f"sc{hg}_{lt}_{hh}")
                            P.add("s", (lambda h_, uf_: lambda s: s.activation(
                                expb[:, uf_, :, :], pband[:, uf_ * 1024:(uf_ + 1) * 1024],
                                AF.Exp, bias=Bcols[:, h_:h_ + 1]))(h, uf),
                                reads=[("pband", uf), "Bcols"], writes=[("expb", uf)],
                                tag=f"ex{hg}_{lt}_{hh}")
                            P.add("v", (lambda uf_: lambda v: v.tensor_mul(
                                expb[:, uf_, :, :], expb[:, uf_, :, :], masks[:, :, :]))(uf),
                                reads=[("expb", uf), "masks"], writes=[("expb", uf)],
                                tag=f"mk{hg}_{lt}_{hh}")
                            def avmm(hh_, uf_, lt_, ecs_):
                                def f(t):
                                    po = (lt_ % 2) * 512
                                    for ci, e in enumerate(ecs_):
                                        ins = t.matmul(
                                            patt[:, po + hh_ * 128:po + (hh_ + 1) * 128],
                                            ringv[:, e % RING, hh_, :],
                                            expb[:, uf_, ci, :],
                                            start=(ci == 0), stop=(ci == 7))
                                    return ins
                                return f
                            P.add("t", avmm(hh, uf, lt, ecs),
                                  reads=[("ringv", e % RING) for e in ecs] +
                                        [("expb", uf)],
                                  writes=[("patt", lt % 2, hh)], tag=f"av{hg}_{lt}_{hh}")
                            def denmm(hh_, uf_, lt_):
                                def f(t):
                                    for ci in range(NCO):
                                        ins = t.matmul(
                                            pden[:, hh_ * 128:(hh_ + 1) * 128],
                                            ones_cb[:, :], expb[:, uf_, ci, :],
                                            start=(ci == 0), stop=(ci == 7))
                                    return ins
                                return f
                            P.add("t", denmm(hh, uf, lt),
                                  reads=[("expb", uf), "ones_cb"],
                                  writes=[("pden", hh)], tag=f"dn{hg}_{lt}_{hh}")
                        P.add("v", lambda v: v.reciprocal(recrow[:, :], pden[:, :]),
                              reads=[("pden", 0), ("pden", 1), ("pden", 2), ("pden", 3)],
                              writes=["recrow"], tag=f"rc{hg}_{lt}")
                        P.add("t", lambda t: t.matmul(prep3[:, :], ones_rf[:, :],
                                                      recrow[:, :], start=True, stop=True),
                              reads=["recrow", "ones_rf"], writes=["prep3"],
                              tag=f"rcrep{hg}_{lt}")
                        P.add("v", lambda v: v.tensor_copy(reprow[:, :], prep3[:, :]),
                              reads=["prep3"], writes=["reprow"], tag=f"rpc{hg}_{lt}")
                        def aoev(hg_, lt_):
                            def f(v):
                                po = (lt_ % 2) * 512
                                return v.tensor_mul(
                                    attnout[:, hg_ * NHG:(hg_ + 1) * NHG, lt_, :],
                                    patt[:, po:po + 512].rearrange("p (h l) -> p h l", h=NHG),
                                    reprow[:, :].rearrange("p (h l) -> p h l", h=NHG))
                            return f
                        P.add("v", aoev(hg, lt),
                              reads=[("patt", lt % 2, hh) for hh in range(NHG)] + ["reprow"],
                              writes=[("attnout", hg, lt)], tag=f"ao{hg}_{lt}")
                if dbg_ao:
                    P.add("sy", lambda sy: sy.dma_start(
                        out=ao_s[:, :, :].transpose([1, 0, 2]),
                        in_=attnout[:, :, :, :].rearrange("p h lt l -> p h (lt l)")),
                        reads=[("attnout", hg_, lt_) for hg_ in range(4) for lt_ in range(NLT)],
                        writes=["ao_s"], dma=True, tag="ao_dump")

                P.barrier()
                p3ctx.close()
                if "P4" in phases:
                    with contextlib.ExitStack() as px:
                        PSB = lambda n_, s_, d_: px.enter_context(nc.sbuf_tensor(n_, s_, d_))
                        PPS = lambda n_, s_: px.enter_context(nc.psum_tensor(n_, s_, F32))
                        wot = PSB("wot", [128, 2, 16, 128], BF16)
                        v1b = PSB("v1b", [128, 16, 1024], F32)
                        xres = PSB("xres", [128, 2, 1024], F32)
                        v1c = PSB("v1c", [128, 2, 512], BF16)
                        pso = PPS("pso", [128, 1024])
                        psl1 = PPS("psl1", [1, 1024])
                        psl2 = PPS("psl2", [1, 1024])
                        psr1 = PPS("psr1", [128, 512])
                        psr2 = PPS("psr2", [128, 512])
                        mrow = PSB("mrow", [1, 1024], F32)
                        irow = PSB("irow", [1, 1024], F32)

                        for sgi in range(2):
                            lbase = sgi * 1024
                            for kc in range(16):
                                P.add("g", (lambda sgi_, kc_: lambda g: g.dma_start(
                                    out=wot[:, kc_ % 2, :, :],
                                    in_=wo_p[kc_].transpose([1, 0, 2])))(sgi, kc),
                                    writes=[("wot", kc % 2)], dma=True,
                                    tag=f"wo{sgi}_{kc}")
                                P.add("sy", (lambda sgi_, kc_: lambda sy: sy.dma_start(
                                    out=xres[:, kc_ % 2, :],
                                    in_=xT[kc_ * 128:(kc_ + 1) * 128,
                                           HALO_B + sgi_ * 1024:HALO_B + (sgi_ + 1) * 1024]))(sgi, kc),
                                    writes=[("xres", kc % 2)], dma=True,
                                    tag=f"xr{sgi}_{kc}")
                                for lh in range(2):
                                    def omm(sgi_, kc_, lh_):
                                        def f(t):
                                            sl = pso[:, lh_ * 512:(lh_ + 1) * 512]
                                            lt0 = sgi_ * 8 + lh_ * 4
                                            for hc in range(16):
                                                rhs = attnout[:, hc, lt0:lt0 + 4, :]
                                                ins = t.matmul(sl, wot[:, kc_ % 2, hc, :],
                                                               rhs, start=(hc == 0),
                                                               stop=(hc == 15))
                                            return ins
                                        return f
                                    P.add("t", omm(sgi, kc, lh),
                                          reads=[("wot", kc % 2)] +
                                                [("attnout", hg_, sgi * 8 + lh * 4 + i_)
                                                 for hg_ in range(4) for i_ in range(4)],
                                          writes=[("pso", lh)], tag=f"o{sgi}_{kc}_{lh}")
                                    def ov(sgi_, kc_, lh_):
                                        def f(v):
                                            sl = pso[:, lh_ * 512:(lh_ + 1) * 512]
                                            return v.scalar_tensor_tensor(
                                                v1b[:, kc_, lh_ * 512:(lh_ + 1) * 512], sl,
                                                cols[:, COL["bo"] + kc_:COL["bo"] + kc_ + 1],
                                                xres[:, kc_ % 2, lh_ * 512:(lh_ + 1) * 512],
                                                ALU.add, ALU.add)
                                        return f
                                    P.add("v", ov(sgi, kc, lh),
                                          reads=[("pso", lh), ("xres", kc % 2), "cols"],
                                          writes=[("v1b", kc, lh)], tag=f"ov{sgi}_{kc}_{lh}")
                                    def ost(sgi_, kc_, lh_):
                                        def f(s):
                                            src = v1b[:, kc_, lh_ * 512:(lh_ + 1) * 512]
                                            return s.activation(v1c[:, lh_, :], src, AF.Copy)
                                        return f
                                    P.add("s", ost(sgi, kc, lh),
                                          reads=[("v1b", kc, lh)], writes=[("v1c", lh)],
                                          tag=f"oc{sgi}_{kc}_{lh}")
                                    P.add("t", (lambda kc_, lh_: lambda t: t.matmul(
                                        psl1[:, lh_ * 512:(lh_ + 1) * 512], ones_cb[:, :],
                                        v1c[:, lh_, :], start=(kc_ == 0), stop=(kc_ == 15)))(kc, lh),
                                        reads=[("v1c", lh), "ones_cb"],
                                        writes=[("psl1", lh)], tag=f"os1_{sgi}_{kc}_{lh}")
                                    def osq(sgi_, kc_, lh_):
                                        def f(s):
                                            return s.activation(
                                                v1c[:, lh_, :],
                                                v1b[:, kc_, lh_ * 512:(lh_ + 1) * 512],
                                                AF.Square)
                                        return f
                                    P.add("s", osq(sgi, kc, lh),
                                          reads=[("v1b", kc, lh)], writes=[("v1c", lh)],
                                          tag=f"oq{sgi}_{kc}_{lh}")
                                    P.add("t", (lambda kc_, lh_: lambda t: t.matmul(
                                        psl2[:, lh_ * 512:(lh_ + 1) * 512], ones_cb[:, :],
                                        v1c[:, lh_, :], start=(kc_ == 0), stop=(kc_ == 15)))(kc, lh),
                                        reads=[("v1c", lh), "ones_cb"],
                                        writes=[("psl2", lh)], tag=f"os2_{sgi}_{kc}_{lh}")
                            for lh in range(2):
                                def lnrow(lh_):
                                    def f(s):
                                        s.activation(mrow[:, lh_ * 512:(lh_ + 1) * 512],
                                                     psl1[:, lh_ * 512:(lh_ + 1) * 512],
                                                     AF.Copy, scale=1.0 / DM)
                                        return s.activation(
                                            irow[:, lh_ * 512:(lh_ + 1) * 512],
                                            psl2[:, lh_ * 512:(lh_ + 1) * 512],
                                            AF.Copy, scale=1.0 / DM)
                                    return f
                                P.add("s", lnrow(lh), reads=[("psl1", lh), ("psl2", lh)],
                                      writes=[("mrow", lh), ("irow", lh)], tag=f"lnr{sgi}_{lh}")
                                P.add("v", (lambda lh_: lambda v: v.scalar_tensor_tensor(
                                    srow[:, 0:512], mrow[:, lh_ * 512:(lh_ + 1) * 512], 1.0,
                                    mrow[:, lh_ * 512:(lh_ + 1) * 512], ALU.mult, ALU.mult))(lh),
                                    reads=[("mrow", lh)], writes=["srow"],
                                    tag=f"lnva{sgi}_{lh}")
                                P.add("v", (lambda lh_: lambda v: v.tensor_sub(
                                    irow[:, lh_ * 512:(lh_ + 1) * 512],
                                    irow[:, lh_ * 512:(lh_ + 1) * 512], srow[:, 0:512]))(lh),
                                    reads=["srow", ("irow", lh)], writes=[("irow", lh)],
                                    tag=f"lnvb{sgi}_{lh}")
                                P.add("s", (lambda lh_: lambda s: s.activation(
                                    irow[:, lh_ * 512:(lh_ + 1) * 512],
                                    irow[:, lh_ * 512:(lh_ + 1) * 512], AF.Sqrt,
                                    bias=epst[:, 1:2]))(lh),
                                    reads=[("irow", lh), "epst"], writes=[("irow", lh)],
                                    tag=f"lns{sgi}_{lh}")
                                P.add("v", (lambda lh_: lambda v: v.reciprocal(
                                    irow[:, lh_ * 512:(lh_ + 1) * 512],
                                    irow[:, lh_ * 512:(lh_ + 1) * 512]))(lh),
                                    reads=[("irow", lh)], writes=[("irow", lh)],
                                    tag=f"lnca{sgi}_{lh}")
                                P.add("v", (lambda lh_: lambda v: v.tensor_mul(
                                    mrow[:, lh_ * 512:(lh_ + 1) * 512],
                                    mrow[:, lh_ * 512:(lh_ + 1) * 512],
                                    irow[:, lh_ * 512:(lh_ + 1) * 512]))(lh),
                                    reads=[("irow", lh), ("mrow", lh)],
                                    writes=[("mrow", lh)], tag=f"lncb{sgi}_{lh}")
                                P.add("t", (lambda lh_: lambda t: t.matmul(
                                    psr1[:, :], ones_rf[:, :],
                                    irow[:, lh_ * 512:(lh_ + 1) * 512],
                                    start=True, stop=True))(lh),
                                    reads=[("irow", lh), "ones_rf"], writes=["psr1"],
                                    tag=f"lnR1{sgi}_{lh}")
                                P.add("t", (lambda lh_: lambda t: t.matmul(
                                    psr2[:, :], ones_rf[:, :],
                                    mrow[:, lh_ * 512:(lh_ + 1) * 512],
                                    start=True, stop=True))(lh),
                                    reads=[("mrow", lh), "ones_rf"], writes=["psr2"],
                                    tag=f"lnR2{sgi}_{lh}")
                                for kc in range(16):
                                    P.add("v", (lambda kc_, lh_: lambda v: v.tensor_mul(
                                        v1b[:, kc_, lh_ * 512:(lh_ + 1) * 512],
                                        v1b[:, kc_, lh_ * 512:(lh_ + 1) * 512],
                                        psr1[:, :]))(kc, lh),
                                        reads=[("v1b", kc, lh), "psr1"],
                                        writes=[("v1b", kc, lh)], tag=f"y1a{sgi}_{kc}_{lh}")
                                    P.add("v", (lambda kc_, lh_: lambda v: v.tensor_sub(
                                        v1b[:, kc_, lh_ * 512:(lh_ + 1) * 512],
                                        v1b[:, kc_, lh_ * 512:(lh_ + 1) * 512],
                                        psr2[:, :]))(kc, lh),
                                        reads=[("v1b", kc, lh), "psr2"],
                                        writes=[("v1b", kc, lh)], tag=f"y1b{sgi}_{kc}_{lh}")
                            P.add("sy", (lambda sgi_: lambda sy: sy.dma_start(
                                out=y1_s[:, :, sgi_ * 1024:(sgi_ + 1) * 1024]
                                .transpose([1, 0, 2]),
                                in_=v1b[:, :, :]))(sgi),
                                reads=[("v1b", kc_, lh_) for kc_ in range(16)
                                       for lh_ in range(2)],
                                writes=["y1_s"], dma=True, tag=f"y1sp{sgi}")

        P.barrier()
        if "P5" in phases:
            with contextlib.ExitStack() as px:
                PSB = lambda n_, s_, d_: px.enter_context(nc.sbuf_tensor(n_, s_, d_))
                gtile = PSB("gtile", [128, ITR], BF16)
                frhs = PSB("frhs", [128, 2, 512], BF16)
                psf = px.enter_context(nc.psum_tensor("psf", [128, 1024], F32))

                for wi, (src_t, dst_t, lhs, nbl, nj) in enumerate([
                        (fup_p, gup_s, hfoldup, 16, ITR),
                        (fgt_p, ggt_s, hfoldup, 16, ITR),
                        (fdn_p, gdn_s, hfolddn, 64, DM)]):
                    rfac = 128 // nbl
                    for bl in range(nbl):
                        njb = nj // 512
                        for jb in range(njb):
                            P.add("g", (lambda s_, bl_, jb_: lambda g: g.dma_start(
                                out=frhs[:, jb_ % 2, :],
                                in_=s_[bl_][:, jb_ * 512:(jb_ + 1) * 512]))(src_t, bl, jb),
                                writes=[("frhs", jb % 2)], dma=True,
                                tag=f"f{wi}_{bl}_{jb}")
                            P.add("t", (lambda l_, jb_: lambda t: t.matmul(
                                psf[:, (jb_ % 2) * 512:(jb_ % 2 + 1) * 512],
                                l_, frhs[:, jb_ % 2, :], start=True, stop=True))(lhs[:, :], jb),
                                reads=[("frhs", jb % 2), "hfoldup", "hfolddn"],
                                writes=[("psf", jb % 2)], tag=f"fm{wi}_{bl}_{jb}")
                            P.add("s", (lambda jb_: lambda s: s.activation(
                                gtile[:, jb_ * 512:(jb_ + 1) * 512],
                                psf[:, (jb_ % 2) * 512:(jb_ % 2 + 1) * 512],
                                AF.Copy))(jb),
                                reads=[("psf", jb % 2)], writes=[("gtile", jb)],
                                tag=f"fe{wi}_{bl}_{jb}")
                        def gsp(d_, bl_, nbl_, nj_):
                            def f(sy):
                                dst = d_.rearrange("a (r bl) j -> (a r) bl j", bl=nbl_)[:, bl_, :]
                                return sy.dma_start(out=dst, in_=gtile[:, 0:nj_])
                            return f
                        P.add("sy", gsp(dst_t, bl, nbl, nj),
                              reads=[("gtile", jb_) for jb_ in range(nj // 512)],
                              writes=[f"g_s{wi}"], dma=True, tag=f"gsp{wi}_{bl}")

        P.barrier()
        if "P6" in phases:
            with contextlib.ExitStack() as px:
                PSB = lambda n_, s_, d_: px.enter_context(nc.sbuf_tensor(n_, s_, d_))
                PPS = lambda n_, s_: px.enter_context(nc.psum_tensor(n_, s_, F32))
                w2048 = PSB("w2048", [128, 16, TC], BF16)
                y1bt = PSB("y1bt", [128, 2, TC], BF16)
                suu = PSB("suu", [128, 16], F32)
                sud = PSB("sud", [128, 64], F32)
                hsa = PSB("hsa", [128, 2, 128], BF16)
                gupsl = PSB("gupsl", [128, 2, 16, 128], BF16)
                ggtsl = PSB("ggtsl", [128, 2, 16, 128], BF16)
                silub = PSB("silub", [128, 2, 512], BF16)
                upb = PSB("upb", [128, 2, 512], BF16)
                x1b = PSB("x1b", [128, 2, 512], BF16)
                w1b = PSB("w1b", [128, 2, TC], BF16)
                psu = PPS("psu", [128, 1024])
                psg6 = PPS("psg6", [128, 1024])
                psw = PPS("psw", [128, 1024])

                P.add("v", lambda v: v.tensor_scalar(suu[:, :], cols[:, COL["su_ug"]:
                                                     COL["su_ug"] + 16],
                                                     float(DM) ** -0.5, None, ALU.mult),
                      reads=["cols"], writes=["suu"], tag="suu")
                P.add("v", lambda v: v.tensor_scalar(sud[:, :], cols[:, COL["su_dp"]:
                                                     COL["su_dp"] + 64],
                                                     float(ITR) ** -0.5, None, ALU.mult),
                      reads=["cols"], writes=["sud"], tag="sud")
                for a in range(16):
                    P.add("g", (lambda a_: lambda g: g.dma_start(
                        out=y1bt[:, a_ % 2, :], in_=y1_s[a_]))(a),
                        reads=["y1_s"], writes=[("y1bt", a % 2)], dma=True, tag=f"y1l{a}")
                    P.add("v", (lambda a_: lambda v: v.tensor_scalar(
                        hsa[:, a_ % 2, :], h128b[:, :], suu[:, a_:a_ + 1], None,
                        ALU.mult))(a),
                        reads=["h128b", "suu"], writes=[("hsa", a % 2)], tag=f"hsa{a}")
                    for lb in range(4):
                        P.add("t", (lambda a_, lb_: lambda t: t.matmul(
                            psu[:, (lb_ % 2) * 512:(lb_ % 2 + 1) * 512], hsa[:, a_ % 2, :],
                            y1bt[:, a_ % 2, lb_ * 512:(lb_ + 1) * 512],
                            start=True, stop=True))(a, lb),
                            reads=[("hsa", a % 2), ("y1bt", a % 2)],
                            writes=[("psu", lb % 2)], tag=f"sa{a}_{lb}")
                        P.add("s", (lambda a_, lb_: lambda s: s.activation(
                            w2048[:, a_, lb_ * 512:(lb_ + 1) * 512],
                            psu[:, (lb_ % 2) * 512:(lb_ % 2 + 1) * 512], AF.Copy))(a, lb),
                            reads=[("psu", lb % 2)], writes=[("w2048", a)],
                            tag=f"sae{a}_{lb}")
                for i in range(64):
                    P.add("g", (lambda i_: lambda g: g.dma_start(
                        out=gupsl[:, i_ % 2, :, :],
                        in_=gup_s[:, :, i_ * 128:(i_ + 1) * 128].transpose([1, 0, 2])))(i),
                        reads=["g_s0"], writes=[("gupsl", i % 2)], dma=True, tag=f"gu{i}")
                    P.add("g", (lambda i_: lambda g: g.dma_start(
                        out=ggtsl[:, i_ % 2, :, :],
                        in_=ggt_s[:, :, i_ * 128:(i_ + 1) * 128].transpose([1, 0, 2])))(i),
                        reads=["g_s1"], writes=[("ggtsl", i % 2)], dma=True, tag=f"gg{i}")
                    P.add("v", (lambda i_: lambda v: v.tensor_scalar(
                        hsa[:, i_ % 2, :], h128b[:, :], sud[:, i_:i_ + 1], None,
                        ALU.mult))(i),
                        reads=["h128b", "sud"], writes=[("hsa", i % 2)], tag=f"hsd{i}")
                    for lb in range(4):
                        def upmm(i_, lb_):
                            def f(t):
                                sl = psu[:, (lb_ % 2) * 512:(lb_ % 2 + 1) * 512]
                                for a in range(16):
                                    ins = t.matmul(sl, gupsl[:, i_ % 2, a, :],
                                                   w2048[:, a, lb_ * 512:(lb_ + 1) * 512],
                                                   start=(a == 0), stop=(a == 15))
                                return ins
                            return f
                        P.add("t", upmm(i, lb),
                              reads=[("gupsl", i % 2)] + [("w2048", a_) for a_ in range(16)],
                              writes=[("psu", lb % 2)], tag=f"up{i}_{lb}")
                        def gtmm(i_, lb_):
                            def f(t):
                                sl = psg6[:, (lb_ % 2) * 512:(lb_ % 2 + 1) * 512]
                                for a in range(16):
                                    ins = t.matmul(sl, ggtsl[:, i_ % 2, a, :],
                                                   w2048[:, a, lb_ * 512:(lb_ + 1) * 512],
                                                   start=(a == 0), stop=(a == 15))
                                return ins
                            return f
                        P.add("t", gtmm(i, lb),
                              reads=[("ggtsl", i % 2)] + [("w2048", a_) for a_ in range(16)],
                              writes=[("psg6", lb % 2)], tag=f"gt{i}_{lb}")
                        P.add("s", (lambda i_, lb_: lambda s: s.activation(
                            upb[:, lb_ % 2, :], psu[:, (lb_ % 2) * 512:(lb_ % 2 + 1) * 512],
                            AF.Identity, scale=cols[:, COL["wsu"] + i_:COL["wsu"] + i_ + 1]))(i, lb),
                            reads=[("psu", lb % 2), "cols"], writes=[("upb", lb % 2)],
                            tag=f"ue{i}_{lb}")
                        P.add("s", (lambda i_, lb_: lambda s: s.activation(
                            silub[:, lb_ % 2, :], psg6[:, (lb_ % 2) * 512:(lb_ % 2 + 1) * 512],
                            AF.Silu, scale=cols[:, COL["wsg"] + i_:COL["wsg"] + i_ + 1]))(i, lb),
                            reads=[("psg6", lb % 2), "cols"], writes=[("silub", lb % 2)],
                            tag=f"se{i}_{lb}")
                        P.add("v", (lambda lb_: lambda v: v.tensor_mul(
                            x1b[:, lb_ % 2, :], silub[:, lb_ % 2, :], upb[:, lb_ % 2, :]))(lb),
                            reads=[("silub", lb % 2), ("upb", lb % 2)],
                            writes=[("x1b", lb % 2)], tag=f"x1{i}_{lb}")
                        P.add("t", (lambda i_, lb_: lambda t: t.matmul(
                            psw[:, (lb_ % 2) * 512:(lb_ % 2 + 1) * 512], hsa[:, i_ % 2, :],
                            x1b[:, lb_ % 2, :], start=True, stop=True))(i, lb),
                            reads=[("hsa", i % 2), ("x1b", lb % 2)], writes=[("psw", lb % 2)],
                            tag=f"sw{i}_{lb}")
                        P.add("s", (lambda i_, lb_: lambda s: s.activation(
                            w1b[:, i_ % 2, lb_ * 512:(lb_ + 1) * 512],
                            psw[:, (lb_ % 2) * 512:(lb_ % 2 + 1) * 512], AF.Copy))(i, lb),
                            reads=[("psw", lb % 2)], writes=[("w1b", i % 2, lb)],
                            tag=f"we{i}_{lb}")
                    P.add("sy", (lambda i_: lambda sy: sy.dma_start(
                        out=w1_s[i_], in_=w1b[:, i_ % 2, :]))(i),
                        reads=[("w1b", i % 2, lb_) for lb_ in range(4)],
                        writes=["w1_s"], dma=True, tag=f"w1sp{i}")

        P.barrier()
        if "P7" in phases:
            with contextlib.ExitStack() as px:
                PSB = lambda n_, s_, d_: px.enter_context(nc.sbuf_tensor(n_, s_, d_))
                PPS = lambda n_, s_: px.enter_context(nc.psum_tensor(n_, s_, F32))
                w1lt = PSB("w1lt", [128, 64, 512], BF16)
                gdnsl = PSB("gdnsl", [128, 2, 64, 128], BF16)
                y1sl = PSB("y1sl", [128, 2, 512], F32)
                v2b = PSB("v2b", [128, 16, 512], F32)
                v2c = PSB("v2c", [128, 2, 512], BF16)
                outb = PSB("outb", [128, 2048], F32)
                mrow2 = PSB("mrow2", [1, 512], F32)
                irow2 = PSB("irow2", [1, 512], F32)
                psd = PPS("psd", [128, 1024])
                psl12 = PPS("psl12", [1, 512])
                psl22 = PPS("psl22", [1, 512])
                psr12 = PPS("psr12", [128, 512])
                psr22 = PPS("psr22", [128, 512])
                pstr = PPS("pstr", [128, 512])

                for lg in range(4):
                    P.add("sy", (lambda lg_: lambda sy: sy.dma_start(
                        out=w1lt[:, :, :],
                        in_=w1_s[:, :, lg_ * 512:(lg_ + 1) * 512].transpose([1, 0, 2])))(lg),
                        reads=["w1_s"], writes=["w1lt"], dma=True, tag=f"w1l{lg}")
                    for kc in range(16):
                        P.add("g", (lambda lg_, kc_: lambda g: g.dma_start(
                            out=gdnsl[:, kc_ % 2, :, :],
                            in_=gdn_s[:, :, kc_ * 128:(kc_ + 1) * 128]
                            .transpose([1, 0, 2])))(lg, kc),
                            reads=["g_s2"], writes=[("gdnsl", kc % 2)], dma=True,
                            tag=f"gd{lg}_{kc}")
                        P.add("sy", (lambda lg_, kc_: lambda sy: sy.dma_start(
                            out=y1sl[:, kc_ % 2, :],
                            in_=y1_s[kc_][:, lg_ * 512:(lg_ + 1) * 512]))(lg, kc),
                            reads=["y1_s"], writes=[("y1sl", kc % 2)], dma=True,
                            tag=f"y1r{lg}_{kc}")
                        def dmm(lg_, kc_):
                            def f(t):
                                sl = psd[:, (kc_ % 2) * 512:(kc_ % 2 + 1) * 512]
                                for i in range(64):
                                    ins = t.matmul(sl, gdnsl[:, kc_ % 2, i, :],
                                                   w1lt[:, i, :],
                                                   start=(i == 0), stop=(i == 63))
                                return ins
                            return f
                        P.add("t", dmm(lg, kc), reads=[("gdnsl", kc % 2), "w1lt"],
                              writes=[("psd", kc % 2)], tag=f"d{lg}_{kc}")
                        def dv(lg_, kc_):
                            def f(v):
                                sl = psd[:, (kc_ % 2) * 512:(kc_ % 2 + 1) * 512]
                                return v.scalar_tensor_tensor(
                                    v2b[:, kc_, :], sl,
                                    cols[:, COL["wsd"] + kc_:COL["wsd"] + kc_ + 1],
                                    y1sl[:, kc_ % 2, :], ALU.mult, ALU.add)
                            return f
                        P.add("v", dv(lg, kc), reads=[("psd", kc % 2), ("y1sl", kc % 2),
                                                      "cols"],
                              writes=[("v2b", kc)], tag=f"dv{lg}_{kc}")
                        P.add("s", (lambda kc_: lambda s: s.activation(
                            v2c[:, 0, :], v2b[:, kc_, :], AF.Copy))(kc),
                            reads=[("v2b", kc)], writes=[("v2c", 0)], tag=f"dc{lg}_{kc}")
                        P.add("t", (lambda kc_: lambda t: t.matmul(
                            psl12[:, :], ones_cb[:, :], v2c[:, 0, :],
                            start=(kc_ == 0), stop=(kc_ == 15)))(kc),
                            reads=[("v2c", 0), "ones_cb"], writes=["psl12"],
                            tag=f"ds1_{lg}_{kc}")
                        P.add("s", (lambda kc_: lambda s: s.activation(
                            v2c[:, 1, :], v2b[:, kc_, :], AF.Square))(kc),
                            reads=[("v2b", kc)], writes=[("v2c", 1)], tag=f"dq{lg}_{kc}")
                        P.add("t", (lambda kc_: lambda t: t.matmul(
                            psl22[:, :], ones_cb[:, :], v2c[:, 1, :],
                            start=(kc_ == 0), stop=(kc_ == 15)))(kc),
                            reads=[("v2c", 1), "ones_cb"], writes=["psl22"],
                            tag=f"ds2_{lg}_{kc}")
                    def ln2row(s):
                        s.activation(mrow2[:, :], psl12[:, :], AF.Copy, scale=1.0 / DM)
                        return s.activation(irow2[:, :], psl22[:, :], AF.Copy, scale=1.0 / DM)
                    P.add("s", ln2row, reads=["psl12", "psl22"],
                          writes=["mrow2", "irow2"], tag=f"l2r{lg}")
                    P.add("v", lambda v: v.scalar_tensor_tensor(
                        srow[:, 0:512], mrow2[:, :], 1.0, mrow2[:, :], ALU.mult, ALU.mult),
                        reads=["mrow2"], writes=["srow"], tag=f"l2va{lg}")
                    P.add("v", lambda v: v.tensor_sub(irow2[:, :], irow2[:, :],
                                                      srow[:, 0:512]),
                          reads=["srow", "irow2"], writes=["irow2"], tag=f"l2vb{lg}")
                    P.add("s", lambda s: s.activation(irow2[:, :], irow2[:, :], AF.Sqrt,
                                                      bias=epst[:, 1:2]),
                          reads=["irow2", "epst"], writes=["irow2"], tag=f"l2s{lg}")
                    P.add("v", lambda v: v.reciprocal(irow2[:, :], irow2[:, :]),
                          reads=["irow2"], writes=["irow2"], tag=f"l2ca{lg}")
                    P.add("v", lambda v: v.tensor_mul(mrow2[:, :], mrow2[:, :],
                                                      irow2[:, :]),
                          reads=["irow2", "mrow2"], writes=["mrow2"], tag=f"l2cb{lg}")
                    P.add("t", lambda t: t.matmul(psr12[:, :], ones_rf[:, :], irow2[:, :],
                                                  start=True, stop=True),
                          reads=["irow2", "ones_rf"], writes=["psr12"], tag=f"l2R1{lg}")
                    P.add("t", lambda t: t.matmul(psr22[:, :], ones_rf[:, :], mrow2[:, :],
                                                  start=True, stop=True),
                          reads=["mrow2", "ones_rf"], writes=["psr22"], tag=f"l2R2{lg}")
                    for kc in range(16):
                        P.add("v", (lambda kc_: lambda v: v.tensor_mul(
                            v2b[:, kc_, :], v2b[:, kc_, :], psr12[:, :]))(kc),
                            reads=[("v2b", kc), "psr12"], writes=[("v2b", kc)],
                            tag=f"y2a{lg}_{kc}")
                        P.add("v", (lambda kc_: lambda v: v.tensor_sub(
                            v2b[:, kc_, :], v2b[:, kc_, :], psr22[:, :]))(kc),
                            reads=[("v2b", kc), "psr22"], writes=[("v2b", kc)],
                            tag=f"y2b{lg}_{kc}")
                    for ls in range(4):
                        for kc in range(16):
                            P.add("t", (lambda kc_, ls_: lambda t: t.transpose(
                                pstr[:, 0:128], v2b[:, kc_, ls_ * 128:(ls_ + 1) * 128],
                                identf[:, :]))(kc, ls),
                                reads=[("v2b", kc), "identf"], writes=["pstr"],
                                tag=f"tr{lg}_{ls}_{kc}")
                            P.add("v", (lambda kc_: lambda v: v.tensor_copy(
                                outb[:, kc_ * 128:(kc_ + 1) * 128], pstr[:, 0:128]))(kc),
                                reads=["pstr"], writes=[("outb", kc)], tag=f"te{lg}_{ls}_{kc}")
                        P.add("sy", (lambda lg_, ls_: lambda sy: sy.dma_start(
                            out=out_t[lg_ * 512 + ls_ * 128:lg_ * 512 + (ls_ + 1) * 128, :],
                            in_=outb[:, :]))(lg, ls),
                            reads=[("outb", kc_) for kc_ in range(16)],
                            writes=["out"], dma=True, tag=f"ost{lg}_{ls}")

        P.add("sy", None, reads=["out"] + (["qt_s", "kt_s", "v_s", "ao_s", "y1_s",
                                           "w1_s", "g_s0", "g_s1", "g_s2"]
                                          if debug else []), tag="final")

        runner = P.emit(nc, {k: sem_h[k] for k in sem_h})
        with nc.Block("main") as b:
            b.gpsimd(lambda g: runner(g, "g"))
            b.sync(lambda sy: runner(sy, "sy"))
            b.tensor(lambda t: runner(t, "t"))
            b.vector(lambda v: runner(v, "v"))
            b.scalar(lambda s: runner(s, "s"))
    return nc


def prepack(inputs):
    x = np.asarray(inputs["L0_attn_x"])[0]
    xT = np.ascontiguousarray(x.T)
    d = {}
    wq = np.asarray(inputs["L0_attn_wq"])
    wk = np.asarray(inputs["L0_attn_wk"])
    wv = np.asarray(inputs["L0_attn_wv"])
    wo = np.asarray(inputs["L0_attn_wo"])
    d["wq_p"] = np.ascontiguousarray(
        wq.T.reshape(16, 128, 16, 128).transpose(2, 0, 1, 3))
    d["wk_p"] = np.ascontiguousarray(
        wk.T.reshape(16, 128, 16, 128).transpose(2, 0, 1, 3))
    d["wv_p"] = np.ascontiguousarray(wv.T.reshape(16, 128, DM))
    d["wo_p"] = np.ascontiguousarray(
        wo.T.reshape(16, 128, 16, 128).transpose(2, 0, 1, 3))
    wup = np.asarray(inputs["L0_ffn_W_up"])
    wgt = np.asarray(inputs["L0_ffn_W_gate"])
    wdn = np.asarray(inputs["L0_ffn_W_down"])
    d["fup_p"] = np.ascontiguousarray(
        wup.T.reshape(16, 8, 16, ITR).transpose(2, 0, 1, 3).reshape(16, 128, ITR))
    d["fgt_p"] = np.ascontiguousarray(
        wgt.T.reshape(16, 8, 16, ITR).transpose(2, 0, 1, 3).reshape(16, 128, ITR))
    d["fdn_p"] = np.ascontiguousarray(
        wdn.T.reshape(64, 2, 64, DM).transpose(2, 0, 1, 3).reshape(64, 128, DM))
    colpk = np.zeros((128, NCOLS), np.float32)

    def put(name, vec):
        w = vec.reshape(-1, 128).T
        colpk[:, COL[name]:COL[name] + w.shape[1]] = w
    put("bq", np.asarray(inputs["L0_attn_bq"]))
    put("bk", np.asarray(inputs["L0_attn_bk"]))
    put("gq", np.asarray(inputs["L0_attn_gq"]))
    put("gk", np.asarray(inputs["L0_attn_gk"]))
    put("bo", np.asarray(inputs["L0_attn_bo"]))
    put("su_ug", np.asarray(inputs["L0_ffn_SU_ug"]))
    put("su_dp", np.asarray(inputs["L0_ffn_SU_dp"]))
    put("wsu", np.asarray(inputs["L0_ffn_Wscale_ug"])[:ITR])
    put("wsg", np.asarray(inputs["L0_ffn_Wscale_ug"])[ITR:])
    put("wsd", np.asarray(inputs["L0_ffn_Wscale_dp"]))
    d["colpk"] = colpk
    d["bv_row"] = np.ascontiguousarray(np.asarray(inputs["L0_attn_bv"])[None, :])
    d["wmix_in"] = np.ascontiguousarray(np.asarray(inputs["L0_attn_w_mix"]))
    d.update(make_consts())
    maps = []
    for c in range(NCORE):
        idx = (np.arange(EXTP) + c * TC - HALO_B) % L
        m = dict(d)
        m["xT"] = np.ascontiguousarray(xT[:, idx])
        maps.append(m)
    return maps


LAST = None


def kernel(**inputs):
    global LAST
    maps = prepack(inputs)
    nc = build(debug=DEBUG, phases=PHASES)
    res = run_bass_kernel_spmd(nc, maps, core_ids=list(range(NCORE)))
    LAST = res
    out = np.concatenate([res.results[c]["out"] for c in range(NCORE)], axis=0)
    return out.reshape(1, L, DM)


# version 03
# speedup vs baseline: 1.0087x; 1.0087x over previous
import contextlib
import numpy as np
import ml_dtypes
import concourse.bass as bass
import concourse.mybir as mybir
from concourse.bass_utils import run_bass_kernel_spmd

F32 = mybir.dt.float32
BF16 = mybir.dt.bfloat16
AF = mybir.ActivationFunctionType
ALU = mybir.AluOpType
AX = mybir.AxisListType
BF = ml_dtypes.bfloat16

L = 16384
DM = 2048
NH = 16
ITR = 8192
NCORE = 8
TC = L // NCORE
HALO_B = 1024
EXTP = 3584
NEC = EXTP // 128
SEQ_SHIFTS = [0, 1, -1, 3, -3, 7, -7, 20, -20, 53, -53, 141, -141, 380, -380, 1024]
OFFS = sorted(-s for s in SEQ_SHIFTS)
COFFS = [-8, -3, -2, -1, 0, 1, 2, 3]
NCO = len(COFFS)
CHAN = [0, 1, 2, 4, 8]
SCALE = float(128) ** -0.5
RMS_EPS = 1e-6
LN_EPS = 1e-5
NLT = TC // 128
RING = 13
NHG = 4
BSHIFT = 40.0

DEBUG = set()
PHASES = ("P1", "P2K", "P2V", "P3", "P4", "P5", "P6", "P7")


NDMASLOT = 8


class _Node:
    __slots__ = ("eng", "sem", "emit", "reads", "writes", "deps", "idx",
                 "marked", "val", "tag", "is_dma", "extra", "waits")

    def __init__(self, eng, sem, emit, reads, writes, idx, tag, is_dma):
        self.eng = eng
        self.sem = sem
        self.emit = emit
        self.reads = tuple(reads)
        self.writes = tuple(writes)
        self.idx = idx
        self.deps = ()
        self.marked = False
        self.val = 0
        self.tag = tag
        self.is_dma = is_dma
        self.extra = ()


class Prog:
    ENGS = ("g", "sy", "t", "v", "s")
    DMA_SEMS = tuple(f"dg{k}" for k in range(NDMASLOT)) + \
        tuple(f"dsy{k}" for k in range(NDMASLOT))
    BAR_SEMS = ("bg", "bsy", "bt", "bv", "bs")

    def __init__(self):
        self.nodes = []
        self._bn = 0
        self._dmacnt = {"dg": 0, "dsy": 0}
        self._dmahist = {"dg": [], "dsy": []}
        self._lasteng = {}

    def add(self, eng, emit, reads=(), writes=(), dma=False, tag="", sem=None):
        reads = tuple(reads)
        if dma:
            q = "dg" if eng == "g" else "dsy"
            sem = f"{q}{self._dmacnt[q] % NDMASLOT}"
            self._dmacnt[q] += 1
            reads = reads + (("qtail", q),)
        elif sem is None:
            sem = eng
        n = _Node(eng, sem, emit, reads, writes, len(self.nodes), tag, dma)
        if dma:
            hist = self._dmahist[q]
            if len(hist) >= NDMASLOT:
                n.extra = (hist[-NDMASLOT],)
            hist.append(n)
        if emit is not None:
            self._lasteng[eng] = n
        self.nodes.append(n)
        self._dirty = True
        return n

    def barrier(self):
        if not getattr(self, "_dirty", False):
            return
        self._dirty = False
        bn = self._bn
        self._bn += 1
        flags = []
        for e in self.ENGS:
            w = [("bflag", bn, e)]
            if e == "g":
                w.append(("qtail", "dg"))
            if e == "sy":
                w.append(("qtail", "dsy"))
            n = self.add(e, None, writes=w, tag=f"bar1_{bn}_{e}", sem=f"b{e}")
            if e in self._lasteng:
                n.extra = n.extra + (self._lasteng[e],)
            flags.append(("bflag", bn, e))
        for e in self.ENGS:
            self.add(e, None, reads=flags, tag=f"bar2_{bn}_{e}", sem=f"b{e}")

    def resolve(self):
        last_w = {}
        readers = {}
        for n in self.nodes:
            deps = set(n.extra)
            for r in n.reads:
                w = last_w.get(r)
                if w is not None:
                    deps.add(w)
            for r in n.writes:
                w = last_w.get(r)
                if w is not None:
                    deps.add(w)
                deps.update(readers.get(r, ()))
            deps.discard(n)
            n.deps = tuple(deps)
            for r in n.reads:
                readers.setdefault(r, set()).add(n)
            for r in n.writes:
                last_w[r] = n
                readers[r] = {n}
        for n in self.nodes:
            if n.emit is not None:
                n.marked = True
            for d in n.deps:
                d.marked = True
        cum = {}
        for n in self.nodes:
            if n.marked:
                inc = 16 if n.is_dma else 1
                cum[n.sem] = cum.get(n.sem, 0) + inc
                n.val = cum[n.sem]
        vc = [None] * len(self.nodes)
        know = {e: {} for e in self.ENGS}
        for n in self.nodes:
            k = know[n.eng]
            waits = []
            for d in sorted(n.deps, key=lambda d_: (d_.sem, d_.val)):
                if k.get(d.sem, 0) >= d.val:
                    continue
                waits.append((d.sem, d.val))
                dv = vc[d.idx]
                if dv:
                    for s_, v_ in dv.items():
                        if k.get(s_, 0) < v_:
                            k[s_] = v_
                if k.get(d.sem, 0) < d.val:
                    k[d.sem] = d.val
            n.waits = waits
            if n.marked:
                k2 = dict(k)
                k2[n.sem] = n.val
                vc[n.idx] = k2
        per_eng = {e: [] for e in self.ENGS}
        for n in self.nodes:
            per_eng[n.eng].append(n)
        ptr = {e: 0 for e in per_eng}
        done = [False] * len(self.nodes)
        progress = True
        ndone = 0
        while progress:
            progress = False
            for e, lst in per_eng.items():
                while ptr[e] < len(lst):
                    n = lst[ptr[e]]
                    if all(done[d.idx] for d in n.deps):
                        done[n.idx] = True
                        ptr[e] += 1
                        ndone += 1
                        progress = True
                    else:
                        break
        if ndone != len(self.nodes):
            stuck = [n for n in self.nodes if not done[n.idx]][:5]
            info = "; ".join(
                f"{n.eng}#{n.idx}:{n.tag} waits " +
                ",".join(f"{d.eng}#{d.idx}:{d.tag}" for d in n.deps
                         if not done[d.idx]) for n in stuck)
            raise RuntimeError(f"deadlock in schedule: {info}")
        return per_eng

    def emit(self, nc, sems):
        per_eng = self.resolve()

        def run_engine(eng_obj, ename):
            for n in per_eng[ename]:
                for s_, v_ in n.waits:
                    eng_obj.wait_ge(sems[s_], v_)
                ins = n.emit(eng_obj) if n.emit is not None else None
                if n.marked:
                    if ins is None:
                        ins = eng_obj.nop()
                    ins.then_inc(sems[n.sem], 16 if n.is_dma else 1)
        return run_engine


def _hadamard(n):
    h = np.array([[1.0]])
    while h.shape[0] < n:
        h = np.block([[h, h], [h, -h]])
    return h


def make_consts():
    c = {}
    c["h128b_in"] = _hadamard(128).astype(BF)
    c["hfoldup_in"] = np.kron(_hadamard(16), np.eye(8)).astype(BF)
    c["hfolddn_in"] = np.kron(_hadamard(64), np.eye(2)).astype(BF)
    pcs = np.zeros((128, 5, 128), np.float32)
    f = np.arange(128)
    for ci, cs in enumerate(CHAN):
        pcs[f, ci, (f - cs) % 128] = 1.0
    c["pcs_in"] = pcs.astype(BF)
    offs = set(OFFS)
    m = np.zeros((128, NCO, 128), np.float32)
    for ci, co in enumerate(COFFS):
        for p in range(128):
            for q in range(128):
                if co * 128 + p - q in offs:
                    m[p, ci, q] = 1.0
    c["masks_in"] = m.astype(BF)
    c["identf_in"] = np.eye(128, dtype=np.float32)
    c["ones_cb_in"] = np.ones((128, 1), BF)
    c["ones_rf_in"] = np.ones((1, 128), np.float32)
    c["epsv_in"] = np.array([[RMS_EPS, LN_EPS]], np.float32)
    return c


COL = {}
_o = 0
for _n, _w in [("bq", 16), ("bk", 16), ("gq", 16), ("gk", 16), ("bo", 16),
               ("su_ug", 16), ("wsd", 16), ("su_dp", 64), ("wsu", 64), ("wsg", 64)]:
    COL[_n] = _o
    _o += _w
NCOLS = _o


def build(debug=(), phases=PHASES):
    debug = set(debug)
    phases = set(phases)
    nc = bass.Bass("TRN2", target_bir_lowering=False)

    def dram(name, shape, dtype, kind="ExternalInput"):
        return nc.dram_tensor(name, shape, dtype, kind=kind).ap()

    def scr(name, shape, dtype):
        kind = "ExternalOutput" if name in debug else "Internal"
        return nc.dram_tensor(name, shape, dtype, kind=kind).ap()

    xT = dram("xT", [DM, EXTP], F32)
    wq_p = dram("wq_p", [16, 16, 128, 128], F32)
    wk_p = dram("wk_p", [16, 16, 128, 128], F32)
    wv_p = dram("wv_p", [16, 128, DM], F32)
    wo_p = dram("wo_p", [16, 16, 128, 128], F32)
    fup_p = dram("fup_p", [16, 128, ITR], F32)
    fgt_p = dram("fgt_p", [16, 128, ITR], F32)
    fdn_p = dram("fdn_p", [64, 128, DM], F32)
    colpk = dram("colpk", [128, NCOLS], F32)
    bv_row = dram("bv_row", [1, DM], F32)
    wmix_in = dram("wmix_in", [1, 5], F32)
    consts = make_consts()
    CIN = {k: dram(k, list(v.shape), BF16 if v.dtype == BF else F32)
           for k, v in consts.items()}
    out_t = dram("out", [TC, DM], F32, kind="ExternalOutput")

    qt_s = scr("qt_s", [16, 128, TC], BF16)
    kt_s = scr("kt_s", [NEC, 128, NH, 128], BF16)
    v_s = scr("v_s", [NEC, 128, NH, 128], BF16)
    y1_s = scr("y1_s", [16, 128, TC], F32)
    w1_s = scr("w1_s", [64, 128, TC], BF16)
    gup_s = scr("gup_s", [16, 128, ITR], BF16)
    ggt_s = scr("ggt_s", [16, 128, ITR], BF16)
    gdn_s = scr("gdn_s", [64, 128, DM], BF16)
    ao_s = scr("ao_s", [NH, 128, TC], BF16)
    dbg_ao = "ao_s" in debug

    P = Prog()
    xTv = xT.rearrange("(c p) n -> p c n", p=128)

    with contextlib.ExitStack() as ctx:
        SB = lambda n_, s_, d_: ctx.enter_context(nc.sbuf_tensor(n_, s_, d_))
        sem_h = {k: ctx.enter_context(nc.semaphore(k))
                 for k in ("g", "sy", "t", "v", "s") + Prog.DMA_SEMS + Prog.BAR_SEMS}

        cols = SB("cols", [128, NCOLS], F32)
        mixh = SB("mixh", [128, NH, 128], BF16)
        h128b = SB("h128b", [128, 128], BF16)
        masks = SB("masks", [128, NCO, 128], BF16)
        Bcols = SB("Bcols", [128, 16], F32)
        maxq2 = SB("maxq2", [1, 16], F32)
        maxk2 = SB("maxk2", [1, 16], F32)
        ones_cb = SB("ones_cb", [128, 1], BF16)
        ones_rf = SB("ones_rf", [1, 128], F32)
        epst = SB("epst", [1, 2], F32)
        srow = SB("srow", [1, 512], F32)
        srow2 = SB("srow2", [1, 512], F32)
        rtmp = SB("rtmp", [1, 8], F32)

        hfoldup = SB("hfoldup", [128, 128], BF16)
        hfolddn = SB("hfolddn", [128, 128], BF16)
        identf = SB("identf", [128, 128], F32)
        bvrep = SB("bvrep", [128, DM], F32)
        p0ctx = contextlib.ExitStack()
        wmix = p0ctx.enter_context(nc.sbuf_tensor("wmix", [1, 8], F32))
        pcs_t = p0ctx.enter_context(nc.sbuf_tensor("pcs_t", [128, 5, 128], BF16))
        wrep = p0ctx.enter_context(nc.sbuf_tensor("wrep", [128, 8], F32))
        mbase = p0ctx.enter_context(nc.sbuf_tensor("mbase", [128, 128], BF16))
        bvrow_t = p0ctx.enter_context(nc.sbuf_tensor("bvrow_t", [1, DM], F32))
        ps_gen = p0ctx.enter_context(nc.psum_tensor("ps_gen", [128, 1024], F32))

        for dst, src, rs in [
                (cols[:, :], colpk, "cols"), (wmix[:, 0:5], wmix_in, "wmix"),
                (pcs_t[:, :, :], CIN["pcs_in"], "pcs"),
                (ones_cb[:, :], CIN["ones_cb_in"], "ones_cb"),
                (ones_rf[:, :], CIN["ones_rf_in"], "ones_rf"),
                (epst[:, :], CIN["epsv_in"], "epst"),
                (h128b[:, :], CIN["h128b_in"], "h128b"),
                (masks[:, :, :], CIN["masks_in"], "masks"),
                (hfoldup[:, :], CIN["hfoldup_in"], "hfoldup"),
                (hfolddn[:, :], CIN["hfolddn_in"], "hfolddn"),
                (identf[:, :], CIN["identf_in"], "identf"),
                (bvrow_t[:, :], bv_row, "bvrow")]:
            P.add("g", (lambda d_, s_: lambda g: g.dma_start(out=d_, in_=s_))(dst, src),
                  writes=[rs], dma=True, tag=f"ld_{rs}")
        P.add("s", lambda s: s.mul(wmix[:, 0:5], wmix[:, 0:5], SCALE),
              reads=["wmix"], writes=["wmix"], tag="wmix_scale")
        P.add("t", lambda t: t.matmul(ps_gen[:, 0:5], ones_rf[:, :], wmix[:, 0:5],
                                      start=True, stop=True),
              reads=["wmix", "ones_rf"], writes=["psg0"], tag="wrep_mm")
        P.add("t", lambda t: t.matmul(ps_gen[:, 512:1024], ones_rf[:, :],
                                      bvrow_t[:, 0:512], start=True, stop=True),
              reads=["bvrow", "ones_rf"], writes=["psg1"], tag="bvrep_mm0")

        P.add("v", lambda v: v.tensor_copy(wrep[:, 0:5], ps_gen[:, 0:5]),
              reads=["psg0"], writes=["wrep"], tag="wrep_cp")
        P.add("v", lambda v: v.tensor_scalar(mbase[:, :], pcs_t[:, 0, :], wrep[:, 0:1],
                                             None, ALU.mult),
              reads=["wrep", "pcs"], writes=["mbase"], tag="mb0")
        for ci in range(1, 5):
            P.add("v", (lambda ci_: lambda v: v.scalar_tensor_tensor(
                mbase[:, :], pcs_t[:, ci_, :], wrep[:, ci_:ci_ + 1], mbase[:, :],
                ALU.mult, ALU.add))(ci),
                reads=["wrep", "pcs", "mbase"], writes=["mbase"], tag=f"mb{ci}")
        for h in range(NH):
            P.add("v", (lambda h_: lambda v: v.tensor_scalar(
                mixh[:, h_, :], mbase[:, :],
                cols[:, COL["gk"] + h_:COL["gk"] + h_ + 1], None, ALU.mult))(h),
                reads=["mbase", "cols"], writes=["mixh"], tag=f"mixh{h}")
        P.add("v", lambda v: v.memset(maxq2[:, :], 0.0), writes=["maxq2"], tag="z_mq")
        P.add("v", lambda v: v.memset(maxk2[:, :], 0.0), writes=["maxk2"], tag="z_mk")
        for q in range(4):
            if q > 0:
                P.add("t", (lambda q_: lambda t: t.matmul(
                    ps_gen[:, 512:1024], ones_rf[:, :], bvrow_t[:, q_ * 512:(q_ + 1) * 512],
                    start=True, stop=True))(q),
                    reads=["bvrow"], writes=["psg1"], tag=f"bvrep_mm{q}")
            P.add("v", (lambda q_: lambda v: v.tensor_copy(
                bvrep[:, q_ * 512:(q_ + 1) * 512], ps_gen[:, 512:1024]))(q),
                reads=["psg1"], writes=["bvrep"], tag=f"bvrep_cp{q}")
        P.barrier()
        p0ctx.close()

        FOLDS = [(0, fup_p, gup_s, hfoldup, 16, ITR),
                 (1, fgt_p, ggt_s, hfoldup, 16, ITR),
                 (2, fdn_p, gdn_s, hfolddn, 64, DM)]

        def make_fold_emitter(PSB_, PPS_, sfx):
            gtile = PSB_(f"gt{sfx}", [128, ITR], BF16)
            frhs = PSB_(f"fr{sfx}", [128, 2, 512], BF16)
            psf = PPS_(f"pf{sfx}", [128, 1024])

            def emit_bl(fw, bl):
                wi, src_t, dst_t, lhs, nbl, nj = fw
                njb = nj // 512
                for jb in range(njb):
                    P.add("g", (lambda s_, bl_, jb_: lambda g: g.dma_start(
                        out=frhs[:, jb_ % 2, :],
                        in_=s_[bl_][:, jb_ * 512:(jb_ + 1) * 512]))(src_t, bl, jb),
                        writes=[("frhs", sfx, jb % 2)], dma=True, tag=f"f{wi}_{bl}_{jb}")
                    P.add("t", (lambda l_, jb_: lambda t: t.matmul(
                        psf[:, (jb_ % 2) * 512:(jb_ % 2 + 1) * 512],
                        l_, frhs[:, jb_ % 2, :], start=True, stop=True))(lhs[:, :], jb),
                        reads=[("frhs", sfx, jb % 2), "hfoldup", "hfolddn"],
                        writes=[("psf", sfx, jb % 2)], tag=f"fm{wi}_{bl}_{jb}")
                    if jb % 2 == 0:
                        P.add("s", (lambda jb_: lambda s: s.activation(
                            gtile[:, jb_ * 512:(jb_ + 1) * 512],
                            psf[:, (jb_ % 2) * 512:(jb_ % 2 + 1) * 512], AF.Copy))(jb),
                            reads=[("psf", sfx, jb % 2)], writes=[("gtile", sfx, jb)],
                            tag=f"fe{wi}_{bl}_{jb}")
                    else:
                        P.add("v", (lambda jb_: lambda v: v.tensor_copy(
                            gtile[:, jb_ * 512:(jb_ + 1) * 512],
                            psf[:, (jb_ % 2) * 512:(jb_ % 2 + 1) * 512]))(jb),
                            reads=[("psf", sfx, jb % 2)], writes=[("gtile", sfx, jb)],
                            tag=f"fe{wi}_{bl}_{jb}")

                def gsp(d_, bl_, nbl_, nj_):
                    def f(sy):
                        dst = d_.rearrange("a (r bl) j -> (a r) bl j", bl=nbl_)[:, bl_, :]
                        return sy.dma_start(out=dst, in_=gtile[:, 0:nj_])
                    return f
                P.add("sy", gsp(dst_t, bl, nbl, nj),
                      reads=[("gtile", sfx, jb_) for jb_ in range(njb)],
                      writes=[f"g_s{wi}"], dma=True, tag=f"gsp{wi}_{bl}")
            return emit_bl


        if "P1" in phases:
            with contextlib.ExitStack() as px:
                PSB = lambda n_, s_, d_: px.enter_context(nc.sbuf_tensor(n_, s_, d_))
                PPS = lambda n_, s_: px.enter_context(nc.psum_tensor(n_, s_, F32))
                wqt = PSB("wqt", [128, 16, 16, 128], BF16)
                xtl = PSB("xtl", [128, 2, 16, 512], BF16)
                qraw = PSB("qraw", [128, 16, 512], F32)
                sqb = PSB("sqb", [128, 2, 512], BF16)
                qtt = PSB("qtt", [128, 2, 16, 512], BF16)
                pq = PPS("pq", [128, 1536])
                pstat = PPS("pstat", [1, 1024])
                prep = PPS("prep", [128, 512])
                fold_up = make_fold_emitter(PSB, PPS, "u") if "P5" in phases else None

                for m in range(16):
                    P.add("g", (lambda m_: lambda g: g.dma_start(
                        out=wqt[:, m_, :, :], in_=wq_p[m_].transpose([1, 0, 2])))(m),
                        writes=[("wqt", m)], dma=True, tag=f"wq{m}")
                for tt in range(4):
                    c0 = HALO_B + tt * 512
                    P.add("g", (lambda tt_, c_: lambda g: g.dma_start(
                        out=xtl[:, tt_ % 2, :, :], in_=xTv[:, :, c_:c_ + 512]))(tt, c0),
                        writes=[("xtl", tt % 2)], dma=True, tag=f"x{tt}")
                    for m in range(16):
                        def qmm(tt_, m_):
                            def f(t):
                                sl = pq[:, (m_ % 3) * 512:(m_ % 3 + 1) * 512]
                                for k in range(16):
                                    ins = t.matmul(sl, wqt[:, m_, k, :], xtl[:, tt_ % 2, k, :],
                                                   start=(k == 0), stop=(k == 15))
                                return ins
                            return f
                        P.add("t", qmm(tt, m),
                              reads=[("wqt", m), ("xtl", tt % 2)],
                              writes=[("pq", m % 3)], tag=f"q{tt}_{m}")
                        def qev(tt_, m_):
                            def f(s):
                                sl = pq[:, (m_ % 3) * 512:(m_ % 3 + 1) * 512]
                                bcol = cols[:, COL["bq"] + m_:COL["bq"] + m_ + 1]
                                s.activation(qraw[:, m_, :], sl, AF.Identity, bias=bcol)
                                return s.activation(sqb[:, m_ % 2, :], sl, AF.Square, bias=bcol)
                            return f
                        P.add("s", qev(tt, m), reads=[("pq", m % 3), "cols"],
                              writes=[("qraw", m), ("sqb", m % 2)], tag=f"qev{tt}_{m}")
                        P.add("t", (lambda m_: lambda t: t.matmul(
                            pstat[:, 0:512], ones_cb[:, :], sqb[:, m_ % 2, :],
                            start=(m_ == 0), stop=(m_ == 15)))(m),
                            reads=[("sqb", m % 2), "ones_cb"], writes=["pstat0"],
                            tag=f"qssq{tt}_{m}")
                    P.add("s", lambda s: s.activation(srow[:, :], pstat[:, 0:512], AF.Sqrt,
                                                      bias=epst[:, 0:1], scale=1.0 / DM),
                          reads=["pstat0", "epst"], writes=["srow"], tag=f"sq_sqrt{tt}")
                    P.add("v", lambda v: v.reciprocal(srow[:, :], srow[:, :]),
                          reads=["srow"], writes=["srow"], tag=f"sq_rcp{tt}")
                    P.add("t", lambda t: t.matmul(prep[:, :], ones_rf[:, :], srow[:, :],
                                                  start=True, stop=True),
                          reads=["srow", "ones_rf"], writes=["prep"], tag=f"sq_rep{tt}")
                    for m in range(16):
                        P.add("v", (lambda tt_, m_: lambda v: v.scalar_tensor_tensor(
                            qtt[:, tt_ % 2, m_, :], qraw[:, m_, :],
                            cols[:, COL["gq"] + m_:COL["gq"] + m_ + 1], prep[:, :],
                            ALU.mult, ALU.mult))(tt, m),
                            reads=[("qraw", m), "prep", "cols"],
                            writes=[("qtt", tt % 2, m)], tag=f"qt{tt}_{m}")
                        P.add("s", (lambda tt_, m_: lambda s: s.activation(
                            sqb[:, m_ % 2, :], qtt[:, tt_ % 2, m_, :], AF.Square))(tt, m),
                            reads=[("qtt", tt % 2, m)], writes=[("sqb", m % 2)],
                            tag=f"nq_sq{tt}_{m}")
                        P.add("t", (lambda m_: lambda t: t.matmul(
                            pstat[:, 512:1024], ones_cb[:, :], sqb[:, m_ % 2, :],
                            start=True, stop=True))(m),
                            reads=[("sqb", m % 2), "ones_cb"], writes=["pstat1"],
                            tag=f"nq_mm{tt}_{m}")
                        P.add("v", lambda v: v.reduce_max(rtmp[:, 0:1],
                                                          pstat[:, 512:1024], AX.X),
                              reads=["pstat1"], writes=["rtmpq"], tag=f"nq_rm{tt}_{m}")
                        P.add("v", (lambda m_: lambda v: v.tensor_max(
                            maxq2[:, m_:m_ + 1], maxq2[:, m_:m_ + 1], rtmp[:, 0:1]))(m),
                            reads=["rtmpq", "maxq2"], writes=["maxq2"],
                            tag=f"nq_mx{tt}_{m}")
                    P.add("sy", (lambda tt_: lambda sy: sy.dma_start(
                        out=qt_s[:, :, tt_ * 512:(tt_ + 1) * 512].transpose([1, 0, 2]),
                        in_=qtt[:, tt_ % 2, :, :]))(tt),
                        reads=[("qtt", tt % 2, m) for m in range(16)],
                        writes=["qt_s"], dma=True, tag=f"qspill{tt}")
                    if fold_up is not None:
                        for bl in range(tt * 4, tt * 4 + 4):
                            fold_up(FOLDS[0], bl)

        P.barrier()
        if "P2K" in phases:
            with contextlib.ExitStack() as px:
                PSB = lambda n_, s_, d_: px.enter_context(nc.sbuf_tensor(n_, s_, d_))
                PPS = lambda n_, s_: px.enter_context(nc.psum_tensor(n_, s_, F32))
                wkt = PSB("wkt", [128, 16, 16, 128], BF16)
                xtl2 = PSB("xtl2", [128, 2, 16, 512], BF16)
                kraw = PSB("kraw", [128, 16, 512], BF16)
                kbb = PSB("kbb", [128, 2, 16, 512], BF16)
                sqb2 = PSB("sqb2", [128, 2, 512], BF16)
                kts = PSB("kts", [128, 2, 16, 512], BF16)
                pk = PPS("pk", [128, 1024])
                pstat2 = PPS("pstat2", [1, 1024])
                prep2 = PPS("prep2", [128, 512])
                pkt = PPS("pkt", [128, 1024])

                for m in range(16):
                    P.add("g", (lambda m_: lambda g: g.dma_start(
                        out=wkt[:, m_, :, :], in_=wk_p[m_].transpose([1, 0, 2])))(m),
                        writes=[("wkt", m)], dma=True, tag=f"wk{m}")
                for tt in range(7):
                    c0 = tt * 512
                    P.add("g", (lambda tt_, c_: lambda g: g.dma_start(
                        out=xtl2[:, tt_ % 2, :, :], in_=xTv[:, :, c_:c_ + 512]))(tt, c0),
                        writes=[("xtl2", tt % 2)], dma=True, tag=f"kx{tt}")
                    for m in range(16):
                        def kmm(tt_, m_):
                            def f(t):
                                sl = pk[:, (m_ % 2) * 512:(m_ % 2 + 1) * 512]
                                for k in range(16):
                                    ins = t.matmul(sl, wkt[:, m_, k, :], xtl2[:, tt_ % 2, k, :],
                                                   start=(k == 0), stop=(k == 15))
                                return ins
                            return f
                        P.add("t", kmm(tt, m), reads=[("wkt", m), ("xtl2", tt % 2)],
                              writes=[("pk", m % 2)], tag=f"k{tt}_{m}")
                        def kev(tt_, m_):
                            def f(s):
                                sl = pk[:, (m_ % 2) * 512:(m_ % 2 + 1) * 512]
                                bcol = cols[:, COL["bk"] + m_:COL["bk"] + m_ + 1]
                                s.activation(kraw[:, m_, :], sl, AF.Identity, bias=bcol)
                                return s.activation(sqb2[:, m_ % 2, :], sl, AF.Square, bias=bcol)
                            return f
                        P.add("s", kev(tt, m), reads=[("pk", m % 2), "cols"],
                              writes=[("kraw", m), ("sqb2", m % 2)], tag=f"kev{tt}_{m}")
                        P.add("t", (lambda m_: lambda t: t.matmul(
                            pstat2[:, 0:512], ones_cb[:, :], sqb2[:, m_ % 2, :],
                            start=(m_ == 0), stop=(m_ == 15)))(m),
                            reads=[("sqb2", m % 2), "ones_cb"], writes=["pstat20"],
                            tag=f"kssq{tt}_{m}")
                    P.add("s", lambda s: s.activation(srow2[:, :], pstat2[:, 0:512], AF.Sqrt,
                                                      bias=epst[:, 0:1], scale=1.0 / DM),
                          reads=["pstat20", "epst"], writes=["srow2"], tag=f"sk_sqrt{tt}")
                    P.add("v", lambda v: v.reciprocal(srow2[:, :], srow2[:, :]),
                          reads=["srow2"], writes=["srow2"], tag=f"sk_rcp{tt}")
                    P.add("t", lambda t: t.matmul(prep2[:, :], ones_rf[:, :], srow2[:, :],
                                                  start=True, stop=True),
                          reads=["srow2", "ones_rf"], writes=["prep2"], tag=f"sk_rep{tt}")
                    for m in range(16):
                        P.add("v", (lambda tt_, m_: lambda v: v.tensor_mul(
                            kbb[:, tt_ % 2, m_, :], kraw[:, m_, :], prep2[:, :]))(tt, m),
                            reads=[("kraw", m), "prep2"], writes=[("kbb", tt % 2, m)],
                            tag=f"kb{tt}_{m}")
                        def ktmm(tt_, m_):
                            def f(t):
                                sl = pkt[:, (m_ % 2) * 512:(m_ % 2 + 1) * 512]
                                return t.matmul(sl, mixh[:, m_, :], kbb[:, tt_ % 2, m_, :],
                                                start=True, stop=True)
                            return f
                        P.add("t", ktmm(tt, m), reads=[("kbb", tt % 2, m), "mixh"],
                              writes=[("pkt", m % 2)], tag=f"ktm{tt}_{m}")
                        def ktev(tt_, m_):
                            def f(s):
                                sl = pkt[:, (m_ % 2) * 512:(m_ % 2 + 1) * 512]
                                s.activation(kts[:, tt_ % 2, m_, :], sl, AF.Copy)
                                return s.activation(sqb2[:, m_ % 2, :], sl, AF.Square)
                            return f
                        P.add("s", ktev(tt, m), reads=[("pkt", m % 2)],
                              writes=[("kts", tt % 2, m), ("sqb2", m % 2)], tag=f"ktev{tt}_{m}")
                        P.add("t", (lambda m_: lambda t: t.matmul(
                            pstat2[:, 512:1024], ones_cb[:, :], sqb2[:, m_ % 2, :],
                            start=True, stop=True))(m),
                            reads=[("sqb2", m % 2), "ones_cb"], writes=["pstat21"],
                            tag=f"nk_mm{tt}_{m}")
                        P.add("v", lambda v: v.reduce_max(rtmp[:, 1:2],
                                                          pstat2[:, 512:1024], AX.X),
                              reads=["pstat21"], writes=["rtmpk"], tag=f"nk_rm{tt}_{m}")
                        P.add("v", (lambda m_: lambda v: v.tensor_max(
                            maxk2[:, m_:m_ + 1], maxk2[:, m_:m_ + 1], rtmp[:, 1:2]))(m),
                            reads=["rtmpk", "maxk2"], writes=["maxk2"],
                            tag=f"nk_mx{tt}_{m}")
                    for ei in range(4):
                        P.add("sy", (lambda tt_, ei_: lambda sy: sy.dma_start(
                            out=kt_s[tt_ * 4 + ei_],
                            in_=kts[:, tt_ % 2, :, ei_ * 128:(ei_ + 1) * 128]))(tt, ei),
                            reads=[("kts", tt % 2, m) for m in range(16)],
                            writes=["kt_s"], dma=True, tag=f"kspill{tt}_{ei}")

        P.barrier()
        if "P2V" in phases:
            with contextlib.ExitStack() as px:
                PSB = lambda n_, s_, d_: px.enter_context(nc.sbuf_tensor(n_, s_, d_))
                PPS = lambda n_, s_: px.enter_context(nc.psum_tensor(n_, s_, F32))
                wvt = PSB("wvt", [128, 16, DM], BF16)
                xtl3 = PSB("xtl3", [128, 2, 16, 512], BF16)
                vts = PSB("vts", [128, 2, 4, DM], BF16)
                pv = PPS("pv", [128, 2048])
                fold_gd = make_fold_emitter(PSB, PPS, "d") if "P5" in phases else None
                fold_work = ([(FOLDS[1], bl) for bl in range(16)] +
                             [(FOLDS[2], bl) for bl in range(64)]) if fold_gd else []

                for k in range(16):
                    P.add("g", (lambda k_: lambda g: g.dma_start(
                        out=wvt[:, k_, :], in_=wv_p[k_]))(k),
                        writes=[("wvt", k)], dma=True, tag=f"wv{k}")
                for tt in range(7):
                    c0 = tt * 512
                    P.add("g", (lambda tt_, c_: lambda g: g.dma_start(
                        out=xtl3[:, tt_ % 2, :, :], in_=xTv[:, :, c_:c_ + 512]))(tt, c0),
                        writes=[("xtl3", tt % 2)], dma=True, tag=f"vx{tt}")
                    for ns in range(4):
                        def vmm(tt_, ns_):
                            def f(t):
                                for jb in range(4):
                                    sl = pv[:, jb * 512:(jb + 1) * 512]
                                    for k in range(16):
                                        ins = t.matmul(
                                            sl, xtl3[:, tt_ % 2, k, ns_ * 128:(ns_ + 1) * 128],
                                            wvt[:, k, jb * 512:(jb + 1) * 512],
                                            start=(k == 0), stop=(k == 15))
                                return ins
                            return f
                        P.add("t", vmm(tt, ns),
                              reads=[("xtl3", tt % 2)] + [("wvt", k) for k in range(16)],
                              writes=["pv"], tag=f"v{tt}_{ns}")
                        P.add("v", (lambda tt_, ns_: lambda v: v.tensor_add(
                            vts[:, tt_ % 2, ns_, :], pv[:, :], bvrep[:, :]))(tt, ns),
                            reads=["pv", "bvrep"], writes=[("vts", tt % 2, ns)],
                            tag=f"vev{tt}_{ns}")
                    for ei in range(4):
                        P.add("sy", (lambda tt_, ei_: lambda sy: sy.dma_start(
                            out=v_s[tt_ * 4 + ei_],
                            in_=vts[:, tt_ % 2, ei_, :].rearrange("p (h d) -> p h d", h=NH)))(tt, ei),
                            reads=[("vts", tt % 2, ei)],
                            writes=["v_s"], dma=True, tag=f"vspill{tt}_{ei}")
                    for _ in range(12):
                        if fold_work:
                            fold_gd(*fold_work.pop(0))
                while fold_work:
                    fold_gd(*fold_work.pop(0))

        P.barrier()
        if "P3" in phases:
            with contextlib.ExitStack() as aoctx:
                attnout = aoctx.enter_context(
                    nc.sbuf_tensor("attnout", [128, NH, NLT, 128], BF16))
                wot = aoctx.enter_context(
                    nc.sbuf_tensor("wot", [128, 16, 16, 128], BF16))
                p3ctx = contextlib.ExitStack()
                P3SB = lambda n_, s_, d_: p3ctx.enter_context(nc.sbuf_tensor(n_, s_, d_))
                P3PS = lambda n_, s_: p3ctx.enter_context(nc.psum_tensor(n_, s_, F32))
                ringk = P3SB("ringk", [128, RING, NHG, 128], BF16)
                ringv = P3SB("ringv", [128, RING, NHG, 128], BF16)
                qtb = P3SB("qtb", [128, 2, NHG, 128], BF16)
                expb = P3SB("expb", [128, 2, NCO, 128], BF16)
                pband = P3PS("pband", [128, 2048])
                patt = P3PS("patt", [128, 1024])
                pden = P3PS("pden", [1, 512])
                prep3 = P3PS("prep3", [128, 512])
                recrow = P3SB("recrow", [1, 512], F32)
                reprow = P3SB("reprow", [128, 512], F32)

                P.add("v", lambda v: v.tensor_mul(srow[:, 0:16], maxq2[:, :], maxk2[:, :]),
                      reads=["maxq2", "maxk2"], writes=["srow"], tag="b_mul")
                P.add("s", lambda s: s.activation(srow[:, 0:16], srow[:, 0:16], AF.Sqrt),
                      reads=["srow"], writes=["srow"], tag="b_sqrt")
                P.add("v", lambda v: v.tensor_scalar(srow[:, 0:16], srow[:, 0:16], -1.0,
                                                     BSHIFT, ALU.mult, ALU.add),
                      reads=["srow"], writes=["srow"], tag="b_neg")
                P.add("t", lambda t: t.matmul(prep3[:, 0:16], ones_rf[:, :], srow[:, 0:16],
                                              start=True, stop=True),
                      reads=["srow", "ones_rf"], writes=["prep3"], tag="b_rep")
                P.add("v", lambda v: v.tensor_copy(Bcols[:, :], prep3[:, 0:16]),
                      reads=["prep3"], writes=["Bcols"], tag="b_cp")

                for hg in range(NH // NHG):
                    h0 = hg * NHG
                    loaded = set()
                    for kcw in (hg * 2, hg * 2 + 1) if "P4" in phases else ():
                        P.add("g", (lambda kc_: lambda g: g.dma_start(
                            out=wot[:, kc_, :, :],
                            in_=wo_p[kc_].transpose([1, 0, 2])))(kcw),
                            writes=[("wot", kcw)], dma=True, tag=f"wol{kcw}")
                    for kcw in (8 + hg * 2, 8 + hg * 2 + 1) if "P4" in phases else ():
                        P.add("g", (lambda kc_: lambda g: g.dma_start(
                            out=wot[:, kc_, :, :],
                            in_=wo_p[kc_].transpose([1, 0, 2])))(kcw),
                            writes=[("wot", kcw)], dma=True, tag=f"wol{kcw}")

                    def ring_load(e, h0_=h0, hg_=hg):
                        P.add("sy", (lambda e_: lambda sy: sy.dma_start(
                            out=ringk[:, e_ % RING, :, :],
                            in_=kt_s[e_, :, h0_:h0_ + NHG, :]))(e),
                            reads=["kt_s"], writes=[("ringk", e % RING)],
                            dma=True, tag=f"rk{hg_}_{e}")
                        P.add("g", (lambda e_: lambda g: g.dma_start(
                            out=ringv[:, e_ % RING, :, :],
                            in_=v_s[e_, :, h0_:h0_ + NHG, :]))(e),
                            reads=["v_s"], writes=[("ringv", e % RING)],
                            dma=True, tag=f"rv{hg_}_{e}")
                        loaded.add(e)

                    for e in [0, 5, 6, 7, 8, 9, 10, 11]:
                        ring_load(e)
                    for lt in range(NLT):
                        for e in (lt, lt + 11):
                            if e not in loaded and 0 <= e < NEC:
                                ring_load(e)
                        P.add("sy", (lambda h0_, lt_: lambda sy: sy.dma_start(
                            out=qtb[:, lt_ % 2, :, :],
                            in_=qt_s[h0_:h0_ + NHG, :, lt_ * 128:(lt_ + 1) * 128]
                            .transpose([1, 0, 2])))(h0, lt),
                            reads=["qt_s"], writes=[("qtb", lt % 2)], dma=True,
                            tag=f"q3{hg}_{lt}")
                        for hh in range(NHG):
                            h = h0 + hh
                            uf = hh % 2
                            ecs = [lt + 8 + co for co in COFFS]
                            def smm(hh_, uf_, lt_, ecs_):
                                def f(t):
                                    for ci, e in enumerate(ecs_):
                                        sl = pband[:, uf_ * 1024 + ci * 128:
                                                   uf_ * 1024 + (ci + 1) * 128]
                                        ins = t.matmul(sl, ringk[:, e % RING, hh_, :],
                                                       qtb[:, lt_ % 2, hh_, :],
                                                       start=True, stop=True)
                                    return ins
                                return f
                            P.add("t", smm(hh, uf, lt, ecs),
                                  reads=[("ringk", e % RING) for e in ecs] +
                                        [("qtb", lt % 2)],
                                  writes=[("pband", uf)], tag=f"sc{hg}_{lt}_{hh}")
                            P.add("s", (lambda h_, uf_: lambda s: s.activation(
                                expb[:, uf_, :, :], pband[:, uf_ * 1024:(uf_ + 1) * 1024],
                                AF.Exp, bias=Bcols[:, h_:h_ + 1]))(h, uf),
                                reads=[("pband", uf), "Bcols"], writes=[("expb", uf)],
                                tag=f"ex{hg}_{lt}_{hh}")
                            P.add("v", (lambda uf_: lambda v: v.tensor_mul(
                                expb[:, uf_, :, :], expb[:, uf_, :, :], masks[:, :, :]))(uf),
                                reads=[("expb", uf), "masks"], writes=[("expb", uf)],
                                tag=f"mk{hg}_{lt}_{hh}")
                            def avmm(hh_, uf_, lt_, ecs_):
                                def f(t):
                                    po = (lt_ % 2) * 512
                                    for ci, e in enumerate(ecs_):
                                        ins = t.matmul(
                                            patt[:, po + hh_ * 128:po + (hh_ + 1) * 128],
                                            ringv[:, e % RING, hh_, :],
                                            expb[:, uf_, ci, :],
                                            start=(ci == 0), stop=(ci == 7))
                                    return ins
                                return f
                            P.add("t", avmm(hh, uf, lt, ecs),
                                  reads=[("ringv", e % RING) for e in ecs] +
                                        [("expb", uf)],
                                  writes=[("patt", lt % 2, hh)], tag=f"av{hg}_{lt}_{hh}")
                            def denmm(hh_, uf_, lt_):
                                def f(t):
                                    for ci in range(NCO):
                                        ins = t.matmul(
                                            pden[:, hh_ * 128:(hh_ + 1) * 128],
                                            ones_cb[:, :], expb[:, uf_, ci, :],
                                            start=(ci == 0), stop=(ci == 7))
                                    return ins
                                return f
                            P.add("t", denmm(hh, uf, lt),
                                  reads=[("expb", uf), "ones_cb"],
                                  writes=[("pden", hh)], tag=f"dn{hg}_{lt}_{hh}")
                        P.add("v", lambda v: v.reciprocal(recrow[:, :], pden[:, :]),
                              reads=[("pden", 0), ("pden", 1), ("pden", 2), ("pden", 3)],
                              writes=["recrow"], tag=f"rc{hg}_{lt}")
                        P.add("t", lambda t: t.matmul(prep3[:, :], ones_rf[:, :],
                                                      recrow[:, :], start=True, stop=True),
                              reads=["recrow", "ones_rf"], writes=["prep3"],
                              tag=f"rcrep{hg}_{lt}")
                        P.add("v", lambda v: v.tensor_copy(reprow[:, :], prep3[:, :]),
                              reads=["prep3"], writes=["reprow"], tag=f"rpc{hg}_{lt}")
                        def aoev(hg_, lt_):
                            def f(v):
                                po = (lt_ % 2) * 512
                                return v.tensor_mul(
                                    attnout[:, hg_ * NHG:(hg_ + 1) * NHG, lt_, :],
                                    patt[:, po:po + 512].rearrange("p (h l) -> p h l", h=NHG),
                                    reprow[:, :].rearrange("p (h l) -> p h l", h=NHG))
                            return f
                        P.add("v", aoev(hg, lt),
                              reads=[("patt", lt % 2, hh) for hh in range(NHG)] + ["reprow"],
                              writes=[("attnout", hg, lt)], tag=f"ao{hg}_{lt}")
                if dbg_ao:
                    P.add("sy", lambda sy: sy.dma_start(
                        out=ao_s[:, :, :].transpose([1, 0, 2]),
                        in_=attnout[:, :, :, :].rearrange("p h lt l -> p h (lt l)")),
                        reads=[("attnout", hg_, lt_) for hg_ in range(4) for lt_ in range(NLT)],
                        writes=["ao_s"], dma=True, tag="ao_dump")

                P.barrier()
                p3ctx.close()
                if "P4" in phases:
                    with contextlib.ExitStack() as px:
                        PSB = lambda n_, s_, d_: px.enter_context(nc.sbuf_tensor(n_, s_, d_))
                        PPS = lambda n_, s_: px.enter_context(nc.psum_tensor(n_, s_, F32))
                        v1b = PSB("v1b", [128, 16, 512], F32)
                        xres = PSB("xres", [128, 2, 512], F32)
                        v1c = PSB("v1c", [128, 2, 512], BF16)
                        pso = PPS("pso", [128, 1024])
                        psl1 = PPS("psl1", [1, 512])
                        psl2 = PPS("psl2", [1, 512])
                        psr1 = PPS("psr1", [128, 512])
                        psr2 = PPS("psr2", [128, 512])
                        mrow = PSB("mrow", [1, 512], F32)
                        irow = PSB("irow", [1, 512], F32)

                        for sgi in range(4):
                            for kc in range(16):
                                P.add("sy", (lambda sgi_, kc_: lambda sy: sy.dma_start(
                                    out=xres[:, kc_ % 2, :],
                                    in_=xT[kc_ * 128:(kc_ + 1) * 128,
                                           HALO_B + sgi_ * 512:HALO_B + (sgi_ + 1) * 512]))(sgi, kc),
                                    writes=[("xres", kc % 2)], dma=True,
                                    tag=f"xr{sgi}_{kc}")
                                def omm(sgi_, kc_):
                                    def f(t):
                                        sl = pso[:, (kc_ % 2) * 512:(kc_ % 2 + 1) * 512]
                                        lt0 = sgi_ * 4
                                        for hc in range(16):
                                            rhs = attnout[:, hc, lt0:lt0 + 4, :]
                                            ins = t.matmul(sl, wot[:, kc_, hc, :],
                                                           rhs, start=(hc == 0),
                                                           stop=(hc == 15))
                                        return ins
                                    return f
                                P.add("t", omm(sgi, kc),
                                      reads=[("wot", kc)] +
                                            [("attnout", hg_, sgi * 4 + i_)
                                             for hg_ in range(4) for i_ in range(4)],
                                      writes=[("pso", kc % 2)], tag=f"o{sgi}_{kc}")
                                def ov(sgi_, kc_):
                                    def f(v):
                                        sl = pso[:, (kc_ % 2) * 512:(kc_ % 2 + 1) * 512]
                                        return v.scalar_tensor_tensor(
                                            v1b[:, kc_, :], sl,
                                            cols[:, COL["bo"] + kc_:COL["bo"] + kc_ + 1],
                                            xres[:, kc_ % 2, :], ALU.add, ALU.add)
                                    return f
                                P.add("v", ov(sgi, kc),
                                      reads=[("pso", kc % 2), ("xres", kc % 2), "cols"],
                                      writes=[("v1b", kc)], tag=f"ov{sgi}_{kc}")
                                P.add("s", (lambda kc_: lambda s: s.activation(
                                    v1c[:, 0, :], v1b[:, kc_, :], AF.Copy))(kc),
                                    reads=[("v1b", kc)], writes=[("v1c", 0)],
                                    tag=f"oc{sgi}_{kc}")
                                P.add("t", (lambda kc_: lambda t: t.matmul(
                                    psl1[:, :], ones_cb[:, :], v1c[:, 0, :],
                                    start=(kc_ == 0), stop=(kc_ == 15)))(kc),
                                    reads=[("v1c", 0), "ones_cb"], writes=["psl1"],
                                    tag=f"os1_{sgi}_{kc}")
                                P.add("s", (lambda kc_: lambda s: s.activation(
                                    v1c[:, 1, :], v1b[:, kc_, :], AF.Square))(kc),
                                    reads=[("v1b", kc)], writes=[("v1c", 1)],
                                    tag=f"oq{sgi}_{kc}")
                                P.add("t", (lambda kc_: lambda t: t.matmul(
                                    psl2[:, :], ones_cb[:, :], v1c[:, 1, :],
                                    start=(kc_ == 0), stop=(kc_ == 15)))(kc),
                                    reads=[("v1c", 1), "ones_cb"], writes=["psl2"],
                                    tag=f"os2_{sgi}_{kc}")
                            def lnrow(s):
                                s.activation(mrow[:, :], psl1[:, :], AF.Copy, scale=1.0 / DM)
                                return s.activation(irow[:, :], psl2[:, :], AF.Copy,
                                                    scale=1.0 / DM)
                            P.add("s", lnrow, reads=["psl1", "psl2"],
                                  writes=["mrow", "irow"], tag=f"lnr{sgi}")
                            P.add("v", lambda v: v.scalar_tensor_tensor(
                                srow[:, 0:512], mrow[:, :], 1.0, mrow[:, :],
                                ALU.mult, ALU.mult),
                                reads=["mrow"], writes=["srow"], tag=f"lnva{sgi}")
                            P.add("v", lambda v: v.tensor_sub(irow[:, :], irow[:, :],
                                                              srow[:, 0:512]),
                                  reads=["srow", "irow"], writes=["irow"], tag=f"lnvb{sgi}")
                            P.add("s", lambda s: s.activation(irow[:, :], irow[:, :],
                                                              AF.Sqrt, bias=epst[:, 1:2]),
                                  reads=["irow", "epst"], writes=["irow"], tag=f"lns{sgi}")
                            P.add("v", lambda v: v.reciprocal(irow[:, :], irow[:, :]),
                                  reads=["irow"], writes=["irow"], tag=f"lnca{sgi}")
                            P.add("v", lambda v: v.tensor_mul(mrow[:, :], mrow[:, :],
                                                              irow[:, :]),
                                  reads=["irow", "mrow"], writes=["mrow"], tag=f"lncb{sgi}")
                            P.add("t", lambda t: t.matmul(psr1[:, :], ones_rf[:, :],
                                                          irow[:, :], start=True, stop=True),
                                  reads=["irow", "ones_rf"], writes=["psr1"],
                                  tag=f"lnR1{sgi}")
                            P.add("t", lambda t: t.matmul(psr2[:, :], ones_rf[:, :],
                                                          mrow[:, :], start=True, stop=True),
                                  reads=["mrow", "ones_rf"], writes=["psr2"],
                                  tag=f"lnR2{sgi}")
                            for kc in range(16):
                                P.add("v", (lambda kc_: lambda v: v.tensor_mul(
                                    v1b[:, kc_, :], v1b[:, kc_, :], psr1[:, :]))(kc),
                                    reads=[("v1b", kc), "psr1"], writes=[("v1b", kc)],
                                    tag=f"y1a{sgi}_{kc}")
                                P.add("v", (lambda kc_: lambda v: v.tensor_sub(
                                    v1b[:, kc_, :], v1b[:, kc_, :], psr2[:, :]))(kc),
                                    reads=[("v1b", kc), "psr2"], writes=[("v1b", kc)],
                                    tag=f"y1b{sgi}_{kc}")
                            P.add("sy", (lambda sgi_: lambda sy: sy.dma_start(
                                out=y1_s[:, :, sgi_ * 512:(sgi_ + 1) * 512]
                                .transpose([1, 0, 2]),
                                in_=v1b[:, :, :]))(sgi),
                                reads=[("v1b", kc_) for kc_ in range(16)],
                                writes=["y1_s"], dma=True, tag=f"y1sp{sgi}")

        P.barrier()
        P.barrier()
        if "P6" in phases:
            with contextlib.ExitStack() as px:
                PSB = lambda n_, s_, d_: px.enter_context(nc.sbuf_tensor(n_, s_, d_))
                PPS = lambda n_, s_: px.enter_context(nc.psum_tensor(n_, s_, F32))
                w2048 = PSB("w2048", [128, 16, TC], BF16)
                y1bt = PSB("y1bt", [128, 2, TC], BF16)
                suu = PSB("suu", [128, 16], F32)
                sud = PSB("sud", [128, 64], F32)
                hsa = PSB("hsa", [128, 2, 128], BF16)
                gupsl = PSB("gupsl", [128, 2, 16, 128], BF16)
                ggtsl = PSB("ggtsl", [128, 2, 16, 128], BF16)
                silub = PSB("silub", [128, 2, 512], BF16)
                upb = PSB("upb", [128, 2, 512], BF16)
                x1b = PSB("x1b", [128, 2, 512], BF16)
                w1b = PSB("w1b", [128, 2, TC], BF16)
                psu = PPS("psu", [128, 1024])
                psg6 = PPS("psg6", [128, 1024])
                psw = PPS("psw", [128, 1024])

                P.add("v", lambda v: v.tensor_scalar(suu[:, :], cols[:, COL["su_ug"]:
                                                     COL["su_ug"] + 16],
                                                     float(DM) ** -0.5, None, ALU.mult),
                      reads=["cols"], writes=["suu"], tag="suu")
                P.add("v", lambda v: v.tensor_scalar(sud[:, :], cols[:, COL["su_dp"]:
                                                     COL["su_dp"] + 64],
                                                     float(ITR) ** -0.5, None, ALU.mult),
                      reads=["cols"], writes=["sud"], tag="sud")
                for a in range(16):
                    P.add("g", (lambda a_: lambda g: g.dma_start(
                        out=y1bt[:, a_ % 2, :], in_=y1_s[a_]))(a),
                        reads=["y1_s"], writes=[("y1bt", a % 2)], dma=True, tag=f"y1l{a}")
                    P.add("v", (lambda a_: lambda v: v.tensor_scalar(
                        hsa[:, a_ % 2, :], h128b[:, :], suu[:, a_:a_ + 1], None,
                        ALU.mult))(a),
                        reads=["h128b", "suu"], writes=[("hsa", a % 2)], tag=f"hsa{a}")
                    for lb in range(4):
                        P.add("t", (lambda a_, lb_: lambda t: t.matmul(
                            psu[:, (lb_ % 2) * 512:(lb_ % 2 + 1) * 512], hsa[:, a_ % 2, :],
                            y1bt[:, a_ % 2, lb_ * 512:(lb_ + 1) * 512],
                            start=True, stop=True))(a, lb),
                            reads=[("hsa", a % 2), ("y1bt", a % 2)],
                            writes=[("psu", lb % 2)], tag=f"sa{a}_{lb}")
                        P.add("s", (lambda a_, lb_: lambda s: s.activation(
                            w2048[:, a_, lb_ * 512:(lb_ + 1) * 512],
                            psu[:, (lb_ % 2) * 512:(lb_ % 2 + 1) * 512], AF.Copy))(a, lb),
                            reads=[("psu", lb % 2)], writes=[("w2048", a)],
                            tag=f"sae{a}_{lb}")
                for i in range(64):
                    P.add("g", (lambda i_: lambda g: g.dma_start(
                        out=gupsl[:, i_ % 2, :, :],
                        in_=gup_s[:, :, i_ * 128:(i_ + 1) * 128].transpose([1, 0, 2])))(i),
                        reads=["g_s0"], writes=[("gupsl", i % 2)], dma=True, tag=f"gu{i}")
                    P.add("g", (lambda i_: lambda g: g.dma_start(
                        out=ggtsl[:, i_ % 2, :, :],
                        in_=ggt_s[:, :, i_ * 128:(i_ + 1) * 128].transpose([1, 0, 2])))(i),
                        reads=["g_s1"], writes=[("ggtsl", i % 2)], dma=True, tag=f"gg{i}")
                    P.add("v", (lambda i_: lambda v: v.tensor_scalar(
                        hsa[:, i_ % 2, :], h128b[:, :], sud[:, i_:i_ + 1], None,
                        ALU.mult))(i),
                        reads=["h128b", "sud"], writes=[("hsa", i % 2)], tag=f"hsd{i}")
                    for lb in range(4):
                        def upmm(i_, lb_):
                            def f(t):
                                sl = psu[:, (lb_ % 2) * 512:(lb_ % 2 + 1) * 512]
                                for a in range(16):
                                    ins = t.matmul(sl, gupsl[:, i_ % 2, a, :],
                                                   w2048[:, a, lb_ * 512:(lb_ + 1) * 512],
                                                   start=(a == 0), stop=(a == 15))
                                return ins
                            return f
                        P.add("t", upmm(i, lb),
                              reads=[("gupsl", i % 2)] + [("w2048", a_) for a_ in range(16)],
                              writes=[("psu", lb % 2)], tag=f"up{i}_{lb}")
                        def gtmm(i_, lb_):
                            def f(t):
                                sl = psg6[:, (lb_ % 2) * 512:(lb_ % 2 + 1) * 512]
                                for a in range(16):
                                    ins = t.matmul(sl, ggtsl[:, i_ % 2, a, :],
                                                   w2048[:, a, lb_ * 512:(lb_ + 1) * 512],
                                                   start=(a == 0), stop=(a == 15))
                                return ins
                            return f
                        P.add("t", gtmm(i, lb),
                              reads=[("ggtsl", i % 2)] + [("w2048", a_) for a_ in range(16)],
                              writes=[("psg6", lb % 2)], tag=f"gt{i}_{lb}")
                        P.add("s", (lambda i_, lb_: lambda s: s.activation(
                            upb[:, lb_ % 2, :], psu[:, (lb_ % 2) * 512:(lb_ % 2 + 1) * 512],
                            AF.Identity, scale=cols[:, COL["wsu"] + i_:COL["wsu"] + i_ + 1]))(i, lb),
                            reads=[("psu", lb % 2), "cols"], writes=[("upb", lb % 2)],
                            tag=f"ue{i}_{lb}")
                        P.add("s", (lambda i_, lb_: lambda s: s.activation(
                            silub[:, lb_ % 2, :], psg6[:, (lb_ % 2) * 512:(lb_ % 2 + 1) * 512],
                            AF.Silu, scale=cols[:, COL["wsg"] + i_:COL["wsg"] + i_ + 1]))(i, lb),
                            reads=[("psg6", lb % 2), "cols"], writes=[("silub", lb % 2)],
                            tag=f"se{i}_{lb}")
                        P.add("v", (lambda lb_: lambda v: v.tensor_mul(
                            x1b[:, lb_ % 2, :], silub[:, lb_ % 2, :], upb[:, lb_ % 2, :]))(lb),
                            reads=[("silub", lb % 2), ("upb", lb % 2)],
                            writes=[("x1b", lb % 2)], tag=f"x1{i}_{lb}")
                        P.add("t", (lambda i_, lb_: lambda t: t.matmul(
                            psw[:, (lb_ % 2) * 512:(lb_ % 2 + 1) * 512], hsa[:, i_ % 2, :],
                            x1b[:, lb_ % 2, :], start=True, stop=True))(i, lb),
                            reads=[("hsa", i % 2), ("x1b", lb % 2)], writes=[("psw", lb % 2)],
                            tag=f"sw{i}_{lb}")
                        P.add("s", (lambda i_, lb_: lambda s: s.activation(
                            w1b[:, i_ % 2, lb_ * 512:(lb_ + 1) * 512],
                            psw[:, (lb_ % 2) * 512:(lb_ % 2 + 1) * 512], AF.Copy))(i, lb),
                            reads=[("psw", lb % 2)], writes=[("w1b", i % 2, lb)],
                            tag=f"we{i}_{lb}")
                    P.add("sy", (lambda i_: lambda sy: sy.dma_start(
                        out=w1_s[i_], in_=w1b[:, i_ % 2, :]))(i),
                        reads=[("w1b", i % 2, lb_) for lb_ in range(4)],
                        writes=["w1_s"], dma=True, tag=f"w1sp{i}")

        P.barrier()
        if "P7" in phases:
            with contextlib.ExitStack() as px:
                PSB = lambda n_, s_, d_: px.enter_context(nc.sbuf_tensor(n_, s_, d_))
                PPS = lambda n_, s_: px.enter_context(nc.psum_tensor(n_, s_, F32))
                w1lt = PSB("w1lt", [128, 64, 512], BF16)
                gdnsl = PSB("gdnsl", [128, 2, 64, 128], BF16)
                y1sl = PSB("y1sl", [128, 2, 512], F32)
                v2b = PSB("v2b", [128, 16, 512], F32)
                v2c = PSB("v2c", [128, 2, 512], BF16)
                outb = PSB("outb", [128, 2048], F32)
                mrow2 = PSB("mrow2", [1, 512], F32)
                irow2 = PSB("irow2", [1, 512], F32)
                psd = PPS("psd", [128, 1024])
                psl12 = PPS("psl12", [1, 512])
                psl22 = PPS("psl22", [1, 512])
                psr12 = PPS("psr12", [128, 512])
                psr22 = PPS("psr22", [128, 512])
                pstr = PPS("pstr", [128, 512])

                for lg in range(4):
                    P.add("sy", (lambda lg_: lambda sy: sy.dma_start(
                        out=w1lt[:, :, :],
                        in_=w1_s[:, :, lg_ * 512:(lg_ + 1) * 512].transpose([1, 0, 2])))(lg),
                        reads=["w1_s"], writes=["w1lt"], dma=True, tag=f"w1l{lg}")
                    for kc in range(16):
                        P.add("g", (lambda lg_, kc_: lambda g: g.dma_start(
                            out=gdnsl[:, kc_ % 2, :, :],
                            in_=gdn_s[:, :, kc_ * 128:(kc_ + 1) * 128]
                            .transpose([1, 0, 2])))(lg, kc),
                            reads=["g_s2"], writes=[("gdnsl", kc % 2)], dma=True,
                            tag=f"gd{lg}_{kc}")
                        P.add("sy", (lambda lg_, kc_: lambda sy: sy.dma_start(
                            out=y1sl[:, kc_ % 2, :],
                            in_=y1_s[kc_][:, lg_ * 512:(lg_ + 1) * 512]))(lg, kc),
                            reads=["y1_s"], writes=[("y1sl", kc % 2)], dma=True,
                            tag=f"y1r{lg}_{kc}")
                        def dmm(lg_, kc_):
                            def f(t):
                                sl = psd[:, (kc_ % 2) * 512:(kc_ % 2 + 1) * 512]
                                for i in range(64):
                                    ins = t.matmul(sl, gdnsl[:, kc_ % 2, i, :],
                                                   w1lt[:, i, :],
                                                   start=(i == 0), stop=(i == 63))
                                return ins
                            return f
                        P.add("t", dmm(lg, kc), reads=[("gdnsl", kc % 2), "w1lt"],
                              writes=[("psd", kc % 2)], tag=f"d{lg}_{kc}")
                        def dv(lg_, kc_):
                            def f(v):
                                sl = psd[:, (kc_ % 2) * 512:(kc_ % 2 + 1) * 512]
                                return v.scalar_tensor_tensor(
                                    v2b[:, kc_, :], sl,
                                    cols[:, COL["wsd"] + kc_:COL["wsd"] + kc_ + 1],
                                    y1sl[:, kc_ % 2, :], ALU.mult, ALU.add)
                            return f
                        P.add("v", dv(lg, kc), reads=[("psd", kc % 2), ("y1sl", kc % 2),
                                                      "cols"],
                              writes=[("v2b", kc)], tag=f"dv{lg}_{kc}")
                        P.add("s", (lambda kc_: lambda s: s.activation(
                            v2c[:, 0, :], v2b[:, kc_, :], AF.Copy))(kc),
                            reads=[("v2b", kc)], writes=[("v2c", 0)], tag=f"dc{lg}_{kc}")
                        P.add("t", (lambda kc_: lambda t: t.matmul(
                            psl12[:, :], ones_cb[:, :], v2c[:, 0, :],
                            start=(kc_ == 0), stop=(kc_ == 15)))(kc),
                            reads=[("v2c", 0), "ones_cb"], writes=["psl12"],
                            tag=f"ds1_{lg}_{kc}")
                        P.add("s", (lambda kc_: lambda s: s.activation(
                            v2c[:, 1, :], v2b[:, kc_, :], AF.Square))(kc),
                            reads=[("v2b", kc)], writes=[("v2c", 1)], tag=f"dq{lg}_{kc}")
                        P.add("t", (lambda kc_: lambda t: t.matmul(
                            psl22[:, :], ones_cb[:, :], v2c[:, 1, :],
                            start=(kc_ == 0), stop=(kc_ == 15)))(kc),
                            reads=[("v2c", 1), "ones_cb"], writes=["psl22"],
                            tag=f"ds2_{lg}_{kc}")
                    def ln2row(s):
                        s.activation(mrow2[:, :], psl12[:, :], AF.Copy, scale=1.0 / DM)
                        return s.activation(irow2[:, :], psl22[:, :], AF.Copy, scale=1.0 / DM)
                    P.add("s", ln2row, reads=["psl12", "psl22"],
                          writes=["mrow2", "irow2"], tag=f"l2r{lg}")
                    P.add("v", lambda v: v.scalar_tensor_tensor(
                        srow[:, 0:512], mrow2[:, :], 1.0, mrow2[:, :], ALU.mult, ALU.mult),
                        reads=["mrow2"], writes=["srow"], tag=f"l2va{lg}")
                    P.add("v", lambda v: v.tensor_sub(irow2[:, :], irow2[:, :],
                                                      srow[:, 0:512]),
                          reads=["srow", "irow2"], writes=["irow2"], tag=f"l2vb{lg}")
                    P.add("s", lambda s: s.activation(irow2[:, :], irow2[:, :], AF.Sqrt,
                                                      bias=epst[:, 1:2]),
                          reads=["irow2", "epst"], writes=["irow2"], tag=f"l2s{lg}")
                    P.add("v", lambda v: v.reciprocal(irow2[:, :], irow2[:, :]),
                          reads=["irow2"], writes=["irow2"], tag=f"l2ca{lg}")
                    P.add("v", lambda v: v.tensor_mul(mrow2[:, :], mrow2[:, :],
                                                      irow2[:, :]),
                          reads=["irow2", "mrow2"], writes=["mrow2"], tag=f"l2cb{lg}")
                    P.add("t", lambda t: t.matmul(psr12[:, :], ones_rf[:, :], irow2[:, :],
                                                  start=True, stop=True),
                          reads=["irow2", "ones_rf"], writes=["psr12"], tag=f"l2R1{lg}")
                    P.add("t", lambda t: t.matmul(psr22[:, :], ones_rf[:, :], mrow2[:, :],
                                                  start=True, stop=True),
                          reads=["mrow2", "ones_rf"], writes=["psr22"], tag=f"l2R2{lg}")
                    for kc in range(16):
                        P.add("v", (lambda kc_: lambda v: v.tensor_mul(
                            v2b[:, kc_, :], v2b[:, kc_, :], psr12[:, :]))(kc),
                            reads=[("v2b", kc), "psr12"], writes=[("v2b", kc)],
                            tag=f"y2a{lg}_{kc}")
                        P.add("v", (lambda kc_: lambda v: v.tensor_sub(
                            v2b[:, kc_, :], v2b[:, kc_, :], psr22[:, :]))(kc),
                            reads=[("v2b", kc), "psr22"], writes=[("v2b", kc)],
                            tag=f"y2b{lg}_{kc}")
                    for ls in range(4):
                        for kc in range(16):
                            P.add("t", (lambda kc_, ls_: lambda t: t.transpose(
                                pstr[:, 0:128], v2b[:, kc_, ls_ * 128:(ls_ + 1) * 128],
                                identf[:, :]))(kc, ls),
                                reads=[("v2b", kc), "identf"], writes=["pstr"],
                                tag=f"tr{lg}_{ls}_{kc}")
                            P.add("v", (lambda kc_: lambda v: v.tensor_copy(
                                outb[:, kc_ * 128:(kc_ + 1) * 128], pstr[:, 0:128]))(kc),
                                reads=["pstr"], writes=[("outb", kc)], tag=f"te{lg}_{ls}_{kc}")
                        P.add("sy", (lambda lg_, ls_: lambda sy: sy.dma_start(
                            out=out_t[lg_ * 512 + ls_ * 128:lg_ * 512 + (ls_ + 1) * 128, :],
                            in_=outb[:, :]))(lg, ls),
                            reads=[("outb", kc_) for kc_ in range(16)],
                            writes=["out"], dma=True, tag=f"ost{lg}_{ls}")

        P.add("sy", None, reads=["out"] + (["qt_s", "kt_s", "v_s", "ao_s", "y1_s",
                                           "w1_s", "g_s0", "g_s1", "g_s2"]
                                          if debug else []), tag="final")

        runner = P.emit(nc, {k: sem_h[k] for k in sem_h})
        with nc.Block("main") as b:
            b.gpsimd(lambda g: runner(g, "g"))
            b.sync(lambda sy: runner(sy, "sy"))
            b.tensor(lambda t: runner(t, "t"))
            b.vector(lambda v: runner(v, "v"))
            b.scalar(lambda s: runner(s, "s"))
    return nc


def prepack(inputs):
    x = np.asarray(inputs["L0_attn_x"])[0]
    xT = np.ascontiguousarray(x.T)
    d = {}
    wq = np.asarray(inputs["L0_attn_wq"])
    wk = np.asarray(inputs["L0_attn_wk"])
    wv = np.asarray(inputs["L0_attn_wv"])
    wo = np.asarray(inputs["L0_attn_wo"])
    d["wq_p"] = np.ascontiguousarray(
        wq.T.reshape(16, 128, 16, 128).transpose(2, 0, 1, 3))
    d["wk_p"] = np.ascontiguousarray(
        wk.T.reshape(16, 128, 16, 128).transpose(2, 0, 1, 3))
    d["wv_p"] = np.ascontiguousarray(wv.T.reshape(16, 128, DM))
    d["wo_p"] = np.ascontiguousarray(
        wo.T.reshape(16, 128, 16, 128).transpose(2, 0, 1, 3))
    wup = np.asarray(inputs["L0_ffn_W_up"])
    wgt = np.asarray(inputs["L0_ffn_W_gate"])
    wdn = np.asarray(inputs["L0_ffn_W_down"])
    d["fup_p"] = np.ascontiguousarray(
        wup.T.reshape(16, 8, 16, ITR).transpose(2, 0, 1, 3).reshape(16, 128, ITR))
    d["fgt_p"] = np.ascontiguousarray(
        wgt.T.reshape(16, 8, 16, ITR).transpose(2, 0, 1, 3).reshape(16, 128, ITR))
    d["fdn_p"] = np.ascontiguousarray(
        wdn.T.reshape(64, 2, 64, DM).transpose(2, 0, 1, 3).reshape(64, 128, DM))
    colpk = np.zeros((128, NCOLS), np.float32)

    def put(name, vec):
        w = vec.reshape(-1, 128).T
        colpk[:, COL[name]:COL[name] + w.shape[1]] = w
    put("bq", np.asarray(inputs["L0_attn_bq"]))
    put("bk", np.asarray(inputs["L0_attn_bk"]))
    put("gq", np.asarray(inputs["L0_attn_gq"]))
    put("gk", np.asarray(inputs["L0_attn_gk"]))
    put("bo", np.asarray(inputs["L0_attn_bo"]))
    put("su_ug", np.asarray(inputs["L0_ffn_SU_ug"]))
    put("su_dp", np.asarray(inputs["L0_ffn_SU_dp"]))
    put("wsu", np.asarray(inputs["L0_ffn_Wscale_ug"])[:ITR])
    put("wsg", np.asarray(inputs["L0_ffn_Wscale_ug"])[ITR:])
    put("wsd", np.asarray(inputs["L0_ffn_Wscale_dp"]))
    d["colpk"] = colpk
    d["bv_row"] = np.ascontiguousarray(np.asarray(inputs["L0_attn_bv"])[None, :])
    d["wmix_in"] = np.ascontiguousarray(np.asarray(inputs["L0_attn_w_mix"]))
    d.update(make_consts())
    maps = []
    for c in range(NCORE):
        idx = (np.arange(EXTP) + c * TC - HALO_B) % L
        m = dict(d)
        m["xT"] = np.ascontiguousarray(xT[:, idx])
        maps.append(m)
    return maps


LAST = None


def kernel(**inputs):
    global LAST
    maps = prepack(inputs)
    nc = build(debug=DEBUG, phases=PHASES)
    res = run_bass_kernel_spmd(nc, maps, core_ids=list(range(NCORE)))
    LAST = res
    out = np.concatenate([res.results[c]["out"] for c in range(NCORE)], axis=0)
    return out.reshape(1, L, DM)


# version 09
# speedup vs baseline: 1.3944x; 1.3825x over previous
import contextlib
import numpy as np
import ml_dtypes
import concourse.bass as bass
import concourse.mybir as mybir
from concourse.bass_utils import run_bass_kernel_spmd

F32 = mybir.dt.float32
BF16 = mybir.dt.bfloat16
AF = mybir.ActivationFunctionType
ALU = mybir.AluOpType
AX = mybir.AxisListType
BF = ml_dtypes.bfloat16

L = 16384
DM = 2048
NH = 16
ITR = 8192
NCORE = 8
TC = L // NCORE
HALO_B = 1024
EXTP = 3584
NEC = EXTP // 128
SEQ_SHIFTS = [0, 1, -1, 3, -3, 7, -7, 20, -20, 53, -53, 141, -141, 380, -380, 1024]
OFFS = sorted(-s for s in SEQ_SHIFTS)
COFFS = [-8, -3, -2, -1, 0, 1, 2, 3]
NCO = len(COFFS)
CHAN = [0, 1, 2, 4, 8]
SCALE = float(128) ** -0.5
RMS_EPS = 1e-6
LN_EPS = 1e-5
NLT = TC // 128
RING = 13
NHG = 4
BSHIFT = 40.0

DEBUG = set()
DEFER6 = True
PHASES = ("P1", "P2K", "P2V", "P3", "P4", "P5", "P6", "P7")


NDMASLOT = 8


class _Node:
    __slots__ = ("eng", "sem", "emit", "reads", "writes", "deps", "idx",
                 "marked", "val", "tag", "is_dma", "extra", "waits")

    def __init__(self, eng, sem, emit, reads, writes, idx, tag, is_dma):
        self.eng = eng
        self.sem = sem
        self.emit = emit
        self.reads = tuple(reads)
        self.writes = tuple(writes)
        self.idx = idx
        self.deps = ()
        self.marked = False
        self.val = 0
        self.tag = tag
        self.is_dma = is_dma
        self.extra = ()


class Prog:
    ENGS = ("g", "sy", "t", "v", "s")
    DMA_SEMS = tuple(f"dg{k}" for k in range(NDMASLOT)) + \
        tuple(f"dsy{k}" for k in range(NDMASLOT))
    BAR_SEMS = ("bg", "bsy", "bt", "bv", "bs")

    def __init__(self):
        self.nodes = []
        self._bn = 0
        self._dmacnt = {"dg": 0, "dsy": 0}
        self._dmahist = {"dg": [], "dsy": []}
        self._lasteng = {}

    def add(self, eng, emit, reads=(), writes=(), dma=False, tag="", sem=None):
        reads = tuple(reads)
        if dma:
            q = "dg" if eng == "g" else "dsy"
            sem = f"{q}{self._dmacnt[q] % NDMASLOT}"
            self._dmacnt[q] += 1
            reads = reads + (("qtail", q),)
        elif sem is None:
            sem = eng
        n = _Node(eng, sem, emit, reads, writes, len(self.nodes), tag, dma)
        if dma:
            hist = self._dmahist[q]
            if len(hist) >= NDMASLOT:
                n.extra = (hist[-NDMASLOT],)
            hist.append(n)
        if emit is not None:
            self._lasteng[eng] = n
        self.nodes.append(n)
        self._dirty = True
        return n

    def barrier(self):
        if not getattr(self, "_dirty", False):
            return
        self._dirty = False
        bn = self._bn
        self._bn += 1
        flags = []
        for e in self.ENGS:
            w = [("bflag", bn, e)]
            if e == "g":
                w.append(("qtail", "dg"))
            if e == "sy":
                w.append(("qtail", "dsy"))
            n = self.add(e, None, writes=w, tag=f"bar1_{bn}_{e}", sem=f"b{e}")
            if e in self._lasteng:
                n.extra = n.extra + (self._lasteng[e],)
            flags.append(("bflag", bn, e))
        for e in self.ENGS:
            self.add(e, None, reads=flags, tag=f"bar2_{bn}_{e}", sem=f"b{e}")

    def resolve(self):
        last_w = {}
        readers = {}
        for n in self.nodes:
            deps = set(n.extra)
            for r in n.reads:
                w = last_w.get(r)
                if w is not None:
                    deps.add(w)
            for r in n.writes:
                w = last_w.get(r)
                if w is not None:
                    deps.add(w)
                deps.update(readers.get(r, ()))
            deps.discard(n)
            n.deps = tuple(deps)
            for r in n.reads:
                readers.setdefault(r, set()).add(n)
            for r in n.writes:
                last_w[r] = n
                readers[r] = {n}
        for n in self.nodes:
            if n.emit is not None:
                n.marked = True
            for d in n.deps:
                d.marked = True
        cum = {}
        for n in self.nodes:
            if n.marked:
                inc = 16 if n.is_dma else 1
                cum[n.sem] = cum.get(n.sem, 0) + inc
                n.val = cum[n.sem]
        vc = [None] * len(self.nodes)
        know = {e: {} for e in self.ENGS}
        for n in self.nodes:
            k = know[n.eng]
            waits = []
            for d in sorted(n.deps, key=lambda d_: (d_.sem, d_.val)):
                if k.get(d.sem, 0) >= d.val:
                    continue
                waits.append((d.sem, d.val))
                dv = vc[d.idx]
                if dv:
                    for s_, v_ in dv.items():
                        if k.get(s_, 0) < v_:
                            k[s_] = v_
                if k.get(d.sem, 0) < d.val:
                    k[d.sem] = d.val
            n.waits = waits
            if n.marked:
                k2 = dict(k)
                k2[n.sem] = n.val
                vc[n.idx] = k2
        per_eng = {e: [] for e in self.ENGS}
        for n in self.nodes:
            per_eng[n.eng].append(n)
        ptr = {e: 0 for e in per_eng}
        done = [False] * len(self.nodes)
        progress = True
        ndone = 0
        while progress:
            progress = False
            for e, lst in per_eng.items():
                while ptr[e] < len(lst):
                    n = lst[ptr[e]]
                    if all(done[d.idx] for d in n.deps):
                        done[n.idx] = True
                        ptr[e] += 1
                        ndone += 1
                        progress = True
                    else:
                        break
        if ndone != len(self.nodes):
            stuck = [n for n in self.nodes if not done[n.idx]][:5]
            info = "; ".join(
                f"{n.eng}#{n.idx}:{n.tag} waits " +
                ",".join(f"{d.eng}#{d.idx}:{d.tag}" for d in n.deps
                         if not done[d.idx]) for n in stuck)
            raise RuntimeError(f"deadlock in schedule: {info}")
        return per_eng

    def emit(self, nc, sems):
        per_eng = self.resolve()

        def run_engine(eng_obj, ename):
            for n in per_eng[ename]:
                for s_, v_ in n.waits:
                    eng_obj.wait_ge(sems[s_], v_)
                ins = n.emit(eng_obj) if n.emit is not None else None
                if n.marked:
                    if ins is None:
                        ins = eng_obj.nop()
                    ins.then_inc(sems[n.sem], 16 if n.is_dma else 1)
        return run_engine


def _hadamard(n):
    h = np.array([[1.0]])
    while h.shape[0] < n:
        h = np.block([[h, h], [h, -h]])
    return h


def make_consts():
    c = {}
    c["h128b_in"] = _hadamard(128).astype(BF)
    c["hfoldup_in"] = np.kron(_hadamard(16), np.eye(8)).astype(BF)
    c["hfolddn_in"] = np.kron(_hadamard(64), np.eye(2)).astype(BF)
    pcs = np.zeros((128, 5, 128), np.float32)
    f = np.arange(128)
    for ci, cs in enumerate(CHAN):
        pcs[f, ci, (f - cs) % 128] = 1.0
    c["pcs_in"] = pcs.astype(BF)
    offs = set(OFFS)
    m = np.zeros((128, NCO, 128), np.float32)
    for ci, co in enumerate(COFFS):
        for p in range(128):
            for q in range(128):
                if co * 128 + p - q in offs:
                    m[p, ci, q] = 1.0
    c["masks_in"] = m.astype(BF)
    c["identf_in"] = np.eye(128, dtype=np.float32)
    ind = np.zeros((16, 128, 16), np.float32)
    for m_ in range(16):
        ind[m_, :, m_] = 1.0
    c["indc_in"] = ind.astype(BF)
    c["ones_cb_in"] = np.ones((128, 1), BF)
    c["ones_rf_in"] = np.ones((1, 128), np.float32)
    c["epsv_in"] = np.array([[RMS_EPS, LN_EPS]], np.float32)
    return c


COL = {}
_o = 0
for _n, _w in [("bq", 16), ("bk", 16), ("gq", 16), ("gk", 16), ("bo", 16),
               ("su_ug", 16), ("wsd", 16), ("su_dp", 64), ("wsu", 64), ("wsg", 64)]:
    COL[_n] = _o
    _o += _w
NCOLS = _o


def build(debug=(), phases=PHASES):
    debug = set(debug)
    phases = set(phases)
    nc = bass.Bass("TRN2", target_bir_lowering=False)

    def dram(name, shape, dtype, kind="ExternalInput"):
        return nc.dram_tensor(name, shape, dtype, kind=kind).ap()

    def scr(name, shape, dtype):
        kind = "ExternalOutput" if name in debug else "Internal"
        return nc.dram_tensor(name, shape, dtype, kind=kind).ap()

    xT = dram("xT", [DM, EXTP], F32)
    wq_p = dram("wq_p", [16, 16, 128, 128], F32)
    wk_p = dram("wk_p", [16, 16, 128, 128], F32)
    wv_p = dram("wv_p", [16, 128, DM], F32)
    wo_p = dram("wo_p", [16, 16, 128, 128], F32)
    fup_p = dram("fup_p", [16, 128, ITR], F32)
    fgt_p = dram("fgt_p", [16, 128, ITR], F32)
    fdn_p = dram("fdn_p", [64, 128, DM], F32)
    colpk = dram("colpk", [128, NCOLS], F32)
    bv_row = dram("bv_row", [1, DM], F32)
    wmix_in = dram("wmix_in", [1, 5], F32)
    consts = make_consts()
    CIN = {k: dram(k, list(v.shape), BF16 if v.dtype == BF else F32)
           for k, v in consts.items()}
    out_t = dram("out", [TC, DM], F32, kind="ExternalOutput")

    qt_s = scr("qt_s", [16, 128, TC], BF16)
    kt_s = scr("kt_s", [NEC, 128, NH, 128], BF16)
    v_s = scr("v_s", [NEC, 128, NH, 128], BF16)
    y1_s = scr("y1_s", [16, 128, TC], F32)
    w1_s = scr("w1_s", [64, 128, TC], BF16)
    gup_s = scr("gup_s", [16, 128, ITR], BF16)
    ggt_s = scr("ggt_s", [16, 128, ITR], BF16)
    gdn_s = scr("gdn_s", [64, 128, DM], BF16)
    bb_s = nc.dram_tensor("bb_s", [16], F32, kind="Internal").ap()
    w2k_s = scr("w2k_s", [16, 128, TC], BF16) if "w2k_s" in debug else None
    ao_s = scr("ao_s", [NH, 128, TC], BF16)
    dbg_ao = "ao_s" in debug

    P = Prog()
    xTv = xT.rearrange("(c p) n -> p c n", p=128)

    with contextlib.ExitStack() as ctx:
        SB = lambda n_, s_, d_: ctx.enter_context(nc.sbuf_tensor(n_, s_, d_))
        sem_h = {k: ctx.enter_context(nc.semaphore(k))
                 for k in ("g", "sy", "t", "v", "s") + Prog.DMA_SEMS + Prog.BAR_SEMS}

        cols = SB("cols", [128, NCOLS], F32)
        mixh = SB("mixh", [128, NH, 128], BF16)
        h128b = SB("h128b", [128, 128], BF16)
        Bcols = SB("Bcols", [128, 16], F32)
        maxq2 = SB("maxq2", [16, 1], F32)
        maxk2 = SB("maxk2", [16, 1], F32)
        rtmp16 = SB("rtmp16", [16, 2], F32)
        indcb = SB("indcb", [128, 16, 16], BF16)
        negb = SB("negb", [16, 1], F32)
        ones_cb = SB("ones_cb", [128, 1], BF16)
        ones_rf = SB("ones_rf", [1, 128], F32)
        epst = SB("epst", [1, 2], F32)
        srow = SB("srow", [1, 512], F32)
        srow2 = SB("srow2", [1, 512], F32)
        rtmp = SB("rtmp", [1, 8], F32)

        hfoldup = SB("hfoldup", [128, 128], BF16)
        hfolddn = SB("hfolddn", [128, 128], BF16)
        identf = SB("identf", [128, 128], F32)
        bvrep = SB("bvrep", [128, DM], BF16)
        p0ctx = contextlib.ExitStack()
        wmix = p0ctx.enter_context(nc.sbuf_tensor("wmix", [1, 8], F32))
        pcs_t = p0ctx.enter_context(nc.sbuf_tensor("pcs_t", [128, 5, 128], BF16))
        wrep = p0ctx.enter_context(nc.sbuf_tensor("wrep", [128, 8], F32))
        mbase = p0ctx.enter_context(nc.sbuf_tensor("mbase", [128, 128], BF16))
        bvrow_t = p0ctx.enter_context(nc.sbuf_tensor("bvrow_t", [1, DM], F32))
        ps_gen = p0ctx.enter_context(nc.psum_tensor("ps_gen", [128, 1024], F32))

        for dst, src, rs in [
                (cols[:, :], colpk, "cols"), (wmix[:, 0:5], wmix_in, "wmix"),
                (pcs_t[:, :, :], CIN["pcs_in"], "pcs"),
                (ones_cb[:, :], CIN["ones_cb_in"], "ones_cb"),
                (ones_rf[:, :], CIN["ones_rf_in"], "ones_rf"),
                (epst[:, :], CIN["epsv_in"], "epst"),
                (h128b[:, :], CIN["h128b_in"], "h128b"),
                (hfoldup[:, :], CIN["hfoldup_in"], "hfoldup"),
                (hfolddn[:, :], CIN["hfolddn_in"], "hfolddn"),
                (identf[:, :], CIN["identf_in"], "identf"),
                (indcb[:, :, :], CIN["indc_in"].transpose([1, 0, 2]), "indcb"),
                (bvrow_t[:, :], bv_row, "bvrow")]:
            P.add("g", (lambda d_, s_: lambda g: g.dma_start(out=d_, in_=s_))(dst, src),
                  writes=[rs], dma=True, tag=f"ld_{rs}")
        P.add("s", lambda s: s.mul(wmix[:, 0:5], wmix[:, 0:5], SCALE),
              reads=["wmix"], writes=["wmix"], tag="wmix_scale")
        P.add("t", lambda t: t.matmul(ps_gen[:, 0:5], ones_rf[:, :], wmix[:, 0:5],
                                      start=True, stop=True),
              reads=["wmix", "ones_rf"], writes=["psg0"], tag="wrep_mm")
        P.add("t", lambda t: t.matmul(ps_gen[:, 512:1024], ones_rf[:, :],
                                      bvrow_t[:, 0:512], start=True, stop=True),
              reads=["bvrow", "ones_rf"], writes=["psg1"], tag="bvrep_mm0")

        P.add("v", lambda v: v.tensor_copy(wrep[:, 0:5], ps_gen[:, 0:5]),
              reads=["psg0"], writes=["wrep"], tag="wrep_cp")
        P.add("v", lambda v: v.tensor_scalar(mbase[:, :], pcs_t[:, 0, :], wrep[:, 0:1],
                                             None, ALU.mult),
              reads=["wrep", "pcs"], writes=["mbase"], tag="mb0")
        for ci in range(1, 5):
            P.add("v", (lambda ci_: lambda v: v.scalar_tensor_tensor(
                mbase[:, :], pcs_t[:, ci_, :], wrep[:, ci_:ci_ + 1], mbase[:, :],
                ALU.mult, ALU.add))(ci),
                reads=["wrep", "pcs", "mbase"], writes=["mbase"], tag=f"mb{ci}")
        for h in range(NH):
            P.add("v", (lambda h_: lambda v: v.tensor_scalar(
                mixh[:, h_, :], mbase[:, :],
                cols[:, COL["gk"] + h_:COL["gk"] + h_ + 1], None, ALU.mult))(h),
                reads=["mbase", "cols"], writes=["mixh"], tag=f"mixh{h}")
        P.add("v", lambda v: v.memset(maxq2[:, :], 0.0), writes=["maxq2"], tag="z_mq")
        P.add("v", lambda v: v.memset(maxk2[:, :], 0.0), writes=["maxk2"], tag="z_mk")
        for q in range(4):
            if q > 0:
                P.add("t", (lambda q_: lambda t: t.matmul(
                    ps_gen[:, 512:1024], ones_rf[:, :], bvrow_t[:, q_ * 512:(q_ + 1) * 512],
                    start=True, stop=True))(q),
                    reads=["bvrow"], writes=["psg1"], tag=f"bvrep_mm{q}")
            P.add("v", (lambda q_: lambda v: v.tensor_copy(
                bvrep[:, q_ * 512:(q_ + 1) * 512], ps_gen[:, 512:1024]))(q),
                reads=["psg1"], writes=["bvrep"], tag=f"bvrep_cp{q}")
        P.barrier()
        p0ctx.close()

        FOLDS = [(0, fup_p, gup_s, hfoldup, 16, ITR),
                 (1, fgt_p, ggt_s, hfoldup, 16, ITR),
                 (2, fdn_p, gdn_s, hfolddn, 64, DM)]

        def make_fold_emitter(PSB_, PPS_, sfx, psf_pool=None):
            gtile = PSB_(f"gt{sfx}", [128, 4096], BF16)
            frhs = PSB_(f"fr{sfx}", [128, 4, 512], BF16)
            psf = psf_pool if psf_pool is not None else PPS_(f"pf{sfx}", [128, 1024])

            def emit_bl(fw, bl):
                wi, src_t, dst_t, lhs, nbl, nj = fw
                njb = nj // 512
                for jb in range(njb):
                    P.add("g", (lambda s_, bl_, jb_: lambda g: g.dma_start(
                        out=frhs[:, jb_ % 4, :],
                        in_=s_[bl_][:, jb_ * 512:(jb_ + 1) * 512]))(src_t, bl, jb),
                        writes=[("frhs", sfx, jb % 4)], dma=True, tag=f"f{wi}_{bl}_{jb}")
                    P.add("t", (lambda l_, jb_: lambda t: t.matmul(
                        psf[:, (jb_ % 2) * 512:(jb_ % 2 + 1) * 512],
                        l_, frhs[:, jb_ % 4, :],
                        start=True, stop=True))(lhs[:, :], jb),
                        reads=[("frhs", sfx, jb % 4), "hfoldup", "hfolddn"],
                        writes=[("aux", jb % 2)], tag=f"fm{wi}_{bl}_{jb}")
                    if jb % 2 == 0:
                        P.add("s", (lambda jb_: lambda s: s.activation(
                            gtile[:, (jb_ % 8) * 512:(jb_ % 8 + 1) * 512],
                            psf[:, (jb_ % 2) * 512:(jb_ % 2 + 1) * 512], AF.Copy))(jb),
                            reads=[("aux", jb % 2)], writes=[("gtile", sfx, jb % 8)],
                            tag=f"fe{wi}_{bl}_{jb}")
                    else:
                        P.add("v", (lambda jb_: lambda v: v.tensor_copy(
                            gtile[:, (jb_ % 8) * 512:(jb_ % 8 + 1) * 512],
                            psf[:, (jb_ % 2) * 512:(jb_ % 2 + 1) * 512]))(jb),
                            reads=[("aux", jb % 2)], writes=[("gtile", sfx, jb % 8)],
                            tag=f"fe{wi}_{bl}_{jb}")
                    if (jb + 1) % 4 == 0 or jb == njb - 1:
                        j0 = (jb // 4) * 4
                        nsl = jb - j0 + 1
                        def gsph(d_, bl_, nbl_, j0_, nsl_):
                            def f(sy):
                                dst = d_.rearrange("a (r bl) j -> (a r) bl j",
                                                   bl=nbl_)[:, bl_,
                                                            j0_ * 512:(j0_ + nsl_) * 512]
                                g0 = (j0_ % 8) * 512
                                return sy.dma_start(out=dst,
                                                    in_=gtile[:, g0:g0 + nsl_ * 512])
                            return f
                        P.add("sy", gsph(dst_t, bl, nbl, j0, nsl),
                              reads=[("gtile", sfx, (j0 % 8) + j_) for j_ in range(nsl)],
                              writes=[f"g_s{wi}"], dma=True, tag=f"gsp{wi}_{bl}_{j0}")

            return emit_bl


        if {"P1", "P2K", "P2V"} & phases:
            with contextlib.ExitStack() as px:
                PSB = lambda n_, s_, d_: px.enter_context(nc.sbuf_tensor(n_, s_, d_))
                PPS = lambda n_, s_: px.enter_context(nc.psum_tensor(n_, s_, F32))
                wpool = PSB("wpool", [128, 16, 16, 128], BF16)
                xpool = PSB("xpool", [128, 2, 16, 512], BF16)
                rawp = PSB("rawp", [128, 16, 512], BF16)
                pool2 = PSB("pool2", [128, 2, 16, 512], BF16)
                pool3 = PSB("pool3", [128, 2, 16, 512], BF16)
                sqb = PSB("sqb", [128, 2, 512], BF16)
                pbig = PPS("pbig", [128, 2048])
                pstat = PPS("pstat", [1, 512])
                pnq = PPS("pnq", [16, 512])
                paux = PPS("paux", [128, 1024])
                prep = paux[:, 0:512]
                fold_em = make_fold_emitter(PSB, PPS, "u", psf_pool=paux) if "P5" in phases else None
                fold_work = ([(FOLDS[0], bl) for bl in range(16)] +
                             [(FOLDS[1], bl) for bl in range(16)] +
                             [(FOLDS[2], bl) for bl in range(64)]) if fold_em else []

                def folds(n):
                    for _ in range(n):
                        if fold_work:
                            fold_em(*fold_work.pop(0))

                tdefer = []

                def tpop(k=1):
                    for _ in range(min(k, len(tdefer))):
                        tdefer.pop(0)()

                def xload(tt, tag):
                    c0 = (HALO_B if tag == "x" else 0) + tt * 512
                    P.add("g", (lambda tt_, c_: lambda g: g.dma_start(
                        out=xpool[:, tt_ % 2, :, :], in_=xTv[:, :, c_:c_ + 512]))(tt, c0),
                        writes=[("xp", tt % 2)], dma=True, tag=f"{tag}{tt}")

                if "P1" in phases:
                    xload(0, "x")
                    xload(1, "x")
                    for m in range(16):
                        P.add("g", (lambda m_: lambda g: g.dma_start(
                            out=wpool[:, m_, :, :], in_=wq_p[m_].transpose([1, 0, 2])))(m),
                            writes=[("wp", m)], dma=True, tag=f"wq{m}")
                    for tt in range(4):
                        if tt >= 2:
                            xload(tt, "x")
                        for m in range(16):
                            def qmm(tt_, m_):
                                def f(t):
                                    sl = pbig[:, (m_ % 3) * 512:(m_ % 3 + 1) * 512]
                                    for k in range(16):
                                        ins = t.matmul(sl, wpool[:, m_, k, :],
                                                       xpool[:, tt_ % 2, k, :],
                                                       start=(k == 0), stop=(k == 15))
                                    return ins
                                return f
                            P.add("t", qmm(tt, m),
                                  reads=[("wp", m), ("xp", tt % 2)],
                                  writes=[("pb", m % 3)], tag=f"q{tt}_{m}")
                            tpop(1)
                            def qev(tt_, m_):
                                def f(s):
                                    sl = pbig[:, (m_ % 3) * 512:(m_ % 3 + 1) * 512]
                                    bcol = cols[:, COL["bq"] + m_:COL["bq"] + m_ + 1]
                                    s.activation(rawp[:, m_, :], sl, AF.Identity, bias=bcol)
                                    return s.activation(sqb[:, m_ % 2, :], sl, AF.Square,
                                                        bias=bcol)
                                return f
                            P.add("s", qev(tt, m), reads=[("pb", m % 3), "cols"],
                                  writes=[("raw", m), ("sqb", m % 2)], tag=f"qev{tt}_{m}")
                            tdefer.append((lambda m_: lambda: P.add(
                                "t", lambda t: t.matmul(
                                    pstat[:, :], ones_cb[:, :], sqb[:, m_ % 2, :],
                                    start=(m_ == 0), stop=(m_ == 15)),
                                reads=[("sqb", m_ % 2), "ones_cb"], writes=["pstat"],
                                tag=f"qssq{tt}_{m_}"))(m))
                            if m % 4 == 3:
                                folds(1)
                        tpop(16)
                        P.add("s", lambda s: s.activation(srow[:, :], pstat[:, :], AF.Sqrt,
                                                          bias=epst[:, 0:1], scale=1.0 / DM),
                              reads=["pstat", "epst"], writes=["srow"], tag=f"sq_sqrt{tt}")
                        P.add("v", lambda v: v.reciprocal(srow[:, :], srow[:, :]),
                              reads=["srow"], writes=["srow"], tag=f"sq_rcp{tt}")
                        P.add("t", lambda t: t.matmul(prep[:, :], ones_rf[:, :], srow[:, :],
                                                      start=True, stop=True),
                              reads=["srow", "ones_rf"], writes=[("aux", 0)],
                              tag=f"sq_rep{tt}")
                        for m in range(16):
                            P.add("v", (lambda tt_, m_: lambda v: v.scalar_tensor_tensor(
                                pool2[:, tt_ % 2, m_, :], rawp[:, m_, :],
                                cols[:, COL["gq"] + m_:COL["gq"] + m_ + 1], prep[:, :],
                                ALU.mult, ALU.mult))(tt, m),
                                reads=[("raw", m), ("aux", 0), "cols"],
                                writes=[("p2", tt % 2, m)], tag=f"qt{tt}_{m}")
                            P.add("s", (lambda tt_, m_: lambda s: s.activation(
                                sqb[:, m_ % 2, :], pool2[:, tt_ % 2, m_, :],
                                AF.Square))(tt, m),
                                reads=[("p2", tt % 2, m)], writes=[("sqb", m % 2)],
                                tag=f"nq_sq{tt}_{m}")
                            if m >= 1:
                                P.add("t", (lambda m_: lambda t: t.matmul(
                                    pnq[:, :], indcb[:, m_, :], sqb[:, m_ % 2, :],
                                    start=(m_ == 0), stop=(m_ == 15)))(m - 1),
                                    reads=[("sqb", (m - 1) % 2), "indcb"], writes=["pnq"],
                                    tag=f"nq_mm{tt}_{m - 1}")
                        P.add("t", (lambda m_: lambda t: t.matmul(
                            pnq[:, :], indcb[:, m_, :], sqb[:, m_ % 2, :],
                            start=(m_ == 0), stop=(m_ == 15)))(15),
                            reads=[("sqb", 15 % 2), "indcb"], writes=["pnq"],
                            tag=f"nq_mm{tt}_15")
                        P.add("v", lambda v: v.reduce_max(rtmp16[:, 0:1], pnq[:, :], AX.X),
                              reads=["pnq"], writes=["rt16q"], tag=f"nq_rm{tt}")
                        P.add("v", lambda v: v.tensor_max(maxq2[:, :], maxq2[:, :],
                                                          rtmp16[:, 0:1]),
                              reads=["rt16q", "maxq2"], writes=["maxq2"], tag=f"nq_mx{tt}")
                        P.add("sy", (lambda tt_: lambda sy: sy.dma_start(
                            out=qt_s[:, :, tt_ * 512:(tt_ + 1) * 512].transpose([1, 0, 2]),
                            in_=pool2[:, tt_ % 2, :, :]))(tt),
                            reads=[("p2", tt % 2, m) for m in range(16)],
                            writes=["qt_s"], dma=True, tag=f"qspill{tt}")

                if "P2K" in phases:
                    xload(0, "kx")
                    xload(1, "kx")
                    for m in range(16):
                        P.add("g", (lambda m_: lambda g: g.dma_start(
                            out=wpool[:, m_, :, :], in_=wk_p[m_].transpose([1, 0, 2])))(m),
                            writes=[("wp", m)], dma=True, tag=f"wk{m}")
                    for tt in range(7):
                        if tt >= 2:
                            xload(tt, "kx")
                        for m in range(16):
                            def kmm(tt_, m_):
                                def f(t):
                                    sl = pbig[:, (m_ % 2) * 512:(m_ % 2 + 1) * 512]
                                    for k in range(16):
                                        ins = t.matmul(sl, wpool[:, m_, k, :],
                                                       xpool[:, tt_ % 2, k, :],
                                                       start=(k == 0), stop=(k == 15))
                                    return ins
                                return f
                            P.add("t", kmm(tt, m), reads=[("wp", m), ("xp", tt % 2)],
                                  writes=[("pb", m % 2)], tag=f"k{tt}_{m}")
                            tpop(1)
                            def kev(tt_, m_):
                                def f(s):
                                    sl = pbig[:, (m_ % 2) * 512:(m_ % 2 + 1) * 512]
                                    bcol = cols[:, COL["bk"] + m_:COL["bk"] + m_ + 1]
                                    s.activation(rawp[:, m_, :], sl, AF.Identity, bias=bcol)
                                    return s.activation(sqb[:, m_ % 2, :], sl, AF.Square,
                                                        bias=bcol)
                                return f
                            P.add("s", kev(tt, m), reads=[("pb", m % 2), "cols"],
                                  writes=[("raw", m), ("sqb", m % 2)], tag=f"kev{tt}_{m}")
                            tdefer.append((lambda m_: lambda: P.add(
                                "t", lambda t: t.matmul(
                                    pstat[:, :], ones_cb[:, :], sqb[:, m_ % 2, :],
                                    start=(m_ == 0), stop=(m_ == 15)),
                                reads=[("sqb", m_ % 2), "ones_cb"], writes=["pstat"],
                                tag=f"kssq{tt}_{m_}"))(m))
                            if m % 4 == 3:
                                folds(1)
                        tpop(16)
                        P.add("s", lambda s: s.activation(srow2[:, :], pstat[:, :], AF.Sqrt,
                                                          bias=epst[:, 0:1], scale=1.0 / DM),
                              reads=["pstat", "epst"], writes=["srow2"], tag=f"sk_sqrt{tt}")
                        P.add("v", lambda v: v.reciprocal(srow2[:, :], srow2[:, :]),
                              reads=["srow2"], writes=["srow2"], tag=f"sk_rcp{tt}")
                        P.add("t", lambda t: t.matmul(prep[:, :], ones_rf[:, :], srow2[:, :],
                                                      start=True, stop=True),
                              reads=["srow2", "ones_rf"], writes=[("aux", 0)],
                              tag=f"sk_rep{tt}")
                        def kb_n(tt_, m_):
                            P.add("v", lambda v: v.tensor_mul(
                                pool2[:, tt_ % 2, m_, :], rawp[:, m_, :], prep[:, :]),
                                reads=[("raw", m_), ("aux", 0)],
                                writes=[("p2", tt_ % 2, m_)], tag=f"kb{tt_}_{m_}")

                        def ktm_n(tt_, m_):
                            def f(t):
                                sl = pbig[:, 1024 + (m_ % 2) * 512:1024 + (m_ % 2 + 1) * 512]
                                return t.matmul(sl, mixh[:, m_, :], pool2[:, tt_ % 2, m_, :],
                                                start=True, stop=True)
                            P.add("t", f, reads=[("p2", tt_ % 2, m_), "mixh"],
                                  writes=[("pb", 2 + m_ % 2)], tag=f"ktm{tt_}_{m_}")

                        def ktev_n(tt_, m_):
                            def f(s):
                                sl = pbig[:, 1024 + (m_ % 2) * 512:1024 + (m_ % 2 + 1) * 512]
                                s.activation(pool3[:, tt_ % 2, m_, :], sl, AF.Copy)
                                return s.activation(sqb[:, m_ % 2, :], sl, AF.Square)
                            P.add("s", f, reads=[("pb", 2 + m_ % 2)],
                                  writes=[("p3", tt_ % 2, m_), ("sqb", m_ % 2)],
                                  tag=f"ktev{tt_}_{m_}")

                        def nk_n(tt_, m_):
                            P.add("t", lambda t: t.matmul(
                                pnq[:, :], indcb[:, m_, :], sqb[:, m_ % 2, :],
                                start=(m_ == 0), stop=(m_ == 15)),
                                reads=[("sqb", m_ % 2), "indcb"], writes=["pnq"],
                                tag=f"nk_mm{tt_}_{m_}")

                        for m in range(16):
                            kb_n(tt, m)
                            if m >= 1:
                                ktm_n(tt, m - 1)
                                ktev_n(tt, m - 1)
                            if m >= 2:
                                nk_n(tt, m - 2)
                        ktm_n(tt, 15)
                        ktev_n(tt, 15)
                        nk_n(tt, 14)
                        nk_n(tt, 15)
                        P.add("v", lambda v: v.reduce_max(rtmp16[:, 1:2], pnq[:, :], AX.X),
                              reads=["pnq"], writes=["rt16k"], tag=f"nk_rm{tt}")
                        P.add("v", lambda v: v.tensor_max(maxk2[:, :], maxk2[:, :],
                                                          rtmp16[:, 1:2]),
                              reads=["rt16k", "maxk2"], writes=["maxk2"], tag=f"nk_mx{tt}")
                        for ei in range(4):
                            P.add("sy", (lambda tt_, ei_: lambda sy: sy.dma_start(
                                out=kt_s[tt_ * 4 + ei_],
                                in_=pool3[:, tt_ % 2, :, ei_ * 128:(ei_ + 1) * 128]))(tt, ei),
                                reads=[("p3", tt % 2, m) for m in range(16)],
                                writes=["kt_s"], dma=True, tag=f"kspill{tt}_{ei}")

                if "P2V" in phases:
                    xload(0, "vx")
                    xload(1, "vx")
                    for k in range(16):
                        P.add("g", (lambda k_: lambda g: g.dma_start(
                            out=wpool[:, k_, :, :],
                            in_=wv_p[k_].rearrange("p (a b) -> p a b", a=16)))(k),
                            writes=[("wp", k)], dma=True, tag=f"wv{k}")
                    for tt in range(7):
                        if tt >= 2:
                            xload(tt, "vx")
                        for ns in range(4):
                            for jb in range(4):
                                def vmm(tt_, ns_, jb_):
                                    def f(t):
                                        sl = pbig[:, jb_ * 512:(jb_ + 1) * 512]
                                        for k in range(16):
                                            ins = t.matmul(
                                                sl,
                                                xpool[:, tt_ % 2, k,
                                                      ns_ * 128:(ns_ + 1) * 128],
                                                wpool[:, k, :, :].rearrange(
                                                    "p a b -> p (a b)")[:, jb_ * 512:
                                                                        (jb_ + 1) * 512],
                                                start=(k == 0), stop=(k == 15))
                                        return ins
                                    return f
                                P.add("t", vmm(tt, ns, jb),
                                      reads=[("xp", tt % 2)] +
                                            [("wp", k) for k in range(16)],
                                      writes=[("pb", jb)], tag=f"v{tt}_{ns}_{jb}")
                                def vev(tt_, ns_, jb_):
                                    def f(v):
                                        dst = pool3[:, tt_ % 2, :, :].rearrange(
                                            "p a b -> p (a b)")[:, ns_ * 2048 + jb_ * 512:
                                                                ns_ * 2048 + (jb_ + 1) * 512]
                                        return v.tensor_add(
                                            dst, pbig[:, jb_ * 512:(jb_ + 1) * 512],
                                            bvrep[:, jb_ * 512:(jb_ + 1) * 512])
                                    return f
                                P.add("v", vev(tt, ns, jb),
                                      reads=[("pb", jb), "bvrep"],
                                      writes=[("p3", tt % 2, 4 * ns + jb)],
                                      tag=f"vev{tt}_{ns}_{jb}")
                            folds(2)
                        for ei in range(4):
                            def vsp(tt_, ei_):
                                def f(sy):
                                    srcv = pool3[:, tt_ % 2, :, :].rearrange(
                                        "p a b -> p (a b)")[:, ei_ * 2048:(ei_ + 1) * 2048]
                                    return sy.dma_start(
                                        out=v_s[tt_ * 4 + ei_],
                                        in_=srcv.rearrange("p (h d) -> p h d", h=NH))
                                return f
                            P.add("sy", vsp(tt, ei),
                                  reads=[("p3", tt % 2, 4 * ei + j) for j in range(4)],
                                  writes=["v_s"], dma=True, tag=f"vspill{tt}_{ei}")
                while fold_work:
                    fold_em(*fold_work.pop(0))

        P.barrier()
        if "P3" in phases:
            with contextlib.ExitStack() as aoctx:
                attnout = aoctx.enter_context(
                    nc.sbuf_tensor("attnout", [128, NH, NLT, 128], BF16))
                wot = aoctx.enter_context(
                    nc.sbuf_tensor("wot", [128, 16, 16, 128], BF16))
                masks = aoctx.enter_context(
                    nc.sbuf_tensor("masks", [128, NCO, 128], BF16))
                P.add("g", lambda g: g.dma_start(out=masks[:, :, :], in_=CIN["masks_in"]),
                      writes=["masks"], dma=True, tag="ld_masks")
                p3ctx = contextlib.ExitStack()
                P3SB = lambda n_, s_, d_: p3ctx.enter_context(nc.sbuf_tensor(n_, s_, d_))
                P3PS = lambda n_, s_: p3ctx.enter_context(nc.psum_tensor(n_, s_, F32))
                ringk = P3SB("ringk", [128, RING, NHG, 128], BF16)
                ringv = P3SB("ringv", [128, RING, NHG, 128], BF16)
                qtb = P3SB("qtb", [128, 2, NHG, 128], BF16)
                expb = P3SB("expb", [128, 2, NCO, 128], BF16)
                pband = P3PS("pband", [128, 2048])
                patt = P3PS("patt", [128, 1024])
                pden = P3PS("pden", [1, 512])
                prep3 = P3PS("prep3", [128, 512])
                recrow = P3SB("recrow", [1, 512], F32)
                reprow = P3SB("reprow", [128, 512], F32)

                P.add("v", lambda v: v.tensor_mul(negb[:, :], maxq2[:, :], maxk2[:, :]),
                      reads=["maxq2", "maxk2"], writes=["negb"], tag="b_mul")
                P.add("s", lambda s: s.activation(negb[:, :], negb[:, :], AF.Sqrt),
                      reads=["negb"], writes=["negb"], tag="b_sqrt")
                P.add("v", lambda v: v.tensor_scalar(negb[:, :], negb[:, :], -1.0,
                                                     BSHIFT, ALU.mult, ALU.add),
                      reads=["negb"], writes=["negb"], tag="b_neg")
                P.add("sy", lambda sy: sy.dma_start(out=bb_s, in_=negb[:, 0]),
                      reads=["negb"], writes=["bb_s"], dma=True, tag="b_dma1")
                P.add("sy", lambda sy: sy.dma_start(out=srow[0:1, 0:16], in_=bb_s[None, :]),
                      reads=["bb_s"], writes=["srow"], dma=True, tag="b_dma2")
                P.add("t", lambda t: t.matmul(prep3[:, 0:16], ones_rf[:, :], srow[:, 0:16],
                                              start=True, stop=True),
                      reads=["srow", "ones_rf"], writes=["prep3"], tag="b_rep")
                P.add("v", lambda v: v.tensor_copy(Bcols[:, :], prep3[:, 0:16]),
                      reads=["prep3"], writes=["Bcols"], tag="b_cp")

                p3tail = []
                for hg in range(NH // NHG):
                    h0 = hg * NHG
                    loaded = set()
                    while p3tail:
                        p3tail.pop(0)()
                    for kcw in (hg * 2, hg * 2 + 1) if "P4" in phases else ():
                        P.add("g", (lambda kc_: lambda g: g.dma_start(
                            out=wot[:, kc_, :, :],
                            in_=wo_p[kc_].transpose([1, 0, 2])))(kcw),
                            writes=[("wot", kcw)], dma=True, tag=f"wol{kcw}")
                    for kcw in (8 + hg * 2, 8 + hg * 2 + 1) if "P4" in phases else ():
                        P.add("g", (lambda kc_: lambda g: g.dma_start(
                            out=wot[:, kc_, :, :],
                            in_=wo_p[kc_].transpose([1, 0, 2])))(kcw),
                            writes=[("wot", kcw)], dma=True, tag=f"wol{kcw}")

                    def ring_load(e, h0_=h0, hg_=hg):
                        P.add("sy", (lambda e_: lambda sy: sy.dma_start(
                            out=ringk[:, e_ % RING, :, :],
                            in_=kt_s[e_, :, h0_:h0_ + NHG, :]))(e),
                            reads=["kt_s"], writes=[("ringk", e % RING)],
                            dma=True, tag=f"rk{hg_}_{e}")
                        P.add("g", (lambda e_: lambda g: g.dma_start(
                            out=ringv[:, e_ % RING, :, :],
                            in_=v_s[e_, :, h0_:h0_ + NHG, :]))(e),
                            reads=["v_s"], writes=[("ringv", e % RING)],
                            dma=True, tag=f"rv{hg_}_{e}")
                        loaded.add(e)

                    for e in [0, 5, 6, 7, 8, 9, 10, 11]:
                        ring_load(e)
                    for lt in range(NLT):
                        for e in (lt, lt + 11):
                            if e not in loaded and 0 <= e < NEC:
                                ring_load(e)
                        P.add("sy", (lambda h0_, lt_: lambda sy: sy.dma_start(
                            out=qtb[:, lt_ % 2, :, :],
                            in_=qt_s[h0_:h0_ + NHG, :, lt_ * 128:(lt_ + 1) * 128]
                            .transpose([1, 0, 2])))(h0, lt),
                            reads=["qt_s"], writes=[("qtb", lt % 2)], dma=True,
                            tag=f"q3{hg}_{lt}")
                        def smm_n(hh, uf, lt_, ecs):
                            def f(t):
                                for ci, e in enumerate(ecs):
                                    sl = pband[:, uf * 1024 + ci * 128:
                                               uf * 1024 + (ci + 1) * 128]
                                    ins = t.matmul(sl, ringk[:, e % RING, hh, :],
                                                   qtb[:, lt_ % 2, hh, :],
                                                   start=True, stop=True)
                                return ins
                            P.add("t", f,
                                  reads=[("ringk", e % RING) for e in ecs] +
                                        [("qtb", lt_ % 2)],
                                  writes=[("pband", uf)], tag=f"sc{hg}_{lt_}_{hh}")

                        def ex_n(hh, uf, lt_):
                            P.add("s", lambda s: s.activation(
                                expb[:, uf, :, :], pband[:, uf * 1024:(uf + 1) * 1024],
                                AF.Exp, bias=Bcols[:, h0 + hh:h0 + hh + 1]),
                                reads=[("pband", uf), "Bcols"], writes=[("expb", uf)],
                                tag=f"ex{hg}_{lt_}_{hh}")

                        def mk_n(hh, uf, lt_):
                            P.add("v", lambda v: v.tensor_mul(
                                expb[:, uf, :, :], expb[:, uf, :, :], masks[:, :, :]),
                                reads=[("expb", uf), "masks"], writes=[("expb", uf)],
                                tag=f"mk{hg}_{lt_}_{hh}")

                        def av_n(hh, uf, lt_, ecs):
                            def f(t):
                                po = (lt_ % 2) * 512
                                for ci, e in enumerate(ecs):
                                    ins = t.matmul(
                                        patt[:, po + hh * 128:po + (hh + 1) * 128],
                                        ringv[:, e % RING, hh, :], expb[:, uf, ci, :],
                                        start=(ci == 0), stop=(ci == 7))
                                return ins
                            P.add("t", f,
                                  reads=[("ringv", e % RING) for e in ecs] +
                                        [("expb", uf)],
                                  writes=[("patt", lt_ % 2, hh)], tag=f"av{hg}_{lt_}_{hh}")

                        def dn_n(hh, uf, lt_):
                            def f(t):
                                for ci in range(NCO):
                                    ins = t.matmul(
                                        pden[:, hh * 128:(hh + 1) * 128],
                                        ones_cb[:, :], expb[:, uf, ci, :],
                                        start=(ci == 0), stop=(ci == 7))
                                return ins
                            P.add("t", f, reads=[("expb", uf), "ones_cb"],
                                  writes=[("pden", hh)], tag=f"dn{hg}_{lt_}_{hh}")

                        def mixtail(hg_, lt_):
                            def emit():
                                P.add("v", lambda v: v.reciprocal(recrow[:, :], pden[:, :]),
                                      reads=[("pden", 0), ("pden", 1), ("pden", 2),
                                             ("pden", 3)],
                                      writes=["recrow"], tag=f"rc{hg_}_{lt_}")
                                P.add("t", lambda t: t.matmul(
                                    prep3[:, :], ones_rf[:, :], recrow[:, :],
                                    start=True, stop=True),
                                    reads=["recrow", "ones_rf"], writes=["prep3"],
                                    tag=f"rcrep{hg_}_{lt_}")
                                P.add("v", lambda v: v.tensor_copy(reprow[:, :],
                                                                   prep3[:, :]),
                                      reads=["prep3"], writes=["reprow"],
                                      tag=f"rpc{hg_}_{lt_}")
                                def f(v):
                                    po = (lt_ % 2) * 512
                                    return v.tensor_mul(
                                        attnout[:, hg_ * NHG:(hg_ + 1) * NHG, lt_, :],
                                        patt[:, po:po + 512].rearrange(
                                            "p (h l) -> p h l", h=NHG),
                                        reprow[:, :].rearrange("p (h l) -> p h l", h=NHG))
                                P.add("v", f,
                                      reads=[("patt", lt_ % 2, hh_) for hh_ in range(NHG)] +
                                            ["reprow"],
                                      writes=[("attnout", hg_, lt_)], tag=f"ao{hg_}_{lt_}")
                            return emit

                        ecs = [lt + 8 + co for co in COFFS]
                        for hh in range(NHG):
                            uf = hh % 2
                            smm_n(hh, uf, lt, ecs)
                            if hh == 1 and p3tail:
                                p3tail.pop(0)()
                            ex_n(hh, uf, lt)
                            mk_n(hh, uf, lt)
                            if hh >= 1:
                                av_n(hh - 1, (hh - 1) % 2, lt, ecs)
                                dn_n(hh - 1, (hh - 1) % 2, lt)
                        av_n(3, 1, lt, ecs)
                        dn_n(3, 1, lt)
                        p3tail.append(mixtail(hg, lt))
                while p3tail:
                    p3tail.pop(0)()
                if dbg_ao:
                    P.add("sy", lambda sy: sy.dma_start(
                        out=ao_s[:, :, :].transpose([1, 0, 2]),
                        in_=attnout[:, :, :, :].rearrange("p h lt l -> p h (lt l)")),
                        reads=[("attnout", hg_, lt_) for hg_ in range(4) for lt_ in range(NLT)],
                        writes=["ao_s"], dma=True, tag="ao_dump")

                P.barrier()
                p3ctx.close()
                if "P4" in phases:
                    with contextlib.ExitStack() as px:
                        PSB = lambda n_, s_, d_: px.enter_context(nc.sbuf_tensor(n_, s_, d_))
                        PPS = lambda n_, s_: px.enter_context(nc.psum_tensor(n_, s_, F32))
                        v1b = PSB("v1b", [128, 16, 512], F32)
                        xres = PSB("xres", [128, 4, 512], F32)
                        v1c = PSB("v1c", [128, 4, 512], BF16)
                        pso = PPS("pso", [128, 1024])
                        psl1 = PPS("psl1", [1, 512])
                        psl2 = PPS("psl2", [1, 512])
                        psr1 = PPS("psr1", [128, 512])
                        psr2 = PPS("psr2", [128, 512])
                        mrow = PSB("mrow", [1, 512], F32)
                        irow = PSB("irow", [1, 512], F32)

                        pend4 = []
                        for sgi in range(4):
                            for kc in range(16):
                                if len(pend4) >= 2:
                                    pend4.pop(0)()
                                P.add("sy", (lambda sgi_, kc_: lambda sy: sy.dma_start(
                                    out=xres[:, kc_ % 4, :],
                                    in_=xT[kc_ * 128:(kc_ + 1) * 128,
                                           HALO_B + sgi_ * 512:HALO_B + (sgi_ + 1) * 512]))(sgi, kc),
                                    writes=[("xres", kc % 4)], dma=True,
                                    tag=f"xr{sgi}_{kc}")
                                def omm(sgi_, kc_):
                                    def f(t):
                                        sl = pso[:, (kc_ % 2) * 512:(kc_ % 2 + 1) * 512]
                                        lt0 = sgi_ * 4
                                        for hc in range(16):
                                            rhs = attnout[:, hc, lt0:lt0 + 4, :]
                                            ins = t.matmul(sl, wot[:, kc_, hc, :],
                                                           rhs, start=(hc == 0),
                                                           stop=(hc == 15))
                                        return ins
                                    return f
                                P.add("t", omm(sgi, kc),
                                      reads=[("wot", kc)] +
                                            [("attnout", hg_, sgi * 4 + i_)
                                             for hg_ in range(4) for i_ in range(4)],
                                      writes=[("pso", kc % 2)], tag=f"o{sgi}_{kc}")
                                def ov(sgi_, kc_):
                                    def f(v):
                                        sl = pso[:, (kc_ % 2) * 512:(kc_ % 2 + 1) * 512]
                                        return v.scalar_tensor_tensor(
                                            v1b[:, kc_, :], sl,
                                            cols[:, COL["bo"] + kc_:COL["bo"] + kc_ + 1],
                                            xres[:, kc_ % 4, :], ALU.add, ALU.add)
                                    return f
                                P.add("v", ov(sgi, kc),
                                      reads=[("pso", kc % 2), ("xres", kc % 4), "cols"],
                                      writes=[("v1b", kc)], tag=f"ov{sgi}_{kc}")
                                P.add("s", (lambda kc_: lambda s: s.activation(
                                    v1c[:, 2 * (kc_ % 2), :], v1b[:, kc_, :], AF.Copy))(kc),
                                    reads=[("v1b", kc)], writes=[("v1c", 2 * (kc % 2))],
                                    tag=f"oc{sgi}_{kc}")
                                P.add("s", (lambda kc_: lambda s: s.activation(
                                    v1c[:, 2 * (kc_ % 2) + 1, :], v1b[:, kc_, :],
                                    AF.Square))(kc),
                                    reads=[("v1b", kc)], writes=[("v1c", 2 * (kc % 2) + 1)],
                                    tag=f"oq{sgi}_{kc}")
                                def osd(sgi_, kc_):
                                    def emit():
                                        P.add("t", lambda t: t.matmul(
                                            psl1[:, :], ones_cb[:, :],
                                            v1c[:, 2 * (kc_ % 2), :],
                                            start=(kc_ == 0), stop=(kc_ == 15)),
                                            reads=[("v1c", 2 * (kc_ % 2)), "ones_cb"],
                                            writes=["psl1"], tag=f"os1_{sgi_}_{kc_}")
                                        P.add("t", lambda t: t.matmul(
                                            psl2[:, :], ones_cb[:, :],
                                            v1c[:, 2 * (kc_ % 2) + 1, :],
                                            start=(kc_ == 0), stop=(kc_ == 15)),
                                            reads=[("v1c", 2 * (kc_ % 2) + 1), "ones_cb"],
                                            writes=["psl2"], tag=f"os2_{sgi_}_{kc_}")
                                    return emit
                                pend4.append(osd(sgi, kc))
                            while pend4:
                                pend4.pop(0)()
                            def lnrow(s):
                                s.activation(mrow[:, :], psl1[:, :], AF.Copy, scale=1.0 / DM)
                                return s.activation(irow[:, :], psl2[:, :], AF.Copy,
                                                    scale=1.0 / DM)
                            P.add("s", lnrow, reads=["psl1", "psl2"],
                                  writes=["mrow", "irow"], tag=f"lnr{sgi}")
                            P.add("v", lambda v: v.scalar_tensor_tensor(
                                srow[:, 0:512], mrow[:, :], 1.0, mrow[:, :],
                                ALU.mult, ALU.mult),
                                reads=["mrow"], writes=["srow"], tag=f"lnva{sgi}")
                            P.add("v", lambda v: v.tensor_sub(irow[:, :], irow[:, :],
                                                              srow[:, 0:512]),
                                  reads=["srow", "irow"], writes=["irow"], tag=f"lnvb{sgi}")
                            P.add("s", lambda s: s.activation(irow[:, :], irow[:, :],
                                                              AF.Sqrt, bias=epst[:, 1:2]),
                                  reads=["irow", "epst"], writes=["irow"], tag=f"lns{sgi}")
                            P.add("v", lambda v: v.reciprocal(irow[:, :], irow[:, :]),
                                  reads=["irow"], writes=["irow"], tag=f"lnca{sgi}")
                            P.add("v", lambda v: v.tensor_mul(mrow[:, :], mrow[:, :],
                                                              irow[:, :]),
                                  reads=["irow", "mrow"], writes=["mrow"], tag=f"lncb{sgi}")
                            P.add("t", lambda t: t.matmul(psr1[:, :], ones_rf[:, :],
                                                          irow[:, :], start=True, stop=True),
                                  reads=["irow", "ones_rf"], writes=["psr1"],
                                  tag=f"lnR1{sgi}")
                            P.add("t", lambda t: t.matmul(psr2[:, :], ones_rf[:, :],
                                                          mrow[:, :], start=True, stop=True),
                                  reads=["mrow", "ones_rf"], writes=["psr2"],
                                  tag=f"lnR2{sgi}")
                            for kc in range(16):
                                P.add("v", (lambda kc_: lambda v: v.tensor_mul(
                                    v1b[:, kc_, :], v1b[:, kc_, :], psr1[:, :]))(kc),
                                    reads=[("v1b", kc), "psr1"], writes=[("v1b", kc)],
                                    tag=f"y1a{sgi}_{kc}")
                                P.add("v", (lambda kc_: lambda v: v.tensor_sub(
                                    v1b[:, kc_, :], v1b[:, kc_, :], psr2[:, :]))(kc),
                                    reads=[("v1b", kc), "psr2"], writes=[("v1b", kc)],
                                    tag=f"y1b{sgi}_{kc}")
                            P.add("sy", (lambda sgi_: lambda sy: sy.dma_start(
                                out=y1_s[:, :, sgi_ * 512:(sgi_ + 1) * 512]
                                .transpose([1, 0, 2]),
                                in_=v1b[:, :, :]))(sgi),
                                reads=[("v1b", kc_) for kc_ in range(16)],
                                writes=["y1_s"], dma=True, tag=f"y1sp{sgi}")

        P.barrier()
        P.barrier()
        if "P6" in phases:
            with contextlib.ExitStack() as px:
                PSB = lambda n_, s_, d_: px.enter_context(nc.sbuf_tensor(n_, s_, d_))
                PPS = lambda n_, s_: px.enter_context(nc.psum_tensor(n_, s_, F32))
                w2048 = PSB("w2048", [128, 16, TC], BF16)
                y1bt = PSB("y1bt", [128, 2, TC], BF16)
                suu = PSB("suu", [128, 16], F32)
                sud = PSB("sud", [128, 64], F32)
                hsa = PSB("hsa", [128, 2, 128], BF16)
                gupsl = PSB("gupsl", [128, 2, 16, 128], BF16)
                ggtsl = PSB("ggtsl", [128, 2, 16, 128], BF16)
                silub = PSB("silub", [128, 2, 512], BF16)
                upb = PSB("upb", [128, 2, 512], BF16)
                x1b = PSB("x1b", [128, 2, 512], BF16)
                w1b = PSB("w1b", [128, 2, TC], BF16)
                psu = PPS("psu", [128, 1024])
                psg6 = PPS("psg6", [128, 1024])
                psw = PPS("psw", [128, 1024])

                P.add("v", lambda v: v.tensor_scalar(suu[:, :], cols[:, COL["su_ug"]:
                                                     COL["su_ug"] + 16],
                                                     float(DM) ** -0.5, None, ALU.mult),
                      reads=["cols"], writes=["suu"], tag="suu")
                P.add("v", lambda v: v.tensor_scalar(sud[:, :], cols[:, COL["su_dp"]:
                                                     COL["su_dp"] + 64],
                                                     float(ITR) ** -0.5, None, ALU.mult),
                      reads=["cols"], writes=["sud"], tag="sud")
                for a in range(16):
                    P.add("g", (lambda a_: lambda g: g.dma_start(
                        out=y1bt[:, a_ % 2, :], in_=y1_s[a_]))(a),
                        reads=["y1_s"], writes=[("y1bt", a % 2)], dma=True, tag=f"y1l{a}")
                    P.add("v", (lambda a_: lambda v: v.tensor_scalar(
                        hsa[:, a_ % 2, :], h128b[:, :], suu[:, a_:a_ + 1], None,
                        ALU.mult))(a),
                        reads=["h128b", "suu"], writes=[("hsa", a % 2)], tag=f"hsa{a}")
                    for lb in range(4):
                        P.add("t", (lambda a_, lb_: lambda t: t.matmul(
                            psu[:, (lb_ % 2) * 512:(lb_ % 2 + 1) * 512], hsa[:, a_ % 2, :],
                            y1bt[:, a_ % 2, lb_ * 512:(lb_ + 1) * 512],
                            start=True, stop=True))(a, lb),
                            reads=[("hsa", a % 2), ("y1bt", a % 2)],
                            writes=[("psu", lb % 2)], tag=f"sa{a}_{lb}")
                        P.add("s", (lambda a_, lb_: lambda s: s.activation(
                            w2048[:, a_, lb_ * 512:(lb_ + 1) * 512],
                            psu[:, (lb_ % 2) * 512:(lb_ % 2 + 1) * 512], AF.Copy))(a, lb),
                            reads=[("psu", lb % 2)], writes=[("w2048", a)],
                            tag=f"sae{a}_{lb}")
                if w2k_s is not None:
                    P.add("sy", lambda sy: sy.dma_start(
                        out=w2k_s.transpose([1, 0, 2]), in_=w2048[:, :, :]),
                        reads=[("w2048", a_) for a_ in range(16)],
                        writes=["w2k_s"], dma=True, tag="w2kdump")
                pend6 = []
                for i in range(64):
                    while len(pend6) >= 2:
                        pend6.pop(0)()
                    P.add("g", (lambda i_: lambda g: g.dma_start(
                        out=gupsl[:, i_ % 2, :, :],
                        in_=gup_s[:, :, i_ * 128:(i_ + 1) * 128].transpose([1, 0, 2])))(i),
                        reads=["g_s0"], writes=[("gupsl", i % 2)], dma=True, tag=f"gu{i}")
                    P.add("g", (lambda i_: lambda g: g.dma_start(
                        out=ggtsl[:, i_ % 2, :, :],
                        in_=ggt_s[:, :, i_ * 128:(i_ + 1) * 128].transpose([1, 0, 2])))(i),
                        reads=["g_s1"], writes=[("ggtsl", i % 2)], dma=True, tag=f"gg{i}")
                    P.add("v", (lambda i_: lambda v: v.tensor_scalar(
                        hsa[:, i_ % 2, :], h128b[:, :], sud[:, i_:i_ + 1], None,
                        ALU.mult))(i),
                        reads=["h128b", "sud"], writes=[("hsa", i % 2)], tag=f"hsd{i}")
                    for lb in range(4):
                        def upmm(i_, lb_):
                            def f(t):
                                sl = psu[:, (lb_ % 2) * 512:(lb_ % 2 + 1) * 512]
                                for a in range(16):
                                    ins = t.matmul(sl, gupsl[:, i_ % 2, a, :],
                                                   w2048[:, a, lb_ * 512:(lb_ + 1) * 512],
                                                   start=(a == 0), stop=(a == 15))
                                return ins
                            return f
                        P.add("t", upmm(i, lb),
                              reads=[("gupsl", i % 2)] + [("w2048", a_) for a_ in range(16)],
                              writes=[("psu", lb % 2)], tag=f"up{i}_{lb}")
                        def gtmm(i_, lb_):
                            def f(t):
                                sl = psg6[:, (lb_ % 2) * 512:(lb_ % 2 + 1) * 512]
                                for a in range(16):
                                    ins = t.matmul(sl, ggtsl[:, i_ % 2, a, :],
                                                   w2048[:, a, lb_ * 512:(lb_ + 1) * 512],
                                                   start=(a == 0), stop=(a == 15))
                                return ins
                            return f
                        P.add("t", gtmm(i, lb),
                              reads=[("ggtsl", i % 2)] + [("w2048", a_) for a_ in range(16)],
                              writes=[("psg6", lb % 2)], tag=f"gt{i}_{lb}")
                        if len(pend6) >= 2:
                            pend6.pop(0)()
                        P.add("s", (lambda i_, lb_: lambda s: s.activation(
                            upb[:, lb_ % 2, :], psu[:, (lb_ % 2) * 512:(lb_ % 2 + 1) * 512],
                            AF.Identity, scale=cols[:, COL["wsu"] + i_:COL["wsu"] + i_ + 1]))(i, lb),
                            reads=[("psu", lb % 2), "cols"], writes=[("upb", lb % 2)],
                            tag=f"ue{i}_{lb}")
                        P.add("s", (lambda i_, lb_: lambda s: s.activation(
                            silub[:, lb_ % 2, :], psg6[:, (lb_ % 2) * 512:(lb_ % 2 + 1) * 512],
                            AF.Silu, scale=cols[:, COL["wsg"] + i_:COL["wsg"] + i_ + 1]))(i, lb),
                            reads=[("psg6", lb % 2), "cols"], writes=[("silub", lb % 2)],
                            tag=f"se{i}_{lb}")
                        P.add("v", (lambda lb_: lambda v: v.tensor_mul(
                            x1b[:, lb_ % 2, :], silub[:, lb_ % 2, :], upb[:, lb_ % 2, :]))(lb),
                            reads=[("silub", lb % 2), ("upb", lb % 2)],
                            writes=[("x1b", lb % 2)], tag=f"x1{i}_{lb}")
                        def swd(i_, lb_):
                            def emit():
                                P.add("t", lambda t: t.matmul(
                                    psw[:, (lb_ % 2) * 512:(lb_ % 2 + 1) * 512],
                                    hsa[:, i_ % 2, :], x1b[:, lb_ % 2, :],
                                    start=True, stop=True),
                                    reads=[("hsa", i_ % 2), ("x1b", lb_ % 2)],
                                    writes=[("psw", lb_ % 2)], tag=f"sw{i_}_{lb_}")
                                P.add("s", lambda s: s.activation(
                                    w1b[:, i_ % 2, lb_ * 512:(lb_ + 1) * 512],
                                    psw[:, (lb_ % 2) * 512:(lb_ % 2 + 1) * 512], AF.Copy),
                                    reads=[("psw", lb_ % 2)], writes=[("w1b", i_ % 2, lb_)],
                                    tag=f"we{i_}_{lb_}")
                            return emit
                        if DEFER6:
                            pend6.append(swd(i, lb))
                        else:
                            swd(i, lb)()
                    def w1spd(i_):
                        def emit():
                            P.add("sy", lambda sy: sy.dma_start(
                                out=w1_s[i_], in_=w1b[:, i_ % 2, :]),
                                reads=[("w1b", i_ % 2, lb_) for lb_ in range(4)],
                                writes=["w1_s"], dma=True, tag=f"w1sp{i_}")
                        return emit
                    if DEFER6:
                        pend6.append(w1spd(i))
                    else:
                        w1spd(i)()

                while pend6:
                    pend6.pop(0)()
        P.barrier()
        if "P7" in phases:
            with contextlib.ExitStack() as px:
                PSB = lambda n_, s_, d_: px.enter_context(nc.sbuf_tensor(n_, s_, d_))
                PPS = lambda n_, s_: px.enter_context(nc.psum_tensor(n_, s_, F32))
                w1lt = PSB("w1lt", [128, 64, 512], BF16)
                gdnsl = PSB("gdnsl", [128, 2, 64, 128], BF16)
                y1sl = PSB("y1sl", [128, 2, 512], F32)
                v2b = PSB("v2b", [128, 16, 512], F32)
                v2c = PSB("v2c", [128, 4, 512], BF16)
                outb = PSB("outb", [128, 4, 2048], F32)
                mrow2 = PSB("mrow2", [1, 512], F32)
                irow2 = PSB("irow2", [1, 512], F32)
                psd = PPS("psd", [128, 1024])
                psl12 = PPS("psl12", [1, 512])
                psl22 = PPS("psl22", [1, 512])
                psr12 = PPS("psr12", [128, 512])
                psr22 = PPS("psr22", [128, 512])
                pstr = PPS("pstr", [128, 1024])

                pend7 = []
                for lg in range(4):
                    P.add("sy", (lambda lg_: lambda sy: sy.dma_start(
                        out=w1lt[:, :, :],
                        in_=w1_s[:, :, lg_ * 512:(lg_ + 1) * 512].transpose([1, 0, 2])))(lg),
                        reads=["w1_s"], writes=["w1lt"], dma=True, tag=f"w1l{lg}")
                    for kc in range(16):
                        if len(pend7) >= 2:
                            pend7.pop(0)()
                        P.add("g", (lambda lg_, kc_: lambda g: g.dma_start(
                            out=gdnsl[:, kc_ % 2, :, :],
                            in_=gdn_s[:, :, kc_ * 128:(kc_ + 1) * 128]
                            .transpose([1, 0, 2])))(lg, kc),
                            reads=["g_s2"], writes=[("gdnsl", kc % 2)], dma=True,
                            tag=f"gd{lg}_{kc}")
                        P.add("sy", (lambda lg_, kc_: lambda sy: sy.dma_start(
                            out=y1sl[:, kc_ % 2, :],
                            in_=y1_s[kc_][:, lg_ * 512:(lg_ + 1) * 512]))(lg, kc),
                            reads=["y1_s"], writes=[("y1sl", kc % 2)], dma=True,
                            tag=f"y1r{lg}_{kc}")
                        def dmm(lg_, kc_):
                            def f(t):
                                sl = psd[:, (kc_ % 2) * 512:(kc_ % 2 + 1) * 512]
                                for i in range(64):
                                    ins = t.matmul(sl, gdnsl[:, kc_ % 2, i, :],
                                                   w1lt[:, i, :],
                                                   start=(i == 0), stop=(i == 63))
                                return ins
                            return f
                        P.add("t", dmm(lg, kc), reads=[("gdnsl", kc % 2), "w1lt"],
                              writes=[("psd", kc % 2)], tag=f"d{lg}_{kc}")
                        def dv(lg_, kc_):
                            def f(v):
                                sl = psd[:, (kc_ % 2) * 512:(kc_ % 2 + 1) * 512]
                                return v.scalar_tensor_tensor(
                                    v2b[:, kc_, :], sl,
                                    cols[:, COL["wsd"] + kc_:COL["wsd"] + kc_ + 1],
                                    y1sl[:, kc_ % 2, :], ALU.mult, ALU.add)
                            return f
                        P.add("v", dv(lg, kc), reads=[("psd", kc % 2), ("y1sl", kc % 2),
                                                      "cols"],
                              writes=[("v2b", kc)], tag=f"dv{lg}_{kc}")
                        P.add("s", (lambda kc_: lambda s: s.activation(
                            v2c[:, 2 * (kc_ % 2), :], v2b[:, kc_, :], AF.Copy))(kc),
                            reads=[("v2b", kc)], writes=[("v2c", 2 * (kc % 2))],
                            tag=f"dc{lg}_{kc}")
                        P.add("s", (lambda kc_: lambda s: s.activation(
                            v2c[:, 2 * (kc_ % 2) + 1, :], v2b[:, kc_, :], AF.Square))(kc),
                            reads=[("v2b", kc)], writes=[("v2c", 2 * (kc % 2) + 1)],
                            tag=f"dq{lg}_{kc}")
                        def dsd(lg_, kc_):
                            def emit():
                                P.add("t", lambda t: t.matmul(
                                    psl12[:, :], ones_cb[:, :], v2c[:, 2 * (kc_ % 2), :],
                                    start=(kc_ == 0), stop=(kc_ == 15)),
                                    reads=[("v2c", 2 * (kc_ % 2)), "ones_cb"],
                                    writes=["psl12"], tag=f"ds1_{lg_}_{kc_}")
                                P.add("t", lambda t: t.matmul(
                                    psl22[:, :], ones_cb[:, :],
                                    v2c[:, 2 * (kc_ % 2) + 1, :],
                                    start=(kc_ == 0), stop=(kc_ == 15)),
                                    reads=[("v2c", 2 * (kc_ % 2) + 1), "ones_cb"],
                                    writes=["psl22"], tag=f"ds2_{lg_}_{kc_}")
                            return emit
                        pend7.append(dsd(lg, kc))
                    while pend7:
                        pend7.pop(0)()
                    def ln2row(s):
                        s.activation(mrow2[:, :], psl12[:, :], AF.Copy, scale=1.0 / DM)
                        return s.activation(irow2[:, :], psl22[:, :], AF.Copy, scale=1.0 / DM)
                    P.add("s", ln2row, reads=["psl12", "psl22"],
                          writes=["mrow2", "irow2"], tag=f"l2r{lg}")
                    P.add("v", lambda v: v.scalar_tensor_tensor(
                        srow[:, 0:512], mrow2[:, :], 1.0, mrow2[:, :], ALU.mult, ALU.mult),
                        reads=["mrow2"], writes=["srow"], tag=f"l2va{lg}")
                    P.add("v", lambda v: v.tensor_sub(irow2[:, :], irow2[:, :],
                                                      srow[:, 0:512]),
                          reads=["srow", "irow2"], writes=["irow2"], tag=f"l2vb{lg}")
                    P.add("s", lambda s: s.activation(irow2[:, :], irow2[:, :], AF.Sqrt,
                                                      bias=epst[:, 1:2]),
                          reads=["irow2", "epst"], writes=["irow2"], tag=f"l2s{lg}")
                    P.add("v", lambda v: v.reciprocal(irow2[:, :], irow2[:, :]),
                          reads=["irow2"], writes=["irow2"], tag=f"l2ca{lg}")
                    P.add("v", lambda v: v.tensor_mul(mrow2[:, :], mrow2[:, :],
                                                      irow2[:, :]),
                          reads=["irow2", "mrow2"], writes=["mrow2"], tag=f"l2cb{lg}")
                    P.add("t", lambda t: t.matmul(psr12[:, :], ones_rf[:, :], irow2[:, :],
                                                  start=True, stop=True),
                          reads=["irow2", "ones_rf"], writes=["psr12"], tag=f"l2R1{lg}")
                    P.add("t", lambda t: t.matmul(psr22[:, :], ones_rf[:, :], mrow2[:, :],
                                                  start=True, stop=True),
                          reads=["mrow2", "ones_rf"], writes=["psr22"], tag=f"l2R2{lg}")
                    trpend = []
                    for kc in range(16):
                        P.add("v", (lambda kc_: lambda v: v.tensor_mul(
                            v2b[:, kc_, :], v2b[:, kc_, :], psr12[:, :]))(kc),
                            reads=[("v2b", kc), "psr12"], writes=[("v2b", kc)],
                            tag=f"y2a{lg}_{kc}")
                        P.add("v", (lambda kc_: lambda v: v.tensor_sub(
                            v2b[:, kc_, :], v2b[:, kc_, :], psr22[:, :]))(kc),
                            reads=[("v2b", kc), "psr22"], writes=[("v2b", kc)],
                            tag=f"y2b{lg}_{kc}")
                        def trd(lg_, kc_):
                            def emit():
                                for ls_ in range(4):
                                    sl = (kc_ * 4 + ls_) % 2
                                    P.add("t", (lambda ls2, sl2: lambda t: t.transpose(
                                        pstr[:, sl2 * 512:sl2 * 512 + 128],
                                        v2b[:, kc_, ls2 * 128:(ls2 + 1) * 128],
                                        identf[:, :]))(ls_, sl),
                                        reads=[("v2b", kc_), "identf"],
                                        writes=[("pstr", sl)], tag=f"tr{lg_}_{ls_}_{kc_}")
                                    P.add("v", (lambda ls2, sl2: lambda v: v.tensor_copy(
                                        outb[:, ls2, kc_ * 128:(kc_ + 1) * 128],
                                        pstr[:, sl2 * 512:sl2 * 512 + 128]))(ls_, sl),
                                        reads=[("pstr", sl)], writes=[("outb", ls_, kc_)],
                                        tag=f"te{lg_}_{ls_}_{kc_}")
                            return emit
                        trpend.append(trd(lg, kc))
                        if len(trpend) >= 2:
                            trpend.pop(0)()
                    while trpend:
                        trpend.pop(0)()
                    for ls in range(4):
                        P.add("sy", (lambda lg_, ls_: lambda sy: sy.dma_start(
                            out=out_t[lg_ * 512 + ls_ * 128:lg_ * 512 + (ls_ + 1) * 128, :],
                            in_=outb[:, ls_, :]))(lg, ls),
                            reads=[("outb", ls, kc_) for kc_ in range(16)],
                            writes=["out"], dma=True, tag=f"ost{lg}_{ls}")

        P.add("sy", None, reads=["out"] + (["qt_s", "kt_s", "v_s", "ao_s", "y1_s",
                                           "w1_s", "g_s0", "g_s1", "g_s2"]
                                          if debug else []), tag="final")

        global LASTPROG
        LASTPROG = P
        runner = P.emit(nc, {k: sem_h[k] for k in sem_h})
        with nc.Block("main") as b:
            b.gpsimd(lambda g: runner(g, "g"))
            b.sync(lambda sy: runner(sy, "sy"))
            b.tensor(lambda t: runner(t, "t"))
            b.vector(lambda v: runner(v, "v"))
            b.scalar(lambda s: runner(s, "s"))
    return nc


def prepack(inputs):
    x = np.asarray(inputs["L0_attn_x"])[0]
    xT = np.ascontiguousarray(x.T)
    d = {}
    wq = np.asarray(inputs["L0_attn_wq"])
    wk = np.asarray(inputs["L0_attn_wk"])
    wv = np.asarray(inputs["L0_attn_wv"])
    wo = np.asarray(inputs["L0_attn_wo"])
    d["wq_p"] = np.ascontiguousarray(
        wq.T.reshape(16, 128, 16, 128).transpose(2, 0, 1, 3))
    d["wk_p"] = np.ascontiguousarray(
        wk.T.reshape(16, 128, 16, 128).transpose(2, 0, 1, 3))
    d["wv_p"] = np.ascontiguousarray(wv.T.reshape(16, 128, DM))
    d["wo_p"] = np.ascontiguousarray(
        wo.T.reshape(16, 128, 16, 128).transpose(2, 0, 1, 3))
    wup = np.asarray(inputs["L0_ffn_W_up"])
    wgt = np.asarray(inputs["L0_ffn_W_gate"])
    wdn = np.asarray(inputs["L0_ffn_W_down"])
    d["fup_p"] = np.ascontiguousarray(
        wup.T.reshape(16, 8, 16, ITR).transpose(2, 0, 1, 3).reshape(16, 128, ITR))
    d["fgt_p"] = np.ascontiguousarray(
        wgt.T.reshape(16, 8, 16, ITR).transpose(2, 0, 1, 3).reshape(16, 128, ITR))
    d["fdn_p"] = np.ascontiguousarray(
        wdn.T.reshape(64, 2, 64, DM).transpose(2, 0, 1, 3).reshape(64, 128, DM))
    colpk = np.zeros((128, NCOLS), np.float32)

    def put(name, vec):
        w = vec.reshape(-1, 128).T
        colpk[:, COL[name]:COL[name] + w.shape[1]] = w
    put("bq", np.asarray(inputs["L0_attn_bq"]))
    put("bk", np.asarray(inputs["L0_attn_bk"]))
    put("gq", np.asarray(inputs["L0_attn_gq"]))
    put("gk", np.asarray(inputs["L0_attn_gk"]))
    put("bo", np.asarray(inputs["L0_attn_bo"]))
    put("su_ug", np.asarray(inputs["L0_ffn_SU_ug"]))
    put("su_dp", np.asarray(inputs["L0_ffn_SU_dp"]))
    put("wsu", np.asarray(inputs["L0_ffn_Wscale_ug"])[:ITR])
    put("wsg", np.asarray(inputs["L0_ffn_Wscale_ug"])[ITR:])
    put("wsd", np.asarray(inputs["L0_ffn_Wscale_dp"]))
    d["colpk"] = colpk
    d["bv_row"] = np.ascontiguousarray(np.asarray(inputs["L0_attn_bv"])[None, :])
    d["wmix_in"] = np.ascontiguousarray(np.asarray(inputs["L0_attn_w_mix"]))
    d.update(make_consts())
    maps = []
    for c in range(NCORE):
        idx = (np.arange(EXTP) + c * TC - HALO_B) % L
        m = dict(d)
        m["xT"] = np.ascontiguousarray(xT[:, idx])
        maps.append(m)
    return maps


LAST = None
LASTPROG = None


def kernel(**inputs):
    global LAST
    maps = prepack(inputs)
    nc = build(debug=DEBUG, phases=PHASES)
    res = run_bass_kernel_spmd(nc, maps, core_ids=list(range(NCORE)))
    LAST = res
    out = np.concatenate([res.results[c]["out"] for c in range(NCORE)], axis=0)
    return out.reshape(1, L, DM)


# version 10
# speedup vs baseline: 1.4466x; 1.0374x over previous
import contextlib
import numpy as np
import ml_dtypes
import concourse.bass as bass
import concourse.mybir as mybir
from concourse.bass_utils import run_bass_kernel_spmd

F32 = mybir.dt.float32
BF16 = mybir.dt.bfloat16
AF = mybir.ActivationFunctionType
ALU = mybir.AluOpType
AX = mybir.AxisListType
BF = ml_dtypes.bfloat16

L = 16384
DM = 2048
NH = 16
ITR = 8192
NCORE = 8
TC = L // NCORE
HALO_B = 1024
EXTP = 3584
NEC = EXTP // 128
SEQ_SHIFTS = [0, 1, -1, 3, -3, 7, -7, 20, -20, 53, -53, 141, -141, 380, -380, 1024]
OFFS = sorted(-s for s in SEQ_SHIFTS)
COFFS = [-8, -3, -2, -1, 0, 1, 2, 3]
NCO = len(COFFS)
CHAN = [0, 1, 2, 4, 8]
SCALE = float(128) ** -0.5
RMS_EPS = 1e-6
LN_EPS = 1e-5
NLT = TC // 128
RING = 13
NHG = 4
BSHIFT = 40.0

DEBUG = set()
DEFER6 = True
PHASES = ("P1", "P2K", "P2V", "P3", "P4", "P5", "P6", "P7")


NDMASLOT = 8


class _Node:
    __slots__ = ("eng", "sem", "emit", "reads", "writes", "deps", "idx",
                 "marked", "val", "tag", "is_dma", "extra", "waits")

    def __init__(self, eng, sem, emit, reads, writes, idx, tag, is_dma):
        self.eng = eng
        self.sem = sem
        self.emit = emit
        self.reads = tuple(reads)
        self.writes = tuple(writes)
        self.idx = idx
        self.deps = ()
        self.marked = False
        self.val = 0
        self.tag = tag
        self.is_dma = is_dma
        self.extra = ()


class Prog:
    ENGS = ("g", "sy", "t", "v", "s")
    DMA_SEMS = tuple(f"dg{k}" for k in range(NDMASLOT)) + \
        tuple(f"dsy{k}" for k in range(NDMASLOT))
    BAR_SEMS = ("bg", "bsy", "bt", "bv", "bs")

    def __init__(self):
        self.nodes = []
        self._bn = 0
        self._dmacnt = {"dg": 0, "dsy": 0}
        self._dmahist = {"dg": [], "dsy": []}
        self._lasteng = {}

    def add(self, eng, emit, reads=(), writes=(), dma=False, tag="", sem=None):
        reads = tuple(reads)
        if dma:
            q = "dg" if eng == "g" else "dsy"
            sem = f"{q}{self._dmacnt[q] % NDMASLOT}"
            self._dmacnt[q] += 1
            reads = reads + (("qtail", q),)
        elif sem is None:
            sem = eng
        n = _Node(eng, sem, emit, reads, writes, len(self.nodes), tag, dma)
        if dma:
            hist = self._dmahist[q]
            if len(hist) >= NDMASLOT:
                n.extra = (hist[-NDMASLOT],)
            hist.append(n)
        if emit is not None:
            self._lasteng[eng] = n
        self.nodes.append(n)
        self._dirty = True
        return n

    def barrier(self):
        if not getattr(self, "_dirty", False):
            return
        self._dirty = False
        bn = self._bn
        self._bn += 1
        flags = []
        for e in self.ENGS:
            w = [("bflag", bn, e)]
            if e == "g":
                w.append(("qtail", "dg"))
            if e == "sy":
                w.append(("qtail", "dsy"))
            n = self.add(e, None, writes=w, tag=f"bar1_{bn}_{e}", sem=f"b{e}")
            if e in self._lasteng:
                n.extra = n.extra + (self._lasteng[e],)
            flags.append(("bflag", bn, e))
        for e in self.ENGS:
            self.add(e, None, reads=flags, tag=f"bar2_{bn}_{e}", sem=f"b{e}")

    def resolve(self):
        last_w = {}
        readers = {}
        for n in self.nodes:
            deps = set(n.extra)
            for r in n.reads:
                w = last_w.get(r)
                if w is not None:
                    deps.add(w)
            for r in n.writes:
                w = last_w.get(r)
                if w is not None:
                    deps.add(w)
                deps.update(readers.get(r, ()))
            deps.discard(n)
            n.deps = tuple(deps)
            for r in n.reads:
                readers.setdefault(r, set()).add(n)
            for r in n.writes:
                last_w[r] = n
                readers[r] = {n}
        for n in self.nodes:
            if n.emit is not None:
                n.marked = True
            for d in n.deps:
                d.marked = True
        cum = {}
        for n in self.nodes:
            if n.marked:
                inc = 16 if n.is_dma else 1
                cum[n.sem] = cum.get(n.sem, 0) + inc
                n.val = cum[n.sem]
        vc = [None] * len(self.nodes)
        know = {e: {} for e in self.ENGS}
        for n in self.nodes:
            k = know[n.eng]
            waits = []
            for d in sorted(n.deps, key=lambda d_: (d_.sem, d_.val)):
                if k.get(d.sem, 0) >= d.val:
                    continue
                waits.append((d.sem, d.val))
                dv = vc[d.idx]
                if dv:
                    for s_, v_ in dv.items():
                        if k.get(s_, 0) < v_:
                            k[s_] = v_
                if k.get(d.sem, 0) < d.val:
                    k[d.sem] = d.val
            n.waits = waits
            if n.marked:
                k2 = dict(k)
                k2[n.sem] = n.val
                vc[n.idx] = k2
        per_eng = {e: [] for e in self.ENGS}
        for n in self.nodes:
            per_eng[n.eng].append(n)
        ptr = {e: 0 for e in per_eng}
        done = [False] * len(self.nodes)
        progress = True
        ndone = 0
        while progress:
            progress = False
            for e, lst in per_eng.items():
                while ptr[e] < len(lst):
                    n = lst[ptr[e]]
                    if all(done[d.idx] for d in n.deps):
                        done[n.idx] = True
                        ptr[e] += 1
                        ndone += 1
                        progress = True
                    else:
                        break
        if ndone != len(self.nodes):
            stuck = [n for n in self.nodes if not done[n.idx]][:5]
            info = "; ".join(
                f"{n.eng}#{n.idx}:{n.tag} waits " +
                ",".join(f"{d.eng}#{d.idx}:{d.tag}" for d in n.deps
                         if not done[d.idx]) for n in stuck)
            raise RuntimeError(f"deadlock in schedule: {info}")
        return per_eng

    def emit(self, nc, sems):
        per_eng = self.resolve()

        def run_engine(eng_obj, ename):
            for n in per_eng[ename]:
                for s_, v_ in n.waits:
                    eng_obj.wait_ge(sems[s_], v_)
                ins = n.emit(eng_obj) if n.emit is not None else None
                if n.marked:
                    if ins is None:
                        ins = eng_obj.nop()
                    ins.then_inc(sems[n.sem], 16 if n.is_dma else 1)
        return run_engine


def _hadamard(n):
    h = np.array([[1.0]])
    while h.shape[0] < n:
        h = np.block([[h, h], [h, -h]])
    return h


def make_consts():
    c = {}
    c["h128b_in"] = _hadamard(128).astype(BF)
    c["hfoldup_in"] = np.kron(_hadamard(16), np.eye(8)).astype(BF)
    c["hfolddn_in"] = np.kron(_hadamard(64), np.eye(2)).astype(BF)
    pcs = np.zeros((128, 5, 128), np.float32)
    f = np.arange(128)
    for ci, cs in enumerate(CHAN):
        pcs[f, ci, (f - cs) % 128] = 1.0
    c["pcs_in"] = pcs.astype(BF)
    offs = set(OFFS)
    m = np.zeros((128, NCO, 128), np.float32)
    for ci, co in enumerate(COFFS):
        for p in range(128):
            for q in range(128):
                if co * 128 + p - q in offs:
                    m[p, ci, q] = 1.0
    c["masks_in"] = m.astype(BF)
    c["identf_in"] = np.eye(128, dtype=np.float32)
    ind = np.zeros((16, 128, 16), np.float32)
    for m_ in range(16):
        ind[m_, :, m_] = 1.0
    c["indc_in"] = ind.astype(BF)
    c["ones_cb_in"] = np.ones((128, 1), BF)
    c["ones_rf_in"] = np.ones((1, 128), np.float32)
    c["epsv_in"] = np.array([[RMS_EPS, LN_EPS]], np.float32)
    return c


COL = {}
_o = 0
for _n, _w in [("bq", 16), ("bk", 16), ("gq", 16), ("gk", 16), ("bo", 16),
               ("su_ug", 16), ("wsd", 16), ("su_dp", 64), ("wsu", 64), ("wsg", 64)]:
    COL[_n] = _o
    _o += _w
NCOLS = _o


def build(debug=(), phases=PHASES):
    debug = set(debug)
    phases = set(phases)
    nc = bass.Bass("TRN2", target_bir_lowering=False)

    def dram(name, shape, dtype, kind="ExternalInput"):
        return nc.dram_tensor(name, shape, dtype, kind=kind).ap()

    def scr(name, shape, dtype):
        kind = "ExternalOutput" if name in debug else "Internal"
        return nc.dram_tensor(name, shape, dtype, kind=kind).ap()

    xT = dram("xT", [DM, EXTP], F32)
    wq_p = dram("wq_p", [16, 16, 128, 128], F32)
    wk_p = dram("wk_p", [16, 16, 128, 128], F32)
    wv_p = dram("wv_p", [16, 128, DM], F32)
    wo_p = dram("wo_p", [16, 16, 128, 128], F32)
    fup_p = dram("fup_p", [16, 128, ITR], F32)
    fgt_p = dram("fgt_p", [16, 128, ITR], F32)
    fdn_p = dram("fdn_p", [64, 128, DM], F32)
    colpk = dram("colpk", [128, NCOLS], F32)
    bv_row = dram("bv_row", [1, DM], F32)
    wmix_in = dram("wmix_in", [1, 5], F32)
    consts = make_consts()
    CIN = {k: dram(k, list(v.shape), BF16 if v.dtype == BF else F32)
           for k, v in consts.items()}
    out_t = dram("out", [TC, DM], F32, kind="ExternalOutput")

    qt_s = scr("qt_s", [16, 128, TC], BF16)
    kt_s = scr("kt_s", [NEC, 128, NH, 128], BF16)
    v_s = scr("v_s", [NEC, 128, NH, 128], BF16)
    y1_s = scr("y1_s", [16, 128, TC], F32)
    w1_s = scr("w1_s", [64, 128, TC], BF16)
    gup_s = scr("gup_s", [16, 128, ITR], BF16)
    ggt_s = scr("ggt_s", [16, 128, ITR], BF16)
    gdn_s = scr("gdn_s", [64, 128, DM], BF16)
    bb_s = nc.dram_tensor("bb_s", [16], F32, kind="Internal").ap()
    w2k_s = scr("w2k_s", [16, 128, TC], BF16) if "w2k_s" in debug else None
    ao_s = scr("ao_s", [NH, 128, TC], BF16)
    dbg_ao = "ao_s" in debug

    P = Prog()
    xTv = xT.rearrange("(c p) n -> p c n", p=128)

    with contextlib.ExitStack() as ctx:
        SB = lambda n_, s_, d_: ctx.enter_context(nc.sbuf_tensor(n_, s_, d_))
        sem_h = {k: ctx.enter_context(nc.semaphore(k))
                 for k in ("g", "sy", "t", "v", "s") + Prog.DMA_SEMS + Prog.BAR_SEMS}

        cols = SB("cols", [128, NCOLS], F32)
        mixh = SB("mixh", [128, NH, 128], BF16)
        h128b = SB("h128b", [128, 128], BF16)
        Bcols = SB("Bcols", [128, 16], F32)
        maxq2 = SB("maxq2", [16, 1], F32)
        maxk2 = SB("maxk2", [16, 1], F32)
        rtmp16 = SB("rtmp16", [16, 2], F32)
        indcb = SB("indcb", [128, 16, 16], BF16)
        negb = SB("negb", [16, 1], F32)
        ones_cb = SB("ones_cb", [128, 1], BF16)
        ones_rf = SB("ones_rf", [1, 128], F32)
        epst = SB("epst", [1, 2], F32)
        srow = SB("srow", [1, 512], F32)
        srow2 = SB("srow2", [1, 512], F32)
        rtmp = SB("rtmp", [1, 8], F32)

        hfoldup = SB("hfoldup", [128, 128], BF16)
        hfolddn = SB("hfolddn", [128, 128], BF16)
        identf = SB("identf", [128, 128], F32)
        bvrep = SB("bvrep", [128, DM], BF16)
        p0ctx = contextlib.ExitStack()
        wmix = p0ctx.enter_context(nc.sbuf_tensor("wmix", [1, 8], F32))
        pcs_t = p0ctx.enter_context(nc.sbuf_tensor("pcs_t", [128, 5, 128], BF16))
        wrep = p0ctx.enter_context(nc.sbuf_tensor("wrep", [128, 8], F32))
        mbase = p0ctx.enter_context(nc.sbuf_tensor("mbase", [128, 128], BF16))
        bvrow_t = p0ctx.enter_context(nc.sbuf_tensor("bvrow_t", [1, DM], F32))
        ps_gen = p0ctx.enter_context(nc.psum_tensor("ps_gen", [128, 1024], F32))

        for dst, src, rs in [
                (cols[:, :], colpk, "cols"), (wmix[:, 0:5], wmix_in, "wmix"),
                (pcs_t[:, :, :], CIN["pcs_in"], "pcs"),
                (ones_cb[:, :], CIN["ones_cb_in"], "ones_cb"),
                (ones_rf[:, :], CIN["ones_rf_in"], "ones_rf"),
                (epst[:, :], CIN["epsv_in"], "epst"),
                (h128b[:, :], CIN["h128b_in"], "h128b"),
                (hfoldup[:, :], CIN["hfoldup_in"], "hfoldup"),
                (hfolddn[:, :], CIN["hfolddn_in"], "hfolddn"),
                (identf[:, :], CIN["identf_in"], "identf"),
                (indcb[:, :, :], CIN["indc_in"].transpose([1, 0, 2]), "indcb"),
                (bvrow_t[:, :], bv_row, "bvrow")]:
            P.add("g", (lambda d_, s_: lambda g: g.dma_start(out=d_, in_=s_))(dst, src),
                  writes=[rs], dma=True, tag=f"ld_{rs}")
        P.add("s", lambda s: s.mul(wmix[:, 0:5], wmix[:, 0:5], SCALE),
              reads=["wmix"], writes=["wmix"], tag="wmix_scale")
        P.add("t", lambda t: t.matmul(ps_gen[:, 0:5], ones_rf[:, :], wmix[:, 0:5],
                                      start=True, stop=True),
              reads=["wmix", "ones_rf"], writes=["psg0"], tag="wrep_mm")
        P.add("t", lambda t: t.matmul(ps_gen[:, 512:1024], ones_rf[:, :],
                                      bvrow_t[:, 0:512], start=True, stop=True),
              reads=["bvrow", "ones_rf"], writes=["psg1"], tag="bvrep_mm0")

        P.add("v", lambda v: v.tensor_copy(wrep[:, 0:5], ps_gen[:, 0:5]),
              reads=["psg0"], writes=["wrep"], tag="wrep_cp")
        P.add("v", lambda v: v.tensor_scalar(mbase[:, :], pcs_t[:, 0, :], wrep[:, 0:1],
                                             None, ALU.mult),
              reads=["wrep", "pcs"], writes=["mbase"], tag="mb0")
        for ci in range(1, 5):
            P.add("v", (lambda ci_: lambda v: v.scalar_tensor_tensor(
                mbase[:, :], pcs_t[:, ci_, :], wrep[:, ci_:ci_ + 1], mbase[:, :],
                ALU.mult, ALU.add))(ci),
                reads=["wrep", "pcs", "mbase"], writes=["mbase"], tag=f"mb{ci}")
        for h in range(NH):
            P.add("v", (lambda h_: lambda v: v.tensor_scalar(
                mixh[:, h_, :], mbase[:, :],
                cols[:, COL["gk"] + h_:COL["gk"] + h_ + 1], None, ALU.mult))(h),
                reads=["mbase", "cols"], writes=["mixh"], tag=f"mixh{h}")
        P.add("v", lambda v: v.memset(maxq2[:, :], 0.0), writes=["maxq2"], tag="z_mq")
        P.add("v", lambda v: v.memset(maxk2[:, :], 0.0), writes=["maxk2"], tag="z_mk")
        for q in range(4):
            if q > 0:
                P.add("t", (lambda q_: lambda t: t.matmul(
                    ps_gen[:, 512:1024], ones_rf[:, :], bvrow_t[:, q_ * 512:(q_ + 1) * 512],
                    start=True, stop=True))(q),
                    reads=["bvrow"], writes=["psg1"], tag=f"bvrep_mm{q}")
            P.add("v", (lambda q_: lambda v: v.tensor_copy(
                bvrep[:, q_ * 512:(q_ + 1) * 512], ps_gen[:, 512:1024]))(q),
                reads=["psg1"], writes=["bvrep"], tag=f"bvrep_cp{q}")
        P.barrier()
        p0ctx.close()

        FOLDS = [(0, fup_p, gup_s, hfoldup, 16, ITR),
                 (1, fgt_p, ggt_s, hfoldup, 16, ITR),
                 (2, fdn_p, gdn_s, hfolddn, 64, DM)]

        def make_fold_emitter(PSB_, PPS_, sfx, psf_pool=None):
            gtile = PSB_(f"gt{sfx}", [128, 4096], BF16)
            frhs = PSB_(f"fr{sfx}", [128, 4, 512], BF16)
            psf = psf_pool if psf_pool is not None else PPS_(f"pf{sfx}", [128, 1024])

            def emit_bl(fw, bl):
                wi, src_t, dst_t, lhs, nbl, nj = fw
                njb = nj // 512
                for jb in range(njb):
                    P.add("g", (lambda s_, bl_, jb_: lambda g: g.dma_start(
                        out=frhs[:, jb_ % 4, :],
                        in_=s_[bl_][:, jb_ * 512:(jb_ + 1) * 512]))(src_t, bl, jb),
                        writes=[("frhs", sfx, jb % 4)], dma=True, tag=f"f{wi}_{bl}_{jb}")
                    P.add("t", (lambda l_, jb_: lambda t: t.matmul(
                        psf[:, (jb_ % 2) * 512:(jb_ % 2 + 1) * 512],
                        l_, frhs[:, jb_ % 4, :],
                        start=True, stop=True))(lhs[:, :], jb),
                        reads=[("frhs", sfx, jb % 4), "hfoldup", "hfolddn"],
                        writes=[("aux", jb % 2)], tag=f"fm{wi}_{bl}_{jb}")
                    if jb % 2 == 0:
                        P.add("s", (lambda jb_: lambda s: s.activation(
                            gtile[:, (jb_ % 8) * 512:(jb_ % 8 + 1) * 512],
                            psf[:, (jb_ % 2) * 512:(jb_ % 2 + 1) * 512], AF.Copy))(jb),
                            reads=[("aux", jb % 2)], writes=[("gtile", sfx, jb % 8)],
                            tag=f"fe{wi}_{bl}_{jb}")
                    else:
                        P.add("v", (lambda jb_: lambda v: v.tensor_copy(
                            gtile[:, (jb_ % 8) * 512:(jb_ % 8 + 1) * 512],
                            psf[:, (jb_ % 2) * 512:(jb_ % 2 + 1) * 512]))(jb),
                            reads=[("aux", jb % 2)], writes=[("gtile", sfx, jb % 8)],
                            tag=f"fe{wi}_{bl}_{jb}")
                    if (jb + 1) % 4 == 0 or jb == njb - 1:
                        j0 = (jb // 4) * 4
                        nsl = jb - j0 + 1
                        def gsph(d_, bl_, nbl_, j0_, nsl_):
                            def f(sy):
                                dst = d_.rearrange("a (r bl) j -> (a r) bl j",
                                                   bl=nbl_)[:, bl_,
                                                            j0_ * 512:(j0_ + nsl_) * 512]
                                g0 = (j0_ % 8) * 512
                                return sy.dma_start(out=dst,
                                                    in_=gtile[:, g0:g0 + nsl_ * 512])
                            return f
                        P.add("sy", gsph(dst_t, bl, nbl, j0, nsl),
                              reads=[("gtile", sfx, (j0 % 8) + j_) for j_ in range(nsl)],
                              writes=[f"g_s{wi}"], dma=True, tag=f"gsp{wi}_{bl}_{j0}")

            return emit_bl


        if {"P1", "P2K", "P2V"} & phases:
            with contextlib.ExitStack() as px:
                PSB = lambda n_, s_, d_: px.enter_context(nc.sbuf_tensor(n_, s_, d_))
                PPS = lambda n_, s_: px.enter_context(nc.psum_tensor(n_, s_, F32))
                wpool = PSB("wpool", [128, 16, 16, 128], BF16)
                xpool = PSB("xpool", [128, 2, 16, 512], BF16)
                rawp = PSB("rawp", [128, 16, 512], BF16)
                pool2 = PSB("pool2", [128, 2, 16, 512], BF16)
                pool3 = PSB("pool3", [128, 2, 16, 512], BF16)
                sqb = PSB("sqb", [128, 2, 512], BF16)
                pbig = PPS("pbig", [128, 2048])
                pstat = PPS("pstat", [1, 512])
                pnq = PPS("pnq", [16, 512])
                paux = PPS("paux", [128, 1024])
                prep = paux[:, 0:512]
                fold_em = make_fold_emitter(PSB, PPS, "u", psf_pool=paux) if "P5" in phases else None
                fold_work = ([(FOLDS[0], bl) for bl in range(16)] +
                             [(FOLDS[1], bl) for bl in range(16)] +
                             [(FOLDS[2], bl) for bl in range(64)]) if fold_em else []

                def folds(n):
                    for _ in range(n):
                        if fold_work:
                            fold_em(*fold_work.pop(0))

                tdefer = []

                def tpop(k=1):
                    for _ in range(min(k, len(tdefer))):
                        tdefer.pop(0)()

                def xload(tt, tag):
                    c0 = (HALO_B if tag == "x" else 0) + tt * 512
                    P.add("g", (lambda tt_, c_: lambda g: g.dma_start(
                        out=xpool[:, tt_ % 2, :, :], in_=xTv[:, :, c_:c_ + 512]))(tt, c0),
                        writes=[("xp", tt % 2)], dma=True, tag=f"{tag}{tt}")

                if "P1" in phases:
                    xload(0, "x")
                    xload(1, "x")
                    for m in range(16):
                        P.add("g", (lambda m_: lambda g: g.dma_start(
                            out=wpool[:, m_, :, :], in_=wq_p[m_].transpose([1, 0, 2])))(m),
                            writes=[("wp", m)], dma=True, tag=f"wq{m}")
                    for tt in range(4):
                        if tt >= 2:
                            xload(tt, "x")
                        for m in range(16):
                            def qmm(tt_, m_):
                                def f(t):
                                    sl = pbig[:, (m_ % 3) * 512:(m_ % 3 + 1) * 512]
                                    for k in range(16):
                                        ins = t.matmul(sl, wpool[:, m_, k, :],
                                                       xpool[:, tt_ % 2, k, :],
                                                       start=(k == 0), stop=(k == 15))
                                    return ins
                                return f
                            P.add("t", qmm(tt, m),
                                  reads=[("wp", m), ("xp", tt % 2)],
                                  writes=[("pb", m % 3)], tag=f"q{tt}_{m}")
                            tpop(1)
                            def qev(tt_, m_):
                                def f(s):
                                    sl = pbig[:, (m_ % 3) * 512:(m_ % 3 + 1) * 512]
                                    bcol = cols[:, COL["bq"] + m_:COL["bq"] + m_ + 1]
                                    s.activation(rawp[:, m_, :], sl, AF.Identity, bias=bcol)
                                    return s.activation(sqb[:, m_ % 2, :], sl, AF.Square,
                                                        bias=bcol)
                                return f
                            P.add("s", qev(tt, m), reads=[("pb", m % 3), "cols"],
                                  writes=[("raw", m), ("sqb", m % 2)], tag=f"qev{tt}_{m}")
                            tdefer.append((lambda m_: lambda: P.add(
                                "t", lambda t: t.matmul(
                                    pstat[:, :], ones_cb[:, :], sqb[:, m_ % 2, :],
                                    start=(m_ == 0), stop=(m_ == 15)),
                                reads=[("sqb", m_ % 2), "ones_cb"], writes=["pstat"],
                                tag=f"qssq{tt}_{m_}"))(m))
                            if m % 4 == 3:
                                folds(1)
                        tpop(16)
                        P.add("s", lambda s: s.activation(srow[:, :], pstat[:, :], AF.Sqrt,
                                                          bias=epst[:, 0:1], scale=1.0 / DM),
                              reads=["pstat", "epst"], writes=["srow"], tag=f"sq_sqrt{tt}")
                        P.add("v", lambda v: v.reciprocal(srow[:, :], srow[:, :]),
                              reads=["srow"], writes=["srow"], tag=f"sq_rcp{tt}")
                        P.add("t", lambda t: t.matmul(prep[:, :], ones_rf[:, :], srow[:, :],
                                                      start=True, stop=True),
                              reads=["srow", "ones_rf"], writes=[("aux", 0)],
                              tag=f"sq_rep{tt}")
                        for m in range(16):
                            P.add("v", (lambda tt_, m_: lambda v: v.scalar_tensor_tensor(
                                pool2[:, tt_ % 2, m_, :], rawp[:, m_, :],
                                cols[:, COL["gq"] + m_:COL["gq"] + m_ + 1], prep[:, :],
                                ALU.mult, ALU.mult))(tt, m),
                                reads=[("raw", m), ("aux", 0), "cols"],
                                writes=[("p2", tt % 2, m)], tag=f"qt{tt}_{m}")
                            P.add("s", (lambda tt_, m_: lambda s: s.activation(
                                sqb[:, m_ % 2, :], pool2[:, tt_ % 2, m_, :],
                                AF.Square))(tt, m),
                                reads=[("p2", tt % 2, m)], writes=[("sqb", m % 2)],
                                tag=f"nq_sq{tt}_{m}")
                            if m >= 1:
                                P.add("t", (lambda m_: lambda t: t.matmul(
                                    pnq[:, :], indcb[:, m_, :], sqb[:, m_ % 2, :],
                                    start=(m_ == 0), stop=(m_ == 15)))(m - 1),
                                    reads=[("sqb", (m - 1) % 2), "indcb"], writes=["pnq"],
                                    tag=f"nq_mm{tt}_{m - 1}")
                        P.add("t", (lambda m_: lambda t: t.matmul(
                            pnq[:, :], indcb[:, m_, :], sqb[:, m_ % 2, :],
                            start=(m_ == 0), stop=(m_ == 15)))(15),
                            reads=[("sqb", 15 % 2), "indcb"], writes=["pnq"],
                            tag=f"nq_mm{tt}_15")
                        P.add("v", lambda v: v.reduce_max(rtmp16[:, 0:1], pnq[:, :], AX.X),
                              reads=["pnq"], writes=["rt16q"], tag=f"nq_rm{tt}")
                        P.add("v", lambda v: v.tensor_max(maxq2[:, :], maxq2[:, :],
                                                          rtmp16[:, 0:1]),
                              reads=["rt16q", "maxq2"], writes=["maxq2"], tag=f"nq_mx{tt}")
                        P.add("sy", (lambda tt_: lambda sy: sy.dma_start(
                            out=qt_s[:, :, tt_ * 512:(tt_ + 1) * 512].transpose([1, 0, 2]),
                            in_=pool2[:, tt_ % 2, :, :]))(tt),
                            reads=[("p2", tt % 2, m) for m in range(16)],
                            writes=["qt_s"], dma=True, tag=f"qspill{tt}")

                if "P2K" in phases:
                    xload(0, "kx")
                    xload(1, "kx")
                    for m in range(16):
                        P.add("g", (lambda m_: lambda g: g.dma_start(
                            out=wpool[:, m_, :, :], in_=wk_p[m_].transpose([1, 0, 2])))(m),
                            writes=[("wp", m)], dma=True, tag=f"wk{m}")
                    for tt in range(7):
                        if tt >= 2:
                            xload(tt, "kx")
                        for m in range(16):
                            def kmm(tt_, m_):
                                def f(t):
                                    sl = pbig[:, (m_ % 2) * 512:(m_ % 2 + 1) * 512]
                                    for k in range(16):
                                        ins = t.matmul(sl, wpool[:, m_, k, :],
                                                       xpool[:, tt_ % 2, k, :],
                                                       start=(k == 0), stop=(k == 15))
                                    return ins
                                return f
                            P.add("t", kmm(tt, m), reads=[("wp", m), ("xp", tt % 2)],
                                  writes=[("pb", m % 2)], tag=f"k{tt}_{m}")
                            tpop(1)
                            def kev(tt_, m_):
                                def f(s):
                                    sl = pbig[:, (m_ % 2) * 512:(m_ % 2 + 1) * 512]
                                    bcol = cols[:, COL["bk"] + m_:COL["bk"] + m_ + 1]
                                    s.activation(rawp[:, m_, :], sl, AF.Identity, bias=bcol)
                                    return s.activation(sqb[:, m_ % 2, :], sl, AF.Square,
                                                        bias=bcol)
                                return f
                            P.add("s", kev(tt, m), reads=[("pb", m % 2), "cols"],
                                  writes=[("raw", m), ("sqb", m % 2)], tag=f"kev{tt}_{m}")
                            tdefer.append((lambda m_: lambda: P.add(
                                "t", lambda t: t.matmul(
                                    pstat[:, :], ones_cb[:, :], sqb[:, m_ % 2, :],
                                    start=(m_ == 0), stop=(m_ == 15)),
                                reads=[("sqb", m_ % 2), "ones_cb"], writes=["pstat"],
                                tag=f"kssq{tt}_{m_}"))(m))
                            if m % 4 == 3:
                                folds(1)
                        tpop(16)
                        P.add("s", lambda s: s.activation(srow2[:, :], pstat[:, :], AF.Sqrt,
                                                          bias=epst[:, 0:1], scale=1.0 / DM),
                              reads=["pstat", "epst"], writes=["srow2"], tag=f"sk_sqrt{tt}")
                        P.add("v", lambda v: v.reciprocal(srow2[:, :], srow2[:, :]),
                              reads=["srow2"], writes=["srow2"], tag=f"sk_rcp{tt}")
                        P.add("t", lambda t: t.matmul(prep[:, :], ones_rf[:, :], srow2[:, :],
                                                      start=True, stop=True),
                              reads=["srow2", "ones_rf"], writes=[("aux", 0)],
                              tag=f"sk_rep{tt}")
                        def kb_n(tt_, m_):
                            P.add("v", lambda v: v.tensor_mul(
                                pool2[:, tt_ % 2, m_, :], rawp[:, m_, :], prep[:, :]),
                                reads=[("raw", m_), ("aux", 0)],
                                writes=[("p2", tt_ % 2, m_)], tag=f"kb{tt_}_{m_}")

                        def ktm_n(tt_, m_):
                            def f(t):
                                sl = pbig[:, 1024 + (m_ % 2) * 512:1024 + (m_ % 2 + 1) * 512]
                                return t.matmul(sl, mixh[:, m_, :], pool2[:, tt_ % 2, m_, :],
                                                start=True, stop=True)
                            P.add("t", f, reads=[("p2", tt_ % 2, m_), "mixh"],
                                  writes=[("pb", 2 + m_ % 2)], tag=f"ktm{tt_}_{m_}")

                        def ktev_n(tt_, m_):
                            def f(s):
                                sl = pbig[:, 1024 + (m_ % 2) * 512:1024 + (m_ % 2 + 1) * 512]
                                s.activation(pool3[:, tt_ % 2, m_, :], sl, AF.Copy)
                                return s.activation(sqb[:, m_ % 2, :], sl, AF.Square)
                            P.add("s", f, reads=[("pb", 2 + m_ % 2)],
                                  writes=[("p3", tt_ % 2, m_), ("sqb", m_ % 2)],
                                  tag=f"ktev{tt_}_{m_}")

                        def nk_n(tt_, m_):
                            P.add("t", lambda t: t.matmul(
                                pnq[:, :], indcb[:, m_, :], sqb[:, m_ % 2, :],
                                start=(m_ == 0), stop=(m_ == 15)),
                                reads=[("sqb", m_ % 2), "indcb"], writes=["pnq"],
                                tag=f"nk_mm{tt_}_{m_}")

                        for m in range(16):
                            kb_n(tt, m)
                            if m >= 1:
                                ktm_n(tt, m - 1)
                                ktev_n(tt, m - 1)
                            if m >= 2:
                                nk_n(tt, m - 2)
                        ktm_n(tt, 15)
                        ktev_n(tt, 15)
                        nk_n(tt, 14)
                        nk_n(tt, 15)
                        P.add("v", lambda v: v.reduce_max(rtmp16[:, 1:2], pnq[:, :], AX.X),
                              reads=["pnq"], writes=["rt16k"], tag=f"nk_rm{tt}")
                        P.add("v", lambda v: v.tensor_max(maxk2[:, :], maxk2[:, :],
                                                          rtmp16[:, 1:2]),
                              reads=["rt16k", "maxk2"], writes=["maxk2"], tag=f"nk_mx{tt}")
                        for ei in range(4):
                            P.add("sy", (lambda tt_, ei_: lambda sy: sy.dma_start(
                                out=kt_s[tt_ * 4 + ei_],
                                in_=pool3[:, tt_ % 2, :, ei_ * 128:(ei_ + 1) * 128]))(tt, ei),
                                reads=[("p3", tt % 2, m) for m in range(16)],
                                writes=["kt_s"], dma=True, tag=f"kspill{tt}_{ei}")

                if "P2V" in phases:
                    xload(0, "vx")
                    xload(1, "vx")
                    for k in range(16):
                        P.add("g", (lambda k_: lambda g: g.dma_start(
                            out=wpool[:, k_, :, :],
                            in_=wv_p[k_].rearrange("p (a b) -> p a b", a=16)))(k),
                            writes=[("wp", k)], dma=True, tag=f"wv{k}")
                    for tt in range(7):
                        if tt >= 2:
                            xload(tt, "vx")
                        for ns in range(4):
                            for jb in range(4):
                                def vmm(tt_, ns_, jb_):
                                    def f(t):
                                        sl = pbig[:, jb_ * 512:(jb_ + 1) * 512]
                                        for k in range(16):
                                            ins = t.matmul(
                                                sl,
                                                xpool[:, tt_ % 2, k,
                                                      ns_ * 128:(ns_ + 1) * 128],
                                                wpool[:, k, :, :].rearrange(
                                                    "p a b -> p (a b)")[:, jb_ * 512:
                                                                        (jb_ + 1) * 512],
                                                start=(k == 0), stop=(k == 15))
                                        return ins
                                    return f
                                P.add("t", vmm(tt, ns, jb),
                                      reads=[("xp", tt % 2)] +
                                            [("wp", k) for k in range(16)],
                                      writes=[("pb", jb)], tag=f"v{tt}_{ns}_{jb}")
                                def vev(tt_, ns_, jb_):
                                    def f(v):
                                        dst = pool3[:, tt_ % 2, :, :].rearrange(
                                            "p a b -> p (a b)")[:, ns_ * 2048 + jb_ * 512:
                                                                ns_ * 2048 + (jb_ + 1) * 512]
                                        return v.tensor_add(
                                            dst, pbig[:, jb_ * 512:(jb_ + 1) * 512],
                                            bvrep[:, jb_ * 512:(jb_ + 1) * 512])
                                    return f
                                P.add("v", vev(tt, ns, jb),
                                      reads=[("pb", jb), "bvrep"],
                                      writes=[("p3", tt % 2, 4 * ns + jb)],
                                      tag=f"vev{tt}_{ns}_{jb}")
                            folds(2)
                        for ei in range(4):
                            def vsp(tt_, ei_):
                                def f(sy):
                                    srcv = pool3[:, tt_ % 2, :, :].rearrange(
                                        "p a b -> p (a b)")[:, ei_ * 2048:(ei_ + 1) * 2048]
                                    return sy.dma_start(
                                        out=v_s[tt_ * 4 + ei_],
                                        in_=srcv.rearrange("p (h d) -> p h d", h=NH))
                                return f
                            P.add("sy", vsp(tt, ei),
                                  reads=[("p3", tt % 2, 4 * ei + j) for j in range(4)],
                                  writes=["v_s"], dma=True, tag=f"vspill{tt}_{ei}")
                while fold_work:
                    fold_em(*fold_work.pop(0))

        P.barrier()
        if "P3" in phases:
            with contextlib.ExitStack() as aoctx:
                attnout = aoctx.enter_context(
                    nc.sbuf_tensor("attnout", [128, NH, NLT, 128], BF16))
                wot = aoctx.enter_context(
                    nc.sbuf_tensor("wot", [128, 16, 16, 128], BF16))
                masks = aoctx.enter_context(
                    nc.sbuf_tensor("masks", [128, NCO, 128], BF16))
                P.add("g", lambda g: g.dma_start(out=masks[:, :, :], in_=CIN["masks_in"]),
                      writes=["masks"], dma=True, tag="ld_masks")
                p3ctx = contextlib.ExitStack()
                P3SB = lambda n_, s_, d_: p3ctx.enter_context(nc.sbuf_tensor(n_, s_, d_))
                P3PS = lambda n_, s_: p3ctx.enter_context(nc.psum_tensor(n_, s_, F32))
                ringk = P3SB("ringk", [128, RING, NHG, 128], BF16)
                ringv = P3SB("ringv", [128, RING, NHG, 128], BF16)
                qtb = P3SB("qtb", [128, 2, NHG, 128], BF16)
                expb = P3SB("expb", [128, 2, NCO, 128], BF16)
                pband = P3PS("pband", [128, 2048])
                patt = P3PS("patt", [128, 1024])
                pden = P3PS("pden", [1, 512])
                prep3 = P3PS("prep3", [128, 512])
                recrow = P3SB("recrow", [1, 512], F32)
                reprow = P3SB("reprow", [128, 512], F32)

                P.add("v", lambda v: v.tensor_mul(negb[:, :], maxq2[:, :], maxk2[:, :]),
                      reads=["maxq2", "maxk2"], writes=["negb"], tag="b_mul")
                P.add("s", lambda s: s.activation(negb[:, :], negb[:, :], AF.Sqrt),
                      reads=["negb"], writes=["negb"], tag="b_sqrt")
                P.add("v", lambda v: v.tensor_scalar(negb[:, :], negb[:, :], -1.0,
                                                     BSHIFT, ALU.mult, ALU.add),
                      reads=["negb"], writes=["negb"], tag="b_neg")
                P.add("sy", lambda sy: sy.dma_start(out=bb_s, in_=negb[:, 0]),
                      reads=["negb"], writes=["bb_s"], dma=True, tag="b_dma1")
                P.add("sy", lambda sy: sy.dma_start(out=srow[0:1, 0:16], in_=bb_s[None, :]),
                      reads=["bb_s"], writes=["srow"], dma=True, tag="b_dma2")
                P.add("t", lambda t: t.matmul(prep3[:, 0:16], ones_rf[:, :], srow[:, 0:16],
                                              start=True, stop=True),
                      reads=["srow", "ones_rf"], writes=["prep3"], tag="b_rep")
                P.add("v", lambda v: v.tensor_copy(Bcols[:, :], prep3[:, 0:16]),
                      reads=["prep3"], writes=["Bcols"], tag="b_cp")

                p3tail = []
                for hg in range(NH // NHG):
                    h0 = hg * NHG
                    loaded = set()
                    while p3tail:
                        p3tail.pop(0)()
                    for kcw in (hg * 2, hg * 2 + 1) if "P4" in phases else ():
                        P.add("g", (lambda kc_: lambda g: g.dma_start(
                            out=wot[:, kc_, :, :],
                            in_=wo_p[kc_].transpose([1, 0, 2])))(kcw),
                            writes=[("wot", kcw)], dma=True, tag=f"wol{kcw}")
                    for kcw in (8 + hg * 2, 8 + hg * 2 + 1) if "P4" in phases else ():
                        P.add("g", (lambda kc_: lambda g: g.dma_start(
                            out=wot[:, kc_, :, :],
                            in_=wo_p[kc_].transpose([1, 0, 2])))(kcw),
                            writes=[("wot", kcw)], dma=True, tag=f"wol{kcw}")

                    def ring_load(e, h0_=h0, hg_=hg):
                        P.add("sy", (lambda e_: lambda sy: sy.dma_start(
                            out=ringk[:, e_ % RING, :, :],
                            in_=kt_s[e_, :, h0_:h0_ + NHG, :]))(e),
                            reads=["kt_s"], writes=[("ringk", e % RING)],
                            dma=True, tag=f"rk{hg_}_{e}")
                        P.add("g", (lambda e_: lambda g: g.dma_start(
                            out=ringv[:, e_ % RING, :, :],
                            in_=v_s[e_, :, h0_:h0_ + NHG, :]))(e),
                            reads=["v_s"], writes=[("ringv", e % RING)],
                            dma=True, tag=f"rv{hg_}_{e}")
                        loaded.add(e)

                    for e in [0, 5, 6, 7, 8, 9, 10, 11]:
                        ring_load(e)
                    for lt in range(NLT):
                        for e in (lt, lt + 11):
                            if e not in loaded and 0 <= e < NEC:
                                ring_load(e)
                        P.add("sy", (lambda h0_, lt_: lambda sy: sy.dma_start(
                            out=qtb[:, lt_ % 2, :, :],
                            in_=qt_s[h0_:h0_ + NHG, :, lt_ * 128:(lt_ + 1) * 128]
                            .transpose([1, 0, 2])))(h0, lt),
                            reads=["qt_s"], writes=[("qtb", lt % 2)], dma=True,
                            tag=f"q3{hg}_{lt}")
                        def smm_n(hh, uf, lt_, ecs):
                            def f(t):
                                for ci, e in enumerate(ecs):
                                    sl = pband[:, uf * 1024 + ci * 128:
                                               uf * 1024 + (ci + 1) * 128]
                                    ins = t.matmul(sl, ringk[:, e % RING, hh, :],
                                                   qtb[:, lt_ % 2, hh, :],
                                                   start=True, stop=True)
                                return ins
                            P.add("t", f,
                                  reads=[("ringk", e % RING) for e in ecs] +
                                        [("qtb", lt_ % 2)],
                                  writes=[("pband", uf)], tag=f"sc{hg}_{lt_}_{hh}")

                        def ex_n(hh, uf, lt_):
                            P.add("s", lambda s: s.activation(
                                expb[:, uf, :, :], pband[:, uf * 1024:(uf + 1) * 1024],
                                AF.Exp, bias=Bcols[:, h0 + hh:h0 + hh + 1]),
                                reads=[("pband", uf), "Bcols"], writes=[("expb", uf)],
                                tag=f"ex{hg}_{lt_}_{hh}")

                        def mk_n(hh, uf, lt_):
                            P.add("v", lambda v: v.tensor_mul(
                                expb[:, uf, :, :], expb[:, uf, :, :], masks[:, :, :]),
                                reads=[("expb", uf), "masks"], writes=[("expb", uf)],
                                tag=f"mk{hg}_{lt_}_{hh}")

                        def av_n(hh, uf, lt_, ecs):
                            def f(t):
                                po = (lt_ % 2) * 512
                                for ci, e in enumerate(ecs):
                                    ins = t.matmul(
                                        patt[:, po + hh * 128:po + (hh + 1) * 128],
                                        ringv[:, e % RING, hh, :], expb[:, uf, ci, :],
                                        start=(ci == 0), stop=(ci == 7))
                                return ins
                            P.add("t", f,
                                  reads=[("ringv", e % RING) for e in ecs] +
                                        [("expb", uf)],
                                  writes=[("patt", lt_ % 2, hh)], tag=f"av{hg}_{lt_}_{hh}")

                        def dn_n(hh, uf, lt_):
                            def f(t):
                                for ci in range(NCO):
                                    ins = t.matmul(
                                        pden[:, hh * 128:(hh + 1) * 128],
                                        ones_cb[:, :], expb[:, uf, ci, :],
                                        start=(ci == 0), stop=(ci == 7))
                                return ins
                            P.add("t", f, reads=[("expb", uf), "ones_cb"],
                                  writes=[("pden", hh)], tag=f"dn{hg}_{lt_}_{hh}")

                        def mixtail(hg_, lt_):
                            def emit():
                                P.add("v", lambda v: v.reciprocal(recrow[:, :], pden[:, :]),
                                      reads=[("pden", 0), ("pden", 1), ("pden", 2),
                                             ("pden", 3)],
                                      writes=["recrow"], tag=f"rc{hg_}_{lt_}")
                                P.add("t", lambda t: t.matmul(
                                    prep3[:, :], ones_rf[:, :], recrow[:, :],
                                    start=True, stop=True),
                                    reads=["recrow", "ones_rf"], writes=["prep3"],
                                    tag=f"rcrep{hg_}_{lt_}")
                                P.add("v", lambda v: v.tensor_copy(reprow[:, :],
                                                                   prep3[:, :]),
                                      reads=["prep3"], writes=["reprow"],
                                      tag=f"rpc{hg_}_{lt_}")
                                def f(v):
                                    po = (lt_ % 2) * 512
                                    return v.tensor_mul(
                                        attnout[:, hg_ * NHG:(hg_ + 1) * NHG, lt_, :],
                                        patt[:, po:po + 512].rearrange(
                                            "p (h l) -> p h l", h=NHG),
                                        reprow[:, :].rearrange("p (h l) -> p h l", h=NHG))
                                P.add("v", f,
                                      reads=[("patt", lt_ % 2, hh_) for hh_ in range(NHG)] +
                                            ["reprow"],
                                      writes=[("attnout", hg_, lt_)], tag=f"ao{hg_}_{lt_}")
                            return emit

                        ecs = [lt + 8 + co for co in COFFS]
                        for hh in range(NHG):
                            uf = hh % 2
                            smm_n(hh, uf, lt, ecs)
                            if hh == 1 and p3tail:
                                p3tail.pop(0)()
                            ex_n(hh, uf, lt)
                            mk_n(hh, uf, lt)
                            if hh >= 1:
                                av_n(hh - 1, (hh - 1) % 2, lt, ecs)
                                dn_n(hh - 1, (hh - 1) % 2, lt)
                        av_n(3, 1, lt, ecs)
                        dn_n(3, 1, lt)
                        p3tail.append(mixtail(hg, lt))
                while p3tail:
                    p3tail.pop(0)()
                if dbg_ao:
                    P.add("sy", lambda sy: sy.dma_start(
                        out=ao_s[:, :, :].transpose([1, 0, 2]),
                        in_=attnout[:, :, :, :].rearrange("p h lt l -> p h (lt l)")),
                        reads=[("attnout", hg_, lt_) for hg_ in range(4) for lt_ in range(NLT)],
                        writes=["ao_s"], dma=True, tag="ao_dump")

                P.barrier()
                p3ctx.close()
                if "P4" in phases:
                    with contextlib.ExitStack() as px:
                        PSB = lambda n_, s_, d_: px.enter_context(nc.sbuf_tensor(n_, s_, d_))
                        PPS = lambda n_, s_: px.enter_context(nc.psum_tensor(n_, s_, F32))
                        v1b = PSB("v1b", [128, 16, 512], F32)
                        xres = PSB("xres", [128, 4, 512], F32)
                        v1c = PSB("v1c", [128, 4, 512], BF16)
                        pso = PPS("pso", [128, 1024])
                        psl1 = PPS("psl1", [1, 512])
                        psl2 = PPS("psl2", [1, 512])
                        psr1 = PPS("psr1", [128, 512])
                        psr2 = PPS("psr2", [128, 512])
                        mrow = PSB("mrow", [1, 512], F32)
                        irow = PSB("irow", [1, 512], F32)

                        pend4 = []
                        for sgi in range(4):
                            for kc in range(16):
                                if len(pend4) >= 2:
                                    pend4.pop(0)()
                                P.add("g", (lambda sgi_, kc_: lambda g: g.dma_start(
                                    out=xres[:, kc_ % 4, :],
                                    in_=xT[kc_ * 128:(kc_ + 1) * 128,
                                           HALO_B + sgi_ * 512:HALO_B + (sgi_ + 1) * 512]))(sgi, kc),
                                    writes=[("xres", kc % 4)], dma=True,
                                    tag=f"xr{sgi}_{kc}")
                                def omm(sgi_, kc_):
                                    def f(t):
                                        sl = pso[:, (kc_ % 2) * 512:(kc_ % 2 + 1) * 512]
                                        lt0 = sgi_ * 4
                                        for hc in range(16):
                                            rhs = attnout[:, hc, lt0:lt0 + 4, :]
                                            ins = t.matmul(sl, wot[:, kc_, hc, :],
                                                           rhs, start=(hc == 0),
                                                           stop=(hc == 15))
                                        return ins
                                    return f
                                P.add("t", omm(sgi, kc),
                                      reads=[("wot", kc)] +
                                            [("attnout", hg_, sgi * 4 + i_)
                                             for hg_ in range(4) for i_ in range(4)],
                                      writes=[("pso", kc % 2)], tag=f"o{sgi}_{kc}")
                                def ov(sgi_, kc_):
                                    def f(v):
                                        sl = pso[:, (kc_ % 2) * 512:(kc_ % 2 + 1) * 512]
                                        return v.scalar_tensor_tensor(
                                            v1b[:, kc_, :], sl,
                                            cols[:, COL["bo"] + kc_:COL["bo"] + kc_ + 1],
                                            xres[:, kc_ % 4, :], ALU.add, ALU.add)
                                    return f
                                P.add("v", ov(sgi, kc),
                                      reads=[("pso", kc % 2), ("xres", kc % 4), "cols"],
                                      writes=[("v1b", kc)], tag=f"ov{sgi}_{kc}")
                                P.add("s", (lambda kc_: lambda s: s.activation(
                                    v1c[:, 2 * (kc_ % 2), :], v1b[:, kc_, :], AF.Copy))(kc),
                                    reads=[("v1b", kc)], writes=[("v1c", 2 * (kc % 2))],
                                    tag=f"oc{sgi}_{kc}")
                                P.add("s", (lambda kc_: lambda s: s.activation(
                                    v1c[:, 2 * (kc_ % 2) + 1, :], v1b[:, kc_, :],
                                    AF.Square))(kc),
                                    reads=[("v1b", kc)], writes=[("v1c", 2 * (kc % 2) + 1)],
                                    tag=f"oq{sgi}_{kc}")
                                def osd(sgi_, kc_):
                                    def emit():
                                        P.add("t", lambda t: t.matmul(
                                            psl1[:, :], ones_cb[:, :],
                                            v1c[:, 2 * (kc_ % 2), :],
                                            start=(kc_ == 0), stop=(kc_ == 15)),
                                            reads=[("v1c", 2 * (kc_ % 2)), "ones_cb"],
                                            writes=["psl1"], tag=f"os1_{sgi_}_{kc_}")
                                        P.add("t", lambda t: t.matmul(
                                            psl2[:, :], ones_cb[:, :],
                                            v1c[:, 2 * (kc_ % 2) + 1, :],
                                            start=(kc_ == 0), stop=(kc_ == 15)),
                                            reads=[("v1c", 2 * (kc_ % 2) + 1), "ones_cb"],
                                            writes=["psl2"], tag=f"os2_{sgi_}_{kc_}")
                                    return emit
                                pend4.append(osd(sgi, kc))
                            while pend4:
                                pend4.pop(0)()
                            def lnrow(s):
                                s.activation(mrow[:, :], psl1[:, :], AF.Copy, scale=1.0 / DM)
                                return s.activation(irow[:, :], psl2[:, :], AF.Copy,
                                                    scale=1.0 / DM)
                            P.add("s", lnrow, reads=["psl1", "psl2"],
                                  writes=["mrow", "irow"], tag=f"lnr{sgi}")
                            P.add("v", lambda v: v.scalar_tensor_tensor(
                                srow[:, 0:512], mrow[:, :], 1.0, mrow[:, :],
                                ALU.mult, ALU.mult),
                                reads=["mrow"], writes=["srow"], tag=f"lnva{sgi}")
                            P.add("v", lambda v: v.tensor_sub(irow[:, :], irow[:, :],
                                                              srow[:, 0:512]),
                                  reads=["srow", "irow"], writes=["irow"], tag=f"lnvb{sgi}")
                            P.add("s", lambda s: s.activation(irow[:, :], irow[:, :],
                                                              AF.Sqrt, bias=epst[:, 1:2]),
                                  reads=["irow", "epst"], writes=["irow"], tag=f"lns{sgi}")
                            P.add("v", lambda v: v.reciprocal(irow[:, :], irow[:, :]),
                                  reads=["irow"], writes=["irow"], tag=f"lnca{sgi}")
                            P.add("v", lambda v: v.tensor_mul(mrow[:, :], mrow[:, :],
                                                              irow[:, :]),
                                  reads=["irow", "mrow"], writes=["mrow"], tag=f"lncb{sgi}")
                            P.add("t", lambda t: t.matmul(psr1[:, :], ones_rf[:, :],
                                                          irow[:, :], start=True, stop=True),
                                  reads=["irow", "ones_rf"], writes=["psr1"],
                                  tag=f"lnR1{sgi}")
                            P.add("t", lambda t: t.matmul(psr2[:, :], ones_rf[:, :],
                                                          mrow[:, :], start=True, stop=True),
                                  reads=["mrow", "ones_rf"], writes=["psr2"],
                                  tag=f"lnR2{sgi}")
                            for kc in range(16):
                                P.add("v", (lambda kc_: lambda v: v.tensor_mul(
                                    v1b[:, kc_, :], v1b[:, kc_, :], psr1[:, :]))(kc),
                                    reads=[("v1b", kc), "psr1"], writes=[("v1b", kc)],
                                    tag=f"y1a{sgi}_{kc}")
                                P.add("v", (lambda kc_: lambda v: v.tensor_sub(
                                    v1b[:, kc_, :], v1b[:, kc_, :], psr2[:, :]))(kc),
                                    reads=[("v1b", kc), "psr2"], writes=[("v1b", kc)],
                                    tag=f"y1b{sgi}_{kc}")
                                if kc % 4 == 3:
                                    q = kc // 4
                                    P.add("sy", (lambda sgi_, q_: lambda sy: sy.dma_start(
                                        out=y1_s[4 * q_:4 * q_ + 4, :,
                                                 sgi_ * 512:(sgi_ + 1) * 512]
                                        .transpose([1, 0, 2]),
                                        in_=v1b[:, 4 * q_:4 * q_ + 4, :]))(sgi, q),
                                        reads=[("v1b", 4 * q + j_) for j_ in range(4)],
                                        writes=["y1_s"], dma=True, tag=f"y1sp{sgi}_{q}")

        P.barrier()
        P.barrier()
        p67 = contextlib.ExitStack()
        w1lt = p67.enter_context(nc.sbuf_tensor("w1lt", [128, 64, 512], BF16))
        if "P6" in phases:
            with contextlib.ExitStack() as px:
                PSB = lambda n_, s_, d_: px.enter_context(nc.sbuf_tensor(n_, s_, d_))
                PPS = lambda n_, s_: px.enter_context(nc.psum_tensor(n_, s_, F32))
                w2048 = PSB("w2048", [128, 16, TC], BF16)
                y1bt = PSB("y1bt", [128, 2, TC], BF16)
                suu = PSB("suu", [128, 16], F32)
                sud = PSB("sud", [128, 64], F32)
                hsa = PSB("hsa", [128, 2, 128], BF16)
                gupsl = PSB("gupsl", [128, 2, 16, 128], BF16)
                ggtsl = PSB("ggtsl", [128, 2, 16, 128], BF16)
                silub = PSB("silub", [128, 2, 512], BF16)
                upb = PSB("upb", [128, 2, 512], BF16)
                x1b = PSB("x1b", [128, 2, 512], BF16)
                w1b = PSB("w1b", [128, 2, TC], BF16)
                psu = PPS("psu", [128, 1024])
                psg6 = PPS("psg6", [128, 1024])
                psw = PPS("psw", [128, 1024])

                P.add("v", lambda v: v.tensor_scalar(suu[:, :], cols[:, COL["su_ug"]:
                                                     COL["su_ug"] + 16],
                                                     float(DM) ** -0.5, None, ALU.mult),
                      reads=["cols"], writes=["suu"], tag="suu")
                P.add("v", lambda v: v.tensor_scalar(sud[:, :], cols[:, COL["su_dp"]:
                                                     COL["su_dp"] + 64],
                                                     float(ITR) ** -0.5, None, ALU.mult),
                      reads=["cols"], writes=["sud"], tag="sud")
                for a in range(16):
                    P.add("g", (lambda a_: lambda g: g.dma_start(
                        out=y1bt[:, a_ % 2, :], in_=y1_s[a_]))(a),
                        reads=["y1_s"], writes=[("y1bt", a % 2)], dma=True, tag=f"y1l{a}")
                    P.add("v", (lambda a_: lambda v: v.tensor_scalar(
                        hsa[:, a_ % 2, :], h128b[:, :], suu[:, a_:a_ + 1], None,
                        ALU.mult))(a),
                        reads=["h128b", "suu"], writes=[("hsa", a % 2)], tag=f"hsa{a}")
                    for lb in range(4):
                        P.add("t", (lambda a_, lb_: lambda t: t.matmul(
                            psu[:, (lb_ % 2) * 512:(lb_ % 2 + 1) * 512], hsa[:, a_ % 2, :],
                            y1bt[:, a_ % 2, lb_ * 512:(lb_ + 1) * 512],
                            start=True, stop=True))(a, lb),
                            reads=[("hsa", a % 2), ("y1bt", a % 2)],
                            writes=[("psu", lb % 2)], tag=f"sa{a}_{lb}")
                        P.add("s", (lambda a_, lb_: lambda s: s.activation(
                            w2048[:, a_, lb_ * 512:(lb_ + 1) * 512],
                            psu[:, (lb_ % 2) * 512:(lb_ % 2 + 1) * 512], AF.Copy))(a, lb),
                            reads=[("psu", lb % 2)], writes=[("w2048", a)],
                            tag=f"sae{a}_{lb}")
                if w2k_s is not None:
                    P.add("sy", lambda sy: sy.dma_start(
                        out=w2k_s.transpose([1, 0, 2]), in_=w2048[:, :, :]),
                        reads=[("w2048", a_) for a_ in range(16)],
                        writes=["w2k_s"], dma=True, tag="w2kdump")
                pend6 = []
                for i in range(64):
                    while len(pend6) >= 2:
                        pend6.pop(0)()
                    P.add("g", (lambda i_: lambda g: g.dma_start(
                        out=gupsl[:, i_ % 2, :, :],
                        in_=gup_s[:, :, i_ * 128:(i_ + 1) * 128].transpose([1, 0, 2])))(i),
                        reads=["g_s0"], writes=[("gupsl", i % 2)], dma=True, tag=f"gu{i}")
                    P.add("g", (lambda i_: lambda g: g.dma_start(
                        out=ggtsl[:, i_ % 2, :, :],
                        in_=ggt_s[:, :, i_ * 128:(i_ + 1) * 128].transpose([1, 0, 2])))(i),
                        reads=["g_s1"], writes=[("ggtsl", i % 2)], dma=True, tag=f"gg{i}")
                    P.add("v", (lambda i_: lambda v: v.tensor_scalar(
                        hsa[:, i_ % 2, :], h128b[:, :], sud[:, i_:i_ + 1], None,
                        ALU.mult))(i),
                        reads=["h128b", "sud"], writes=[("hsa", i % 2)], tag=f"hsd{i}")
                    for lb in range(4):
                        def upmm(i_, lb_):
                            def f(t):
                                sl = psu[:, (lb_ % 2) * 512:(lb_ % 2 + 1) * 512]
                                for a in range(16):
                                    ins = t.matmul(sl, gupsl[:, i_ % 2, a, :],
                                                   w2048[:, a, lb_ * 512:(lb_ + 1) * 512],
                                                   start=(a == 0), stop=(a == 15))
                                return ins
                            return f
                        P.add("t", upmm(i, lb),
                              reads=[("gupsl", i % 2)] + [("w2048", a_) for a_ in range(16)],
                              writes=[("psu", lb % 2)], tag=f"up{i}_{lb}")
                        def gtmm(i_, lb_):
                            def f(t):
                                sl = psg6[:, (lb_ % 2) * 512:(lb_ % 2 + 1) * 512]
                                for a in range(16):
                                    ins = t.matmul(sl, ggtsl[:, i_ % 2, a, :],
                                                   w2048[:, a, lb_ * 512:(lb_ + 1) * 512],
                                                   start=(a == 0), stop=(a == 15))
                                return ins
                            return f
                        P.add("t", gtmm(i, lb),
                              reads=[("ggtsl", i % 2)] + [("w2048", a_) for a_ in range(16)],
                              writes=[("psg6", lb % 2)], tag=f"gt{i}_{lb}")
                        if len(pend6) >= 2:
                            pend6.pop(0)()
                        P.add("s", (lambda i_, lb_: lambda s: s.activation(
                            upb[:, lb_ % 2, :], psu[:, (lb_ % 2) * 512:(lb_ % 2 + 1) * 512],
                            AF.Identity, scale=cols[:, COL["wsu"] + i_:COL["wsu"] + i_ + 1]))(i, lb),
                            reads=[("psu", lb % 2), "cols"], writes=[("upb", lb % 2)],
                            tag=f"ue{i}_{lb}")
                        P.add("s", (lambda i_, lb_: lambda s: s.activation(
                            silub[:, lb_ % 2, :], psg6[:, (lb_ % 2) * 512:(lb_ % 2 + 1) * 512],
                            AF.Silu, scale=cols[:, COL["wsg"] + i_:COL["wsg"] + i_ + 1]))(i, lb),
                            reads=[("psg6", lb % 2), "cols"], writes=[("silub", lb % 2)],
                            tag=f"se{i}_{lb}")
                        P.add("v", (lambda lb_: lambda v: v.tensor_mul(
                            x1b[:, lb_ % 2, :], silub[:, lb_ % 2, :], upb[:, lb_ % 2, :]))(lb),
                            reads=[("silub", lb % 2), ("upb", lb % 2)],
                            writes=[("x1b", lb % 2)], tag=f"x1{i}_{lb}")
                        def swd(i_, lb_):
                            def emit():
                                P.add("t", lambda t: t.matmul(
                                    psw[:, (lb_ % 2) * 512:(lb_ % 2 + 1) * 512],
                                    hsa[:, i_ % 2, :], x1b[:, lb_ % 2, :],
                                    start=True, stop=True),
                                    reads=[("hsa", i_ % 2), ("x1b", lb_ % 2)],
                                    writes=[("psw", lb_ % 2)], tag=f"sw{i_}_{lb_}")
                                P.add("s", lambda s: s.activation(
                                    w1b[:, i_ % 2, lb_ * 512:(lb_ + 1) * 512],
                                    psw[:, (lb_ % 2) * 512:(lb_ % 2 + 1) * 512], AF.Copy),
                                    reads=[("psw", lb_ % 2)], writes=[("w1b", i_ % 2, lb_)],
                                    tag=f"we{i_}_{lb_}")
                            return emit
                        if DEFER6:
                            pend6.append(swd(i, lb))
                        else:
                            swd(i, lb)()
                    def w1spd(i_):
                        def emit():
                            P.add("sy", lambda sy: sy.dma_start(
                                out=w1_s[i_], in_=w1b[:, i_ % 2, :]),
                                reads=[("w1b", i_ % 2, lb_) for lb_ in range(4)],
                                writes=["w1_s"], dma=True, tag=f"w1sp{i_}")
                            P.add("g", lambda g: g.dma_start(
                                out=w1lt[:, i_, :], in_=w1b[:, i_ % 2, 0:512]),
                                reads=[("w1b", i_ % 2, 0)], writes=["w1lt"],
                                dma=True, tag=f"w1c{i_}")
                        return emit
                    if DEFER6:
                        pend6.append(w1spd(i))
                    else:
                        w1spd(i)()

                while pend6:
                    pend6.pop(0)()
        P.barrier()
        if "P7" in phases:
            with contextlib.ExitStack() as px:
                PSB = lambda n_, s_, d_: px.enter_context(nc.sbuf_tensor(n_, s_, d_))
                PPS = lambda n_, s_: px.enter_context(nc.psum_tensor(n_, s_, F32))
                gdnsl = PSB("gdnsl", [128, 2, 64, 128], BF16)
                y1sl = PSB("y1sl", [128, 2, 512], F32)
                v2b = PSB("v2b", [128, 16, 512], F32)
                v2c = PSB("v2c", [128, 4, 512], BF16)
                outb = PSB("outb", [128, 4, 2048], F32)
                mrow2 = PSB("mrow2", [1, 512], F32)
                irow2 = PSB("irow2", [1, 512], F32)
                psd = PPS("psd", [128, 1024])
                psl12 = PPS("psl12", [1, 512])
                psl22 = PPS("psl22", [1, 512])
                psr12 = PPS("psr12", [128, 512])
                psr22 = PPS("psr22", [128, 512])
                pstr = PPS("pstr", [128, 1024])

                pend7 = []

                def w1load(lg_):
                    P.add("sy", (lambda l_: lambda sy: sy.dma_start(
                        out=w1lt[:, :, :],
                        in_=w1_s[:, :, l_ * 512:(l_ + 1) * 512]
                        .transpose([1, 0, 2])))(lg_),
                        reads=["w1_s"], writes=["w1lt"], dma=True, tag=f"w1l{lg_}")

                if "P6" not in phases:
                    w1load(0)
                for lg in range(4):
                    for kc in range(16):
                        if len(pend7) >= 2:
                            pend7.pop(0)()
                        P.add("g", (lambda lg_, kc_: lambda g: g.dma_start(
                            out=gdnsl[:, kc_ % 2, :, :],
                            in_=gdn_s[:, :, kc_ * 128:(kc_ + 1) * 128]
                            .transpose([1, 0, 2])))(lg, kc),
                            reads=["g_s2"], writes=[("gdnsl", kc % 2)], dma=True,
                            tag=f"gd{lg}_{kc}")
                        P.add("g", (lambda lg_, kc_: lambda g: g.dma_start(
                            out=y1sl[:, kc_ % 2, :],
                            in_=y1_s[kc_][:, lg_ * 512:(lg_ + 1) * 512]))(lg, kc),
                            reads=["y1_s"], writes=[("y1sl", kc % 2)], dma=True,
                            tag=f"y1r{lg}_{kc}")
                        def dmm(lg_, kc_):
                            def f(t):
                                sl = psd[:, (kc_ % 2) * 512:(kc_ % 2 + 1) * 512]
                                for i in range(64):
                                    ins = t.matmul(sl, gdnsl[:, kc_ % 2, i, :],
                                                   w1lt[:, i, :],
                                                   start=(i == 0), stop=(i == 63))
                                return ins
                            return f
                        P.add("t", dmm(lg, kc), reads=[("gdnsl", kc % 2), "w1lt"],
                              writes=[("psd", kc % 2)], tag=f"d{lg}_{kc}")
                        def dv(lg_, kc_):
                            def f(v):
                                sl = psd[:, (kc_ % 2) * 512:(kc_ % 2 + 1) * 512]
                                return v.scalar_tensor_tensor(
                                    v2b[:, kc_, :], sl,
                                    cols[:, COL["wsd"] + kc_:COL["wsd"] + kc_ + 1],
                                    y1sl[:, kc_ % 2, :], ALU.mult, ALU.add)
                            return f
                        P.add("v", dv(lg, kc), reads=[("psd", kc % 2), ("y1sl", kc % 2),
                                                      "cols"],
                              writes=[("v2b", kc)], tag=f"dv{lg}_{kc}")
                        P.add("s", (lambda kc_: lambda s: s.activation(
                            v2c[:, 2 * (kc_ % 2), :], v2b[:, kc_, :], AF.Copy))(kc),
                            reads=[("v2b", kc)], writes=[("v2c", 2 * (kc % 2))],
                            tag=f"dc{lg}_{kc}")
                        P.add("s", (lambda kc_: lambda s: s.activation(
                            v2c[:, 2 * (kc_ % 2) + 1, :], v2b[:, kc_, :], AF.Square))(kc),
                            reads=[("v2b", kc)], writes=[("v2c", 2 * (kc % 2) + 1)],
                            tag=f"dq{lg}_{kc}")
                        def dsd(lg_, kc_):
                            def emit():
                                P.add("t", lambda t: t.matmul(
                                    psl12[:, :], ones_cb[:, :], v2c[:, 2 * (kc_ % 2), :],
                                    start=(kc_ == 0), stop=(kc_ == 15)),
                                    reads=[("v2c", 2 * (kc_ % 2)), "ones_cb"],
                                    writes=["psl12"], tag=f"ds1_{lg_}_{kc_}")
                                P.add("t", lambda t: t.matmul(
                                    psl22[:, :], ones_cb[:, :],
                                    v2c[:, 2 * (kc_ % 2) + 1, :],
                                    start=(kc_ == 0), stop=(kc_ == 15)),
                                    reads=[("v2c", 2 * (kc_ % 2) + 1), "ones_cb"],
                                    writes=["psl22"], tag=f"ds2_{lg_}_{kc_}")
                            return emit
                        pend7.append(dsd(lg, kc))
                    if lg < 3:
                        w1load(lg + 1)
                    while pend7:
                        pend7.pop(0)()
                    def ln2row(s):
                        s.activation(mrow2[:, :], psl12[:, :], AF.Copy, scale=1.0 / DM)
                        return s.activation(irow2[:, :], psl22[:, :], AF.Copy, scale=1.0 / DM)
                    P.add("s", ln2row, reads=["psl12", "psl22"],
                          writes=["mrow2", "irow2"], tag=f"l2r{lg}")
                    P.add("v", lambda v: v.scalar_tensor_tensor(
                        srow[:, 0:512], mrow2[:, :], 1.0, mrow2[:, :], ALU.mult, ALU.mult),
                        reads=["mrow2"], writes=["srow"], tag=f"l2va{lg}")
                    P.add("v", lambda v: v.tensor_sub(irow2[:, :], irow2[:, :],
                                                      srow[:, 0:512]),
                          reads=["srow", "irow2"], writes=["irow2"], tag=f"l2vb{lg}")
                    P.add("s", lambda s: s.activation(irow2[:, :], irow2[:, :], AF.Sqrt,
                                                      bias=epst[:, 1:2]),
                          reads=["irow2", "epst"], writes=["irow2"], tag=f"l2s{lg}")
                    P.add("v", lambda v: v.reciprocal(irow2[:, :], irow2[:, :]),
                          reads=["irow2"], writes=["irow2"], tag=f"l2ca{lg}")
                    P.add("v", lambda v: v.tensor_mul(mrow2[:, :], mrow2[:, :],
                                                      irow2[:, :]),
                          reads=["irow2", "mrow2"], writes=["mrow2"], tag=f"l2cb{lg}")
                    P.add("t", lambda t: t.matmul(psr12[:, :], ones_rf[:, :], irow2[:, :],
                                                  start=True, stop=True),
                          reads=["irow2", "ones_rf"], writes=["psr12"], tag=f"l2R1{lg}")
                    P.add("t", lambda t: t.matmul(psr22[:, :], ones_rf[:, :], mrow2[:, :],
                                                  start=True, stop=True),
                          reads=["mrow2", "ones_rf"], writes=["psr22"], tag=f"l2R2{lg}")
                    trpend = []
                    for kc in range(16):
                        P.add("v", (lambda kc_: lambda v: v.tensor_mul(
                            v2b[:, kc_, :], v2b[:, kc_, :], psr12[:, :]))(kc),
                            reads=[("v2b", kc), "psr12"], writes=[("v2b", kc)],
                            tag=f"y2a{lg}_{kc}")
                        P.add("v", (lambda kc_: lambda v: v.tensor_sub(
                            v2b[:, kc_, :], v2b[:, kc_, :], psr22[:, :]))(kc),
                            reads=[("v2b", kc), "psr22"], writes=[("v2b", kc)],
                            tag=f"y2b{lg}_{kc}")
                        def trd(lg_, kc_):
                            def emit():
                                for ls_ in range(4):
                                    sl = (kc_ * 4 + ls_) % 2
                                    P.add("t", (lambda ls2, sl2: lambda t: t.transpose(
                                        pstr[:, sl2 * 512:sl2 * 512 + 128],
                                        v2b[:, kc_, ls2 * 128:(ls2 + 1) * 128],
                                        identf[:, :]))(ls_, sl),
                                        reads=[("v2b", kc_), "identf"],
                                        writes=[("pstr", sl)], tag=f"tr{lg_}_{ls_}_{kc_}")
                                    P.add("v", (lambda ls2, sl2: lambda v: v.tensor_copy(
                                        outb[:, ls2, kc_ * 128:(kc_ + 1) * 128],
                                        pstr[:, sl2 * 512:sl2 * 512 + 128]))(ls_, sl),
                                        reads=[("pstr", sl)], writes=[("outb", ls_, kc_)],
                                        tag=f"te{lg_}_{ls_}_{kc_}")
                            return emit
                        trpend.append(trd(lg, kc))
                        if len(trpend) >= 2:
                            trpend.pop(0)()
                    while trpend:
                        trpend.pop(0)()
                    for ls in range(4):
                        P.add("sy", (lambda lg_, ls_: lambda sy: sy.dma_start(
                            out=out_t[lg_ * 512 + ls_ * 128:lg_ * 512 + (ls_ + 1) * 128, :],
                            in_=outb[:, ls_, :]))(lg, ls),
                            reads=[("outb", ls, kc_) for kc_ in range(16)],
                            writes=["out"], dma=True, tag=f"ost{lg}_{ls}")

        p67.close()
        P.add("sy", None, reads=["out"] + (["qt_s", "kt_s", "v_s", "ao_s", "y1_s",
                                           "w1_s", "g_s0", "g_s1", "g_s2"]
                                          if debug else []), tag="final")

        global LASTPROG
        LASTPROG = P
        runner = P.emit(nc, {k: sem_h[k] for k in sem_h})
        with nc.Block("main") as b:
            b.gpsimd(lambda g: runner(g, "g"))
            b.sync(lambda sy: runner(sy, "sy"))
            b.tensor(lambda t: runner(t, "t"))
            b.vector(lambda v: runner(v, "v"))
            b.scalar(lambda s: runner(s, "s"))
    return nc


def prepack(inputs):
    x = np.asarray(inputs["L0_attn_x"])[0]
    xT = np.ascontiguousarray(x.T)
    d = {}
    wq = np.asarray(inputs["L0_attn_wq"])
    wk = np.asarray(inputs["L0_attn_wk"])
    wv = np.asarray(inputs["L0_attn_wv"])
    wo = np.asarray(inputs["L0_attn_wo"])
    d["wq_p"] = np.ascontiguousarray(
        wq.T.reshape(16, 128, 16, 128).transpose(2, 0, 1, 3))
    d["wk_p"] = np.ascontiguousarray(
        wk.T.reshape(16, 128, 16, 128).transpose(2, 0, 1, 3))
    d["wv_p"] = np.ascontiguousarray(wv.T.reshape(16, 128, DM))
    d["wo_p"] = np.ascontiguousarray(
        wo.T.reshape(16, 128, 16, 128).transpose(2, 0, 1, 3))
    wup = np.asarray(inputs["L0_ffn_W_up"])
    wgt = np.asarray(inputs["L0_ffn_W_gate"])
    wdn = np.asarray(inputs["L0_ffn_W_down"])
    d["fup_p"] = np.ascontiguousarray(
        wup.T.reshape(16, 8, 16, ITR).transpose(2, 0, 1, 3).reshape(16, 128, ITR))
    d["fgt_p"] = np.ascontiguousarray(
        wgt.T.reshape(16, 8, 16, ITR).transpose(2, 0, 1, 3).reshape(16, 128, ITR))
    d["fdn_p"] = np.ascontiguousarray(
        wdn.T.reshape(64, 2, 64, DM).transpose(2, 0, 1, 3).reshape(64, 128, DM))
    colpk = np.zeros((128, NCOLS), np.float32)

    def put(name, vec):
        w = vec.reshape(-1, 128).T
        colpk[:, COL[name]:COL[name] + w.shape[1]] = w
    put("bq", np.asarray(inputs["L0_attn_bq"]))
    put("bk", np.asarray(inputs["L0_attn_bk"]))
    put("gq", np.asarray(inputs["L0_attn_gq"]))
    put("gk", np.asarray(inputs["L0_attn_gk"]))
    put("bo", np.asarray(inputs["L0_attn_bo"]))
    put("su_ug", np.asarray(inputs["L0_ffn_SU_ug"]))
    put("su_dp", np.asarray(inputs["L0_ffn_SU_dp"]))
    put("wsu", np.asarray(inputs["L0_ffn_Wscale_ug"])[:ITR])
    put("wsg", np.asarray(inputs["L0_ffn_Wscale_ug"])[ITR:])
    put("wsd", np.asarray(inputs["L0_ffn_Wscale_dp"]))
    d["colpk"] = colpk
    d["bv_row"] = np.ascontiguousarray(np.asarray(inputs["L0_attn_bv"])[None, :])
    d["wmix_in"] = np.ascontiguousarray(np.asarray(inputs["L0_attn_w_mix"]))
    d.update(make_consts())
    maps = []
    for c in range(NCORE):
        idx = (np.arange(EXTP) + c * TC - HALO_B) % L
        m = dict(d)
        m["xT"] = np.ascontiguousarray(xT[:, idx])
        maps.append(m)
    return maps


LAST = None
LASTPROG = None


def kernel(**inputs):
    global LAST
    maps = prepack(inputs)
    nc = build(debug=DEBUG, phases=PHASES)
    res = run_bass_kernel_spmd(nc, maps, core_ids=list(range(NCORE)))
    LAST = res
    out = np.concatenate([res.results[c]["out"] for c in range(NCORE)], axis=0)
    return out.reshape(1, L, DM)


# version 11 (final)
# speedup vs baseline: 1.5315x; 1.0586x over previous
import contextlib
import numpy as np
import ml_dtypes
import concourse.bass as bass
import concourse.mybir as mybir
from concourse.bass_utils import run_bass_kernel_spmd

F32 = mybir.dt.float32
BF16 = mybir.dt.bfloat16
AF = mybir.ActivationFunctionType
ALU = mybir.AluOpType
AX = mybir.AxisListType
BF = ml_dtypes.bfloat16

L = 16384
DM = 2048
NH = 16
ITR = 8192
NCORE = 8
TC = L // NCORE
HALO_B = 1024
EXTP = 3584
NEC = EXTP // 128
SEQ_SHIFTS = [0, 1, -1, 3, -3, 7, -7, 20, -20, 53, -53, 141, -141, 380, -380, 1024]
OFFS = sorted(-s for s in SEQ_SHIFTS)
COFFS = [-8, -3, -2, -1, 0, 1, 2, 3]
NCO = len(COFFS)
CHAN = [0, 1, 2, 4, 8]
SCALE = float(128) ** -0.5
RMS_EPS = 1e-6
LN_EPS = 1e-5
NLT = TC // 128
RING = 13
NHG = 4
BSHIFT = 40.0

DEBUG = set()
DEFER6 = True
PHASES = ("P1", "P2K", "P2V", "P3", "P4", "P5", "P6", "P7")


NDMASLOT = 8


class _Node:
    __slots__ = ("eng", "sem", "emit", "reads", "writes", "deps", "idx",
                 "marked", "val", "tag", "is_dma", "extra", "waits")

    def __init__(self, eng, sem, emit, reads, writes, idx, tag, is_dma):
        self.eng = eng
        self.sem = sem
        self.emit = emit
        self.reads = tuple(reads)
        self.writes = tuple(writes)
        self.idx = idx
        self.deps = ()
        self.marked = False
        self.val = 0
        self.tag = tag
        self.is_dma = is_dma
        self.extra = ()


class Prog:
    ENGS = ("g", "sy", "t", "v", "s")
    DMA_SEMS = tuple(f"dg{k}" for k in range(NDMASLOT)) + \
        tuple(f"dsy{k}" for k in range(NDMASLOT))
    BAR_SEMS = ("bg", "bsy", "bt", "bv", "bs")

    def __init__(self):
        self.nodes = []
        self._bn = 0
        self._dmacnt = {"dg": 0, "dsy": 0}
        self._dmahist = {"dg": [], "dsy": []}
        self._lasteng = {}

    def add(self, eng, emit, reads=(), writes=(), dma=False, tag="", sem=None):
        reads = tuple(reads)
        if dma:
            q = "dg" if eng == "g" else "dsy"
            sem = f"{q}{self._dmacnt[q] % NDMASLOT}"
            self._dmacnt[q] += 1
            reads = reads + (("qtail", q),)
        elif sem is None:
            sem = eng
        n = _Node(eng, sem, emit, reads, writes, len(self.nodes), tag, dma)
        if dma:
            hist = self._dmahist[q]
            if len(hist) >= NDMASLOT:
                n.extra = (hist[-NDMASLOT],)
            hist.append(n)
        if emit is not None:
            self._lasteng[eng] = n
        self.nodes.append(n)
        self._dirty = True
        return n

    def barrier(self):
        if not getattr(self, "_dirty", False):
            return
        self._dirty = False
        bn = self._bn
        self._bn += 1
        flags = []
        for e in self.ENGS:
            w = [("bflag", bn, e)]
            if e == "g":
                w.append(("qtail", "dg"))
            if e == "sy":
                w.append(("qtail", "dsy"))
            n = self.add(e, None, writes=w, tag=f"bar1_{bn}_{e}", sem=f"b{e}")
            if e in self._lasteng:
                n.extra = n.extra + (self._lasteng[e],)
            flags.append(("bflag", bn, e))
        for e in self.ENGS:
            self.add(e, None, reads=flags, tag=f"bar2_{bn}_{e}", sem=f"b{e}")

    def resolve(self):
        last_w = {}
        readers = {}
        for n in self.nodes:
            deps = set(n.extra)
            for r in n.reads:
                w = last_w.get(r)
                if w is not None:
                    deps.add(w)
            for r in n.writes:
                w = last_w.get(r)
                if w is not None:
                    deps.add(w)
                deps.update(readers.get(r, ()))
            deps.discard(n)
            n.deps = tuple(deps)
            for r in n.reads:
                readers.setdefault(r, set()).add(n)
            for r in n.writes:
                last_w[r] = n
                readers[r] = {n}
        for n in self.nodes:
            if n.emit is not None:
                n.marked = True
            for d in n.deps:
                d.marked = True
        cum = {}
        for n in self.nodes:
            if n.marked:
                inc = 16 if n.is_dma else 1
                cum[n.sem] = cum.get(n.sem, 0) + inc
                n.val = cum[n.sem]
        vc = [None] * len(self.nodes)
        know = {e: {} for e in self.ENGS}
        for n in self.nodes:
            k = know[n.eng]
            waits = []
            for d in sorted(n.deps, key=lambda d_: (d_.sem, d_.val)):
                if k.get(d.sem, 0) >= d.val:
                    continue
                waits.append((d.sem, d.val))
                dv = vc[d.idx]
                if dv:
                    for s_, v_ in dv.items():
                        if k.get(s_, 0) < v_:
                            k[s_] = v_
                if k.get(d.sem, 0) < d.val:
                    k[d.sem] = d.val
            n.waits = waits
            if n.marked:
                k2 = dict(k)
                k2[n.sem] = n.val
                vc[n.idx] = k2
        per_eng = {e: [] for e in self.ENGS}
        for n in self.nodes:
            per_eng[n.eng].append(n)
        ptr = {e: 0 for e in per_eng}
        done = [False] * len(self.nodes)
        progress = True
        ndone = 0
        while progress:
            progress = False
            for e, lst in per_eng.items():
                while ptr[e] < len(lst):
                    n = lst[ptr[e]]
                    if all(done[d.idx] for d in n.deps):
                        done[n.idx] = True
                        ptr[e] += 1
                        ndone += 1
                        progress = True
                    else:
                        break
        if ndone != len(self.nodes):
            stuck = [n for n in self.nodes if not done[n.idx]][:5]
            info = "; ".join(
                f"{n.eng}#{n.idx}:{n.tag} waits " +
                ",".join(f"{d.eng}#{d.idx}:{d.tag}" for d in n.deps
                         if not done[d.idx]) for n in stuck)
            raise RuntimeError(f"deadlock in schedule: {info}")
        return per_eng

    def emit(self, nc, sems):
        per_eng = self.resolve()

        def run_engine(eng_obj, ename):
            for n in per_eng[ename]:
                for s_, v_ in n.waits:
                    eng_obj.wait_ge(sems[s_], v_)
                ins = n.emit(eng_obj) if n.emit is not None else None
                if n.marked:
                    if ins is None:
                        ins = eng_obj.nop()
                    ins.then_inc(sems[n.sem], 16 if n.is_dma else 1)
        return run_engine


def _hadamard(n):
    h = np.array([[1.0]])
    while h.shape[0] < n:
        h = np.block([[h, h], [h, -h]])
    return h


def make_consts():
    c = {}
    c["h128b_in"] = _hadamard(128).astype(BF)
    c["hfoldup_in"] = np.kron(_hadamard(16), np.eye(8)).astype(BF)
    c["hfolddn_in"] = np.kron(_hadamard(64), np.eye(2)).astype(BF)
    pcs = np.zeros((128, 5, 128), np.float32)
    f = np.arange(128)
    for ci, cs in enumerate(CHAN):
        pcs[f, ci, (f - cs) % 128] = 1.0
    c["pcs_in"] = pcs.astype(BF)
    offs = set(OFFS)
    m = np.zeros((128, NCO, 128), np.float32)
    for ci, co in enumerate(COFFS):
        for p in range(128):
            for q in range(128):
                if co * 128 + p - q in offs:
                    m[p, ci, q] = 1.0
    c["masks_in"] = m.astype(BF)
    c["identf_in"] = np.eye(128, dtype=np.float32)
    ind = np.zeros((16, 128, 16), np.float32)
    for m_ in range(16):
        ind[m_, :, m_] = 1.0
    c["indc_in"] = ind.astype(BF)
    c["ones_cb_in"] = np.ones((128, 1), BF)
    c["ones_rf_in"] = np.ones((1, 128), np.float32)
    c["epsv_in"] = np.array([[RMS_EPS, LN_EPS]], np.float32)
    return c


COL = {}
_o = 0
for _n, _w in [("bq", 16), ("bk", 16), ("gq", 16), ("gk", 16), ("bo", 16),
               ("su_ug", 16), ("wsd", 16), ("su_dp", 64), ("wsu", 64), ("wsg", 64)]:
    COL[_n] = _o
    _o += _w
NCOLS = _o


def build(debug=(), phases=PHASES):
    debug = set(debug)
    phases = set(phases)
    nc = bass.Bass("TRN2", target_bir_lowering=False)

    def dram(name, shape, dtype, kind="ExternalInput"):
        return nc.dram_tensor(name, shape, dtype, kind=kind).ap()

    def scr(name, shape, dtype):
        kind = "ExternalOutput" if name in debug else "Internal"
        return nc.dram_tensor(name, shape, dtype, kind=kind).ap()

    xT = dram("xT", [DM, EXTP], F32)
    wq_p = dram("wq_p", [16, 16, 128, 128], F32)
    wk_p = dram("wk_p", [16, 16, 128, 128], F32)
    wv_p = dram("wv_p", [16, 128, DM], F32)
    wo_p = dram("wo_p", [16, 16, 128, 128], F32)
    fup_p = dram("fup_p", [16, 128, ITR], F32)
    fgt_p = dram("fgt_p", [16, 128, ITR], F32)
    fdn_p = dram("fdn_p", [64, 128, DM], F32)
    colpk = dram("colpk", [128, NCOLS], F32)
    bv_row = dram("bv_row", [1, DM], F32)
    wmix_in = dram("wmix_in", [1, 5], F32)
    consts = make_consts()
    CIN = {k: dram(k, list(v.shape), BF16 if v.dtype == BF else F32)
           for k, v in consts.items()}
    out_t = dram("out", [TC, DM], F32, kind="ExternalOutput")

    qt_s = scr("qt_s", [16, 128, TC], BF16)
    kt_s = scr("kt_s", [NEC, 128, NH, 128], BF16)
    v_s = scr("v_s", [NEC, 128, NH, 128], BF16)
    y1_s = scr("y1_s", [16, 128, TC], F32)
    w1_s = scr("w1_s", [64, 128, TC], BF16)
    gup_s = scr("gup_s", [16, 128, ITR], BF16)
    ggt_s = scr("ggt_s", [16, 128, ITR], BF16)
    gdn_s = scr("gdn_s", [64, 128, DM], BF16)
    bb_s = nc.dram_tensor("bb_s", [16], F32, kind="Internal").ap()
    w2k_s = scr("w2k_s", [16, 128, TC], BF16) if "w2k_s" in debug else None
    ao_s = scr("ao_s", [NH, 128, TC], BF16)
    dbg_ao = "ao_s" in debug

    P = Prog()
    xTv = xT.rearrange("(c p) n -> p c n", p=128)

    with contextlib.ExitStack() as ctx:
        SB = lambda n_, s_, d_: ctx.enter_context(nc.sbuf_tensor(n_, s_, d_))
        sem_h = {k: ctx.enter_context(nc.semaphore(k))
                 for k in ("g", "sy", "t", "v", "s") + Prog.DMA_SEMS + Prog.BAR_SEMS}

        cols = SB("cols", [128, NCOLS], F32)
        mixh = SB("mixh", [128, NH, 128], BF16)
        h128b = SB("h128b", [128, 128], BF16)
        Bcols = SB("Bcols", [128, 16], F32)
        maxq2 = SB("maxq2", [16, 1], F32)
        maxk2 = SB("maxk2", [16, 1], F32)
        rtmp16 = SB("rtmp16", [16, 2], F32)
        indcb = SB("indcb", [128, 16, 16], BF16)
        negb = SB("negb", [16, 1], F32)
        ones_cb = SB("ones_cb", [128, 1], BF16)
        ones_rf = SB("ones_rf", [1, 128], F32)
        epst = SB("epst", [1, 2], F32)
        srow = SB("srow", [1, 512], F32)
        srow2 = SB("srow2", [1, 512], F32)
        rtmp = SB("rtmp", [1, 8], F32)

        hfoldup = SB("hfoldup", [128, 128], BF16)
        hfolddn = SB("hfolddn", [128, 128], BF16)
        identf = SB("identf", [128, 128], F32)
        bvrep = SB("bvrep", [128, DM], BF16)
        p0ctx = contextlib.ExitStack()
        wmix = p0ctx.enter_context(nc.sbuf_tensor("wmix", [1, 8], F32))
        pcs_t = p0ctx.enter_context(nc.sbuf_tensor("pcs_t", [128, 5, 128], BF16))
        wrep = p0ctx.enter_context(nc.sbuf_tensor("wrep", [128, 8], F32))
        mbase = p0ctx.enter_context(nc.sbuf_tensor("mbase", [128, 128], BF16))
        bvrow_t = p0ctx.enter_context(nc.sbuf_tensor("bvrow_t", [1, DM], F32))
        ps_gen = p0ctx.enter_context(nc.psum_tensor("ps_gen", [128, 1024], F32))

        for dst, src, rs in [
                (cols[:, :], colpk, "cols"), (wmix[:, 0:5], wmix_in, "wmix"),
                (pcs_t[:, :, :], CIN["pcs_in"], "pcs"),
                (ones_cb[:, :], CIN["ones_cb_in"], "ones_cb"),
                (ones_rf[:, :], CIN["ones_rf_in"], "ones_rf"),
                (epst[:, :], CIN["epsv_in"], "epst"),
                (h128b[:, :], CIN["h128b_in"], "h128b"),
                (hfoldup[:, :], CIN["hfoldup_in"], "hfoldup"),
                (hfolddn[:, :], CIN["hfolddn_in"], "hfolddn"),
                (identf[:, :], CIN["identf_in"], "identf"),
                (indcb[:, :, :], CIN["indc_in"].transpose([1, 0, 2]), "indcb"),
                (bvrow_t[:, :], bv_row, "bvrow")]:
            P.add("g", (lambda d_, s_: lambda g: g.dma_start(out=d_, in_=s_))(dst, src),
                  writes=[rs], dma=True, tag=f"ld_{rs}")
        P.add("s", lambda s: s.mul(wmix[:, 0:5], wmix[:, 0:5], SCALE),
              reads=["wmix"], writes=["wmix"], tag="wmix_scale")
        P.add("t", lambda t: t.matmul(ps_gen[:, 0:5], ones_rf[:, :], wmix[:, 0:5],
                                      start=True, stop=True),
              reads=["wmix", "ones_rf"], writes=["psg0"], tag="wrep_mm")
        P.add("t", lambda t: t.matmul(ps_gen[:, 512:1024], ones_rf[:, :],
                                      bvrow_t[:, 0:512], start=True, stop=True),
              reads=["bvrow", "ones_rf"], writes=["psg1"], tag="bvrep_mm0")

        P.add("v", lambda v: v.tensor_copy(wrep[:, 0:5], ps_gen[:, 0:5]),
              reads=["psg0"], writes=["wrep"], tag="wrep_cp")
        P.add("v", lambda v: v.tensor_scalar(mbase[:, :], pcs_t[:, 0, :], wrep[:, 0:1],
                                             None, ALU.mult),
              reads=["wrep", "pcs"], writes=["mbase"], tag="mb0")
        for ci in range(1, 5):
            P.add("v", (lambda ci_: lambda v: v.scalar_tensor_tensor(
                mbase[:, :], pcs_t[:, ci_, :], wrep[:, ci_:ci_ + 1], mbase[:, :],
                ALU.mult, ALU.add))(ci),
                reads=["wrep", "pcs", "mbase"], writes=["mbase"], tag=f"mb{ci}")
        for h in range(NH):
            P.add("v", (lambda h_: lambda v: v.tensor_scalar(
                mixh[:, h_, :], mbase[:, :],
                cols[:, COL["gk"] + h_:COL["gk"] + h_ + 1], None, ALU.mult))(h),
                reads=["mbase", "cols"], writes=["mixh"], tag=f"mixh{h}")
        P.add("v", lambda v: v.memset(maxq2[:, :], 0.0), writes=["maxq2"], tag="z_mq")
        P.add("v", lambda v: v.memset(maxk2[:, :], 0.0), writes=["maxk2"], tag="z_mk")
        for q in range(4):
            if q > 0:
                P.add("t", (lambda q_: lambda t: t.matmul(
                    ps_gen[:, 512:1024], ones_rf[:, :], bvrow_t[:, q_ * 512:(q_ + 1) * 512],
                    start=True, stop=True))(q),
                    reads=["bvrow"], writes=["psg1"], tag=f"bvrep_mm{q}")
            P.add("v", (lambda q_: lambda v: v.tensor_copy(
                bvrep[:, q_ * 512:(q_ + 1) * 512], ps_gen[:, 512:1024]))(q),
                reads=["psg1"], writes=["bvrep"], tag=f"bvrep_cp{q}")
        P.barrier()
        p0ctx.close()

        FOLDS = [(0, fup_p, gup_s, hfoldup, 16, ITR),
                 (1, fgt_p, ggt_s, hfoldup, 16, ITR),
                 (2, fdn_p, gdn_s, hfolddn, 64, DM)]

        def make_fold_emitter(PSB_, PPS_, sfx, psf_pool=None):
            gtile = PSB_(f"gt{sfx}", [128, 4096], BF16)
            frhs = PSB_(f"fr{sfx}", [128, 4, 512], BF16)
            psf = psf_pool if psf_pool is not None else PPS_(f"pf{sfx}", [128, 1024])

            def emit_bl(fw, bl, jb0=0):
                wi, src_t, dst_t, lhs, nbl, nj = fw
                njb = nj // 512
                for jb in range(jb0, min(jb0 + 4, njb)):
                    P.add("g", (lambda s_, bl_, jb_: lambda g: g.dma_start(
                        out=frhs[:, jb_ % 4, :],
                        in_=s_[bl_][:, jb_ * 512:(jb_ + 1) * 512]))(src_t, bl, jb),
                        writes=[("frhs", sfx, jb % 4)], dma=True, tag=f"f{wi}_{bl}_{jb}")
                    P.add("t", (lambda l_, jb_: lambda t: t.matmul(
                        psf[:, (jb_ % 2) * 512:(jb_ % 2 + 1) * 512],
                        l_, frhs[:, jb_ % 4, :],
                        start=True, stop=True))(lhs[:, :], jb),
                        reads=[("frhs", sfx, jb % 4), "hfoldup", "hfolddn"],
                        writes=[("aux", jb % 2)], tag=f"fm{wi}_{bl}_{jb}")
                    if jb % 2 == 0:
                        P.add("s", (lambda jb_: lambda s: s.activation(
                            gtile[:, (jb_ % 8) * 512:(jb_ % 8 + 1) * 512],
                            psf[:, (jb_ % 2) * 512:(jb_ % 2 + 1) * 512], AF.Copy))(jb),
                            reads=[("aux", jb % 2)], writes=[("gtile", sfx, jb % 8)],
                            tag=f"fe{wi}_{bl}_{jb}")
                    else:
                        P.add("v", (lambda jb_: lambda v: v.tensor_copy(
                            gtile[:, (jb_ % 8) * 512:(jb_ % 8 + 1) * 512],
                            psf[:, (jb_ % 2) * 512:(jb_ % 2 + 1) * 512]))(jb),
                            reads=[("aux", jb % 2)], writes=[("gtile", sfx, jb % 8)],
                            tag=f"fe{wi}_{bl}_{jb}")
                    if (jb + 1) % 4 == 0 or jb == njb - 1:
                        j0 = (jb // 4) * 4
                        nsl = jb - j0 + 1
                        def gsph(d_, bl_, nbl_, j0_, nsl_):
                            def f(sy):
                                dst = d_.rearrange("a (r bl) j -> (a r) bl j",
                                                   bl=nbl_)[:, bl_,
                                                            j0_ * 512:(j0_ + nsl_) * 512]
                                g0 = (j0_ % 8) * 512
                                return sy.dma_start(out=dst,
                                                    in_=gtile[:, g0:g0 + nsl_ * 512])
                            return f
                        P.add("sy", gsph(dst_t, bl, nbl, j0, nsl),
                              reads=[("gtile", sfx, (j0 % 8) + j_) for j_ in range(nsl)],
                              writes=[f"g_s{wi}"], dma=True, tag=f"gsp{wi}_{bl}_{j0}")

            return emit_bl


        if {"P1", "P2K", "P2V"} & phases:
            with contextlib.ExitStack() as px:
                PSB = lambda n_, s_, d_: px.enter_context(nc.sbuf_tensor(n_, s_, d_))
                PPS = lambda n_, s_: px.enter_context(nc.psum_tensor(n_, s_, F32))
                wpool = PSB("wpool", [128, 16, 16, 128], BF16)
                xpool = PSB("xpool", [128, 2, 16, 512], BF16)
                rawp = PSB("rawp", [128, 16, 512], BF16)
                pool2 = PSB("pool2", [128, 2, 16, 512], BF16)
                pool3 = PSB("pool3", [128, 2, 16, 512], BF16)
                sqb = PSB("sqb", [128, 2, 512], BF16)
                pbig = PPS("pbig", [128, 2048])
                pstat = PPS("pstat", [1, 512])
                pnq = PPS("pnq", [16, 512])
                paux = PPS("paux", [128, 1024])
                prep = paux[:, 0:512]
                fold_em = make_fold_emitter(PSB, PPS, "u", psf_pool=paux) if "P5" in phases else None
                fold_work = ([(FOLDS[0], bl, j0) for bl in range(16) for j0 in (0, 4, 8, 12)] +
                             [(FOLDS[1], bl, j0) for bl in range(16) for j0 in (0, 4, 8, 12)] +
                             [(FOLDS[2], bl, 0) for bl in range(64)]) if fold_em else []

                def folds(n):
                    for _ in range(n):
                        if fold_work:
                            fold_em(*fold_work.pop(0))

                tdefer = []

                def tpop(k=1):
                    for _ in range(min(k, len(tdefer))):
                        tdefer.pop(0)()

                def xload(tt, tag):
                    c0 = (HALO_B if tag == "x" else 0) + tt * 512
                    P.add("g", (lambda tt_, c_: lambda g: g.dma_start(
                        out=xpool[:, tt_ % 2, :, :], in_=xTv[:, :, c_:c_ + 512]))(tt, c0),
                        writes=[("xp", tt % 2)], dma=True, tag=f"{tag}{tt}")

                if "P1" in phases:
                    xload(0, "x")
                    xload(1, "x")
                    for m in range(16):
                        P.add("g", (lambda m_: lambda g: g.dma_start(
                            out=wpool[:, m_, :, :], in_=wq_p[m_].transpose([1, 0, 2])))(m),
                            writes=[("wp", m)], dma=True, tag=f"wq{m}")
                    for tt in range(4):
                        if tt >= 2:
                            xload(tt, "x")
                        for m in range(16):
                            def qmm(tt_, m_):
                                def f(t):
                                    sl = pbig[:, (m_ % 3) * 512:(m_ % 3 + 1) * 512]
                                    for k in range(16):
                                        ins = t.matmul(sl, wpool[:, m_, k, :],
                                                       xpool[:, tt_ % 2, k, :],
                                                       start=(k == 0), stop=(k == 15))
                                    return ins
                                return f
                            P.add("t", qmm(tt, m),
                                  reads=[("wp", m), ("xp", tt % 2)],
                                  writes=[("pb", m % 3)], tag=f"q{tt}_{m}")
                            tpop(1)
                            def qev(tt_, m_):
                                def f(s):
                                    sl = pbig[:, (m_ % 3) * 512:(m_ % 3 + 1) * 512]
                                    bcol = cols[:, COL["bq"] + m_:COL["bq"] + m_ + 1]
                                    s.activation(rawp[:, m_, :], sl, AF.Identity, bias=bcol)
                                    return s.activation(sqb[:, m_ % 2, :], sl, AF.Square,
                                                        bias=bcol)
                                return f
                            P.add("s", qev(tt, m), reads=[("pb", m % 3), "cols"],
                                  writes=[("raw", m), ("sqb", m % 2)], tag=f"qev{tt}_{m}")
                            tdefer.append((lambda m_: lambda: P.add(
                                "t", lambda t: t.matmul(
                                    pstat[:, :], ones_cb[:, :], sqb[:, m_ % 2, :],
                                    start=(m_ == 0), stop=(m_ == 15)),
                                reads=[("sqb", m_ % 2), "ones_cb"], writes=["pstat"],
                                tag=f"qssq{tt}_{m_}"))(m))
                            if m % 2 == 1:
                                folds(1)
                        tpop(16)
                        P.add("s", lambda s: s.activation(srow[:, :], pstat[:, :], AF.Sqrt,
                                                          bias=epst[:, 0:1], scale=1.0 / DM),
                              reads=["pstat", "epst"], writes=["srow"], tag=f"sq_sqrt{tt}")
                        P.add("v", lambda v: v.reciprocal(srow[:, :], srow[:, :]),
                              reads=["srow"], writes=["srow"], tag=f"sq_rcp{tt}")
                        P.add("t", lambda t: t.matmul(prep[:, :], ones_rf[:, :], srow[:, :],
                                                      start=True, stop=True),
                              reads=["srow", "ones_rf"], writes=[("aux", 0)],
                              tag=f"sq_rep{tt}")
                        for m in range(16):
                            P.add("v", (lambda tt_, m_: lambda v: v.scalar_tensor_tensor(
                                pool2[:, tt_ % 2, m_, :], rawp[:, m_, :],
                                cols[:, COL["gq"] + m_:COL["gq"] + m_ + 1], prep[:, :],
                                ALU.mult, ALU.mult))(tt, m),
                                reads=[("raw", m), ("aux", 0), "cols"],
                                writes=[("p2", tt % 2, m)], tag=f"qt{tt}_{m}")
                            P.add("s", (lambda tt_, m_: lambda s: s.activation(
                                sqb[:, m_ % 2, :], pool2[:, tt_ % 2, m_, :],
                                AF.Square))(tt, m),
                                reads=[("p2", tt % 2, m)], writes=[("sqb", m % 2)],
                                tag=f"nq_sq{tt}_{m}")
                            if m >= 1:
                                P.add("t", (lambda m_: lambda t: t.matmul(
                                    pnq[:, :], indcb[:, m_, :], sqb[:, m_ % 2, :],
                                    start=(m_ == 0), stop=(m_ == 15)))(m - 1),
                                    reads=[("sqb", (m - 1) % 2), "indcb"], writes=["pnq"],
                                    tag=f"nq_mm{tt}_{m - 1}")
                        P.add("t", (lambda m_: lambda t: t.matmul(
                            pnq[:, :], indcb[:, m_, :], sqb[:, m_ % 2, :],
                            start=(m_ == 0), stop=(m_ == 15)))(15),
                            reads=[("sqb", 15 % 2), "indcb"], writes=["pnq"],
                            tag=f"nq_mm{tt}_15")
                        P.add("v", lambda v: v.reduce_max(rtmp16[:, 0:1], pnq[:, :], AX.X),
                              reads=["pnq"], writes=["rt16q"], tag=f"nq_rm{tt}")
                        P.add("v", lambda v: v.tensor_max(maxq2[:, :], maxq2[:, :],
                                                          rtmp16[:, 0:1]),
                              reads=["rt16q", "maxq2"], writes=["maxq2"], tag=f"nq_mx{tt}")
                        P.add("sy", (lambda tt_: lambda sy: sy.dma_start(
                            out=qt_s[:, :, tt_ * 512:(tt_ + 1) * 512].transpose([1, 0, 2]),
                            in_=pool2[:, tt_ % 2, :, :]))(tt),
                            reads=[("p2", tt % 2, m) for m in range(16)],
                            writes=["qt_s"], dma=True, tag=f"qspill{tt}")

                if "P2K" in phases:
                    xload(0, "kx")
                    xload(1, "kx")
                    for m in range(16):
                        P.add("g", (lambda m_: lambda g: g.dma_start(
                            out=wpool[:, m_, :, :], in_=wk_p[m_].transpose([1, 0, 2])))(m),
                            writes=[("wp", m)], dma=True, tag=f"wk{m}")
                    for tt in range(7):
                        if tt >= 2:
                            xload(tt, "kx")
                        for m in range(16):
                            def kmm(tt_, m_):
                                def f(t):
                                    sl = pbig[:, (m_ % 2) * 512:(m_ % 2 + 1) * 512]
                                    for k in range(16):
                                        ins = t.matmul(sl, wpool[:, m_, k, :],
                                                       xpool[:, tt_ % 2, k, :],
                                                       start=(k == 0), stop=(k == 15))
                                    return ins
                                return f
                            P.add("t", kmm(tt, m), reads=[("wp", m), ("xp", tt % 2)],
                                  writes=[("pb", m % 2)], tag=f"k{tt}_{m}")
                            tpop(1)
                            def kev(tt_, m_):
                                def f(s):
                                    sl = pbig[:, (m_ % 2) * 512:(m_ % 2 + 1) * 512]
                                    bcol = cols[:, COL["bk"] + m_:COL["bk"] + m_ + 1]
                                    s.activation(rawp[:, m_, :], sl, AF.Identity, bias=bcol)
                                    return s.activation(sqb[:, m_ % 2, :], sl, AF.Square,
                                                        bias=bcol)
                                return f
                            P.add("s", kev(tt, m), reads=[("pb", m % 2), "cols"],
                                  writes=[("raw", m), ("sqb", m % 2)], tag=f"kev{tt}_{m}")
                            tdefer.append((lambda m_: lambda: P.add(
                                "t", lambda t: t.matmul(
                                    pstat[:, :], ones_cb[:, :], sqb[:, m_ % 2, :],
                                    start=(m_ == 0), stop=(m_ == 15)),
                                reads=[("sqb", m_ % 2), "ones_cb"], writes=["pstat"],
                                tag=f"kssq{tt}_{m_}"))(m))
                            if m % 2 == 1:
                                folds(1)
                        tpop(16)
                        P.add("s", lambda s: s.activation(srow2[:, :], pstat[:, :], AF.Sqrt,
                                                          bias=epst[:, 0:1], scale=1.0 / DM),
                              reads=["pstat", "epst"], writes=["srow2"], tag=f"sk_sqrt{tt}")
                        P.add("v", lambda v: v.reciprocal(srow2[:, :], srow2[:, :]),
                              reads=["srow2"], writes=["srow2"], tag=f"sk_rcp{tt}")
                        P.add("t", lambda t: t.matmul(prep[:, :], ones_rf[:, :], srow2[:, :],
                                                      start=True, stop=True),
                              reads=["srow2", "ones_rf"], writes=[("aux", 0)],
                              tag=f"sk_rep{tt}")
                        def kb_n(tt_, m_):
                            P.add("v", lambda v: v.tensor_mul(
                                pool2[:, tt_ % 2, m_, :], rawp[:, m_, :], prep[:, :]),
                                reads=[("raw", m_), ("aux", 0)],
                                writes=[("p2", tt_ % 2, m_)], tag=f"kb{tt_}_{m_}")

                        def ktm_n(tt_, m_):
                            def f(t):
                                sl = pbig[:, 1024 + (m_ % 2) * 512:1024 + (m_ % 2 + 1) * 512]
                                return t.matmul(sl, mixh[:, m_, :], pool2[:, tt_ % 2, m_, :],
                                                start=True, stop=True)
                            P.add("t", f, reads=[("p2", tt_ % 2, m_), "mixh"],
                                  writes=[("pb", 2 + m_ % 2)], tag=f"ktm{tt_}_{m_}")

                        def ktev_n(tt_, m_):
                            def f(s):
                                sl = pbig[:, 1024 + (m_ % 2) * 512:1024 + (m_ % 2 + 1) * 512]
                                s.activation(pool3[:, tt_ % 2, m_, :], sl, AF.Copy)
                                return s.activation(sqb[:, m_ % 2, :], sl, AF.Square)
                            P.add("s", f, reads=[("pb", 2 + m_ % 2)],
                                  writes=[("p3", tt_ % 2, m_), ("sqb", m_ % 2)],
                                  tag=f"ktev{tt_}_{m_}")

                        def nk_n(tt_, m_):
                            P.add("t", lambda t: t.matmul(
                                pnq[:, :], indcb[:, m_, :], sqb[:, m_ % 2, :],
                                start=(m_ == 0), stop=(m_ == 15)),
                                reads=[("sqb", m_ % 2), "indcb"], writes=["pnq"],
                                tag=f"nk_mm{tt_}_{m_}")

                        for m in range(16):
                            kb_n(tt, m)
                            if m >= 1:
                                ktm_n(tt, m - 1)
                                ktev_n(tt, m - 1)
                            if m >= 2:
                                nk_n(tt, m - 2)
                        ktm_n(tt, 15)
                        ktev_n(tt, 15)
                        nk_n(tt, 14)
                        nk_n(tt, 15)
                        P.add("v", lambda v: v.reduce_max(rtmp16[:, 1:2], pnq[:, :], AX.X),
                              reads=["pnq"], writes=["rt16k"], tag=f"nk_rm{tt}")
                        P.add("v", lambda v: v.tensor_max(maxk2[:, :], maxk2[:, :],
                                                          rtmp16[:, 1:2]),
                              reads=["rt16k", "maxk2"], writes=["maxk2"], tag=f"nk_mx{tt}")
                        for ei in range(4):
                            P.add("sy", (lambda tt_, ei_: lambda sy: sy.dma_start(
                                out=kt_s[tt_ * 4 + ei_],
                                in_=pool3[:, tt_ % 2, :, ei_ * 128:(ei_ + 1) * 128]))(tt, ei),
                                reads=[("p3", tt % 2, m) for m in range(16)],
                                writes=["kt_s"], dma=True, tag=f"kspill{tt}_{ei}")

                if "P2V" in phases:
                    xload(0, "vx")
                    xload(1, "vx")
                    for k in range(16):
                        P.add("g", (lambda k_: lambda g: g.dma_start(
                            out=wpool[:, k_, :, :],
                            in_=wv_p[k_].rearrange("p (a b) -> p a b", a=16)))(k),
                            writes=[("wp", k)], dma=True, tag=f"wv{k}")
                    for tt in range(7):
                        if tt >= 2:
                            xload(tt, "vx")
                        for ns in range(4):
                            for jb in range(4):
                                def vmm(tt_, ns_, jb_):
                                    def f(t):
                                        sl = pbig[:, jb_ * 512:(jb_ + 1) * 512]
                                        for k in range(16):
                                            ins = t.matmul(
                                                sl,
                                                xpool[:, tt_ % 2, k,
                                                      ns_ * 128:(ns_ + 1) * 128],
                                                wpool[:, k, :, :].rearrange(
                                                    "p a b -> p (a b)")[:, jb_ * 512:
                                                                        (jb_ + 1) * 512],
                                                start=(k == 0), stop=(k == 15))
                                        return ins
                                    return f
                                P.add("t", vmm(tt, ns, jb),
                                      reads=[("xp", tt % 2)] +
                                            [("wp", k) for k in range(16)],
                                      writes=[("pb", jb)], tag=f"v{tt}_{ns}_{jb}")
                                def vev(tt_, ns_, jb_):
                                    def f(v):
                                        dst = pool3[:, tt_ % 2, :, :].rearrange(
                                            "p a b -> p (a b)")[:, ns_ * 2048 + jb_ * 512:
                                                                ns_ * 2048 + (jb_ + 1) * 512]
                                        return v.tensor_add(
                                            dst, pbig[:, jb_ * 512:(jb_ + 1) * 512],
                                            bvrep[:, jb_ * 512:(jb_ + 1) * 512])
                                    return f
                                P.add("v", vev(tt, ns, jb),
                                      reads=[("pb", jb), "bvrep"],
                                      writes=[("p3", tt % 2, 4 * ns + jb)],
                                      tag=f"vev{tt}_{ns}_{jb}")
                                folds(1)
                        for ei in range(4):
                            def vsp(tt_, ei_):
                                def f(sy):
                                    srcv = pool3[:, tt_ % 2, :, :].rearrange(
                                        "p a b -> p (a b)")[:, ei_ * 2048:(ei_ + 1) * 2048]
                                    return sy.dma_start(
                                        out=v_s[tt_ * 4 + ei_],
                                        in_=srcv.rearrange("p (h d) -> p h d", h=NH))
                                return f
                            P.add("sy", vsp(tt, ei),
                                  reads=[("p3", tt % 2, 4 * ei + j) for j in range(4)],
                                  writes=["v_s"], dma=True, tag=f"vspill{tt}_{ei}")
                while fold_work:
                    fold_em(*fold_work.pop(0))

        P.barrier()
        if "P3" in phases:
            with contextlib.ExitStack() as aoctx:
                attnout = aoctx.enter_context(
                    nc.sbuf_tensor("attnout", [128, NH, NLT, 128], BF16))
                wot = aoctx.enter_context(
                    nc.sbuf_tensor("wot", [128, 16, 16, 128], BF16))
                masks = aoctx.enter_context(
                    nc.sbuf_tensor("masks", [128, NCO, 128], BF16))
                P.add("g", lambda g: g.dma_start(out=masks[:, :, :], in_=CIN["masks_in"]),
                      writes=["masks"], dma=True, tag="ld_masks")
                p3ctx = contextlib.ExitStack()
                P3SB = lambda n_, s_, d_: p3ctx.enter_context(nc.sbuf_tensor(n_, s_, d_))
                P3PS = lambda n_, s_: p3ctx.enter_context(nc.psum_tensor(n_, s_, F32))
                ringk = P3SB("ringk", [128, RING, NHG, 128], BF16)
                ringv = P3SB("ringv", [128, RING, NHG, 128], BF16)
                qtb = P3SB("qtb", [128, 2, NHG, 128], BF16)
                expb = P3SB("expb", [128, 2, NCO, 128], BF16)
                pband = P3PS("pband", [128, 2048])
                patt = P3PS("patt", [128, 1024])
                pden = P3PS("pden", [1, 512])
                prep3 = P3PS("prep3", [128, 512])
                recrow = P3SB("recrow", [1, 512], F32)
                reprow = P3SB("reprow", [128, 512], F32)

                P.add("v", lambda v: v.tensor_mul(negb[:, :], maxq2[:, :], maxk2[:, :]),
                      reads=["maxq2", "maxk2"], writes=["negb"], tag="b_mul")
                P.add("s", lambda s: s.activation(negb[:, :], negb[:, :], AF.Sqrt),
                      reads=["negb"], writes=["negb"], tag="b_sqrt")
                P.add("v", lambda v: v.tensor_scalar(negb[:, :], negb[:, :], -1.0,
                                                     BSHIFT, ALU.mult, ALU.add),
                      reads=["negb"], writes=["negb"], tag="b_neg")
                P.add("sy", lambda sy: sy.dma_start(out=bb_s, in_=negb[:, 0]),
                      reads=["negb"], writes=["bb_s"], dma=True, tag="b_dma1")
                P.add("sy", lambda sy: sy.dma_start(out=srow[0:1, 0:16], in_=bb_s[None, :]),
                      reads=["bb_s"], writes=["srow"], dma=True, tag="b_dma2")
                P.add("t", lambda t: t.matmul(prep3[:, 0:16], ones_rf[:, :], srow[:, 0:16],
                                              start=True, stop=True),
                      reads=["srow", "ones_rf"], writes=["prep3"], tag="b_rep")
                P.add("v", lambda v: v.tensor_copy(Bcols[:, :], prep3[:, 0:16]),
                      reads=["prep3"], writes=["Bcols"], tag="b_cp")

                p3tail = []
                for hg in range(NH // NHG):
                    h0 = hg * NHG
                    loaded = set()
                    while p3tail:
                        p3tail.pop(0)()
                    for kcw in (hg * 2, hg * 2 + 1) if "P4" in phases else ():
                        P.add("g", (lambda kc_: lambda g: g.dma_start(
                            out=wot[:, kc_, :, :],
                            in_=wo_p[kc_].transpose([1, 0, 2])))(kcw),
                            writes=[("wot", kcw)], dma=True, tag=f"wol{kcw}")
                    for kcw in (8 + hg * 2, 8 + hg * 2 + 1) if "P4" in phases else ():
                        P.add("g", (lambda kc_: lambda g: g.dma_start(
                            out=wot[:, kc_, :, :],
                            in_=wo_p[kc_].transpose([1, 0, 2])))(kcw),
                            writes=[("wot", kcw)], dma=True, tag=f"wol{kcw}")

                    def ring_load(e, h0_=h0, hg_=hg):
                        P.add("sy", (lambda e_: lambda sy: sy.dma_start(
                            out=ringk[:, e_ % RING, :, :],
                            in_=kt_s[e_, :, h0_:h0_ + NHG, :]))(e),
                            reads=["kt_s"], writes=[("ringk", e % RING)],
                            dma=True, tag=f"rk{hg_}_{e}")
                        P.add("g", (lambda e_: lambda g: g.dma_start(
                            out=ringv[:, e_ % RING, :, :],
                            in_=v_s[e_, :, h0_:h0_ + NHG, :]))(e),
                            reads=["v_s"], writes=[("ringv", e % RING)],
                            dma=True, tag=f"rv{hg_}_{e}")
                        loaded.add(e)

                    for e in [0, 5, 6, 7, 8, 9, 10, 11]:
                        ring_load(e)
                    for lt in range(NLT):
                        for e in (lt, lt + 11):
                            if e not in loaded and 0 <= e < NEC:
                                ring_load(e)
                        P.add("sy", (lambda h0_, lt_: lambda sy: sy.dma_start(
                            out=qtb[:, lt_ % 2, :, :],
                            in_=qt_s[h0_:h0_ + NHG, :, lt_ * 128:(lt_ + 1) * 128]
                            .transpose([1, 0, 2])))(h0, lt),
                            reads=["qt_s"], writes=[("qtb", lt % 2)], dma=True,
                            tag=f"q3{hg}_{lt}")
                        def smm_n(hh, uf, lt_, ecs):
                            def f(t):
                                for ci, e in enumerate(ecs):
                                    sl = pband[:, uf * 1024 + ci * 128:
                                               uf * 1024 + (ci + 1) * 128]
                                    ins = t.matmul(sl, ringk[:, e % RING, hh, :],
                                                   qtb[:, lt_ % 2, hh, :],
                                                   start=True, stop=True)
                                return ins
                            P.add("t", f,
                                  reads=[("ringk", e % RING) for e in ecs] +
                                        [("qtb", lt_ % 2)],
                                  writes=[("pband", uf)], tag=f"sc{hg}_{lt_}_{hh}")

                        def ex_n(hh, uf, lt_):
                            P.add("s", lambda s: s.activation(
                                expb[:, uf, :, :], pband[:, uf * 1024:(uf + 1) * 1024],
                                AF.Exp, bias=Bcols[:, h0 + hh:h0 + hh + 1]),
                                reads=[("pband", uf), "Bcols"], writes=[("expb", uf)],
                                tag=f"ex{hg}_{lt_}_{hh}")

                        def mk_n(hh, uf, lt_):
                            P.add("v", lambda v: v.tensor_mul(
                                expb[:, uf, :, :], expb[:, uf, :, :], masks[:, :, :]),
                                reads=[("expb", uf), "masks"], writes=[("expb", uf)],
                                tag=f"mk{hg}_{lt_}_{hh}")

                        def av_n(hh, uf, lt_, ecs):
                            def f(t):
                                po = (lt_ % 2) * 512
                                for ci, e in enumerate(ecs):
                                    ins = t.matmul(
                                        patt[:, po + hh * 128:po + (hh + 1) * 128],
                                        ringv[:, e % RING, hh, :], expb[:, uf, ci, :],
                                        start=(ci == 0), stop=(ci == 7))
                                return ins
                            P.add("t", f,
                                  reads=[("ringv", e % RING) for e in ecs] +
                                        [("expb", uf)],
                                  writes=[("patt", lt_ % 2, hh)], tag=f"av{hg}_{lt_}_{hh}")

                        def dn_n(hh, uf, lt_):
                            def f(t):
                                for ci in range(NCO):
                                    ins = t.matmul(
                                        pden[:, hh * 128:(hh + 1) * 128],
                                        ones_cb[:, :], expb[:, uf, ci, :],
                                        start=(ci == 0), stop=(ci == 7))
                                return ins
                            P.add("t", f, reads=[("expb", uf), "ones_cb"],
                                  writes=[("pden", hh)], tag=f"dn{hg}_{lt_}_{hh}")

                        def mixtail(hg_, lt_):
                            def emit():
                                P.add("v", lambda v: v.reciprocal(recrow[:, :], pden[:, :]),
                                      reads=[("pden", 0), ("pden", 1), ("pden", 2),
                                             ("pden", 3)],
                                      writes=["recrow"], tag=f"rc{hg_}_{lt_}")
                                P.add("t", lambda t: t.matmul(
                                    prep3[:, :], ones_rf[:, :], recrow[:, :],
                                    start=True, stop=True),
                                    reads=["recrow", "ones_rf"], writes=["prep3"],
                                    tag=f"rcrep{hg_}_{lt_}")
                                P.add("v", lambda v: v.tensor_copy(reprow[:, :],
                                                                   prep3[:, :]),
                                      reads=["prep3"], writes=["reprow"],
                                      tag=f"rpc{hg_}_{lt_}")
                                def f(v):
                                    po = (lt_ % 2) * 512
                                    return v.tensor_mul(
                                        attnout[:, hg_ * NHG:(hg_ + 1) * NHG, lt_, :],
                                        patt[:, po:po + 512].rearrange(
                                            "p (h l) -> p h l", h=NHG),
                                        reprow[:, :].rearrange("p (h l) -> p h l", h=NHG))
                                P.add("v", f,
                                      reads=[("patt", lt_ % 2, hh_) for hh_ in range(NHG)] +
                                            ["reprow"],
                                      writes=[("attnout", hg_, lt_)], tag=f"ao{hg_}_{lt_}")
                            return emit

                        ecs = [lt + 8 + co for co in COFFS]
                        for hh in range(NHG):
                            uf = hh % 2
                            smm_n(hh, uf, lt, ecs)
                            if hh == 1 and p3tail:
                                p3tail.pop(0)()
                            ex_n(hh, uf, lt)
                            mk_n(hh, uf, lt)
                            if hh >= 1:
                                av_n(hh - 1, (hh - 1) % 2, lt, ecs)
                                dn_n(hh - 1, (hh - 1) % 2, lt)
                        av_n(3, 1, lt, ecs)
                        dn_n(3, 1, lt)
                        p3tail.append(mixtail(hg, lt))
                while p3tail:
                    p3tail.pop(0)()
                if dbg_ao:
                    P.add("sy", lambda sy: sy.dma_start(
                        out=ao_s[:, :, :].transpose([1, 0, 2]),
                        in_=attnout[:, :, :, :].rearrange("p h lt l -> p h (lt l)")),
                        reads=[("attnout", hg_, lt_) for hg_ in range(4) for lt_ in range(NLT)],
                        writes=["ao_s"], dma=True, tag="ao_dump")

                P.barrier()
                p3ctx.close()
                if "P4" in phases:
                    with contextlib.ExitStack() as px:
                        PSB = lambda n_, s_, d_: px.enter_context(nc.sbuf_tensor(n_, s_, d_))
                        PPS = lambda n_, s_: px.enter_context(nc.psum_tensor(n_, s_, F32))
                        v1b = PSB("v1b", [128, 16, 512], F32)
                        xres = PSB("xres", [128, 4, 512], F32)
                        v1c = PSB("v1c", [128, 4, 512], BF16)
                        pso = PPS("pso", [128, 1024])
                        psl1 = PPS("psl1", [1, 512])
                        psl2 = PPS("psl2", [1, 512])
                        psr1 = PPS("psr1", [128, 512])
                        psr2 = PPS("psr2", [128, 512])
                        mrow = PSB("mrow", [1, 512], F32)
                        irow = PSB("irow", [1, 512], F32)

                        pend4 = []
                        for sgi in range(4):
                            for kc in range(16):
                                if len(pend4) >= 2:
                                    pend4.pop(0)()
                                P.add("g", (lambda sgi_, kc_: lambda g: g.dma_start(
                                    out=xres[:, kc_ % 4, :],
                                    in_=xT[kc_ * 128:(kc_ + 1) * 128,
                                           HALO_B + sgi_ * 512:HALO_B + (sgi_ + 1) * 512]))(sgi, kc),
                                    writes=[("xres", kc % 4)], dma=True,
                                    tag=f"xr{sgi}_{kc}")
                                def omm(sgi_, kc_):
                                    def f(t):
                                        sl = pso[:, (kc_ % 2) * 512:(kc_ % 2 + 1) * 512]
                                        lt0 = sgi_ * 4
                                        for hc in range(16):
                                            rhs = attnout[:, hc, lt0:lt0 + 4, :]
                                            ins = t.matmul(sl, wot[:, kc_, hc, :],
                                                           rhs, start=(hc == 0),
                                                           stop=(hc == 15))
                                        return ins
                                    return f
                                P.add("t", omm(sgi, kc),
                                      reads=[("wot", kc)] +
                                            [("attnout", hg_, sgi * 4 + i_)
                                             for hg_ in range(4) for i_ in range(4)],
                                      writes=[("pso", kc % 2)], tag=f"o{sgi}_{kc}")
                                def ov(sgi_, kc_):
                                    def f(v):
                                        sl = pso[:, (kc_ % 2) * 512:(kc_ % 2 + 1) * 512]
                                        return v.scalar_tensor_tensor(
                                            v1b[:, kc_, :], sl,
                                            cols[:, COL["bo"] + kc_:COL["bo"] + kc_ + 1],
                                            xres[:, kc_ % 4, :], ALU.add, ALU.add)
                                    return f
                                P.add("v", ov(sgi, kc),
                                      reads=[("pso", kc % 2), ("xres", kc % 4), "cols"],
                                      writes=[("v1b", kc)], tag=f"ov{sgi}_{kc}")
                                P.add("s", (lambda kc_: lambda s: s.activation(
                                    v1c[:, 2 * (kc_ % 2), :], v1b[:, kc_, :], AF.Copy))(kc),
                                    reads=[("v1b", kc)], writes=[("v1c", 2 * (kc % 2))],
                                    tag=f"oc{sgi}_{kc}")
                                P.add("s", (lambda kc_: lambda s: s.activation(
                                    v1c[:, 2 * (kc_ % 2) + 1, :], v1b[:, kc_, :],
                                    AF.Square))(kc),
                                    reads=[("v1b", kc)], writes=[("v1c", 2 * (kc % 2) + 1)],
                                    tag=f"oq{sgi}_{kc}")
                                def osd(sgi_, kc_):
                                    def emit():
                                        P.add("t", lambda t: t.matmul(
                                            psl1[:, :], ones_cb[:, :],
                                            v1c[:, 2 * (kc_ % 2), :],
                                            start=(kc_ == 0), stop=(kc_ == 15)),
                                            reads=[("v1c", 2 * (kc_ % 2)), "ones_cb"],
                                            writes=["psl1"], tag=f"os1_{sgi_}_{kc_}")
                                        P.add("t", lambda t: t.matmul(
                                            psl2[:, :], ones_cb[:, :],
                                            v1c[:, 2 * (kc_ % 2) + 1, :],
                                            start=(kc_ == 0), stop=(kc_ == 15)),
                                            reads=[("v1c", 2 * (kc_ % 2) + 1), "ones_cb"],
                                            writes=["psl2"], tag=f"os2_{sgi_}_{kc_}")
                                    return emit
                                pend4.append(osd(sgi, kc))
                            while pend4:
                                pend4.pop(0)()
                            def lnrow(s):
                                s.activation(mrow[:, :], psl1[:, :], AF.Copy, scale=1.0 / DM)
                                return s.activation(irow[:, :], psl2[:, :], AF.Copy,
                                                    scale=1.0 / DM)
                            P.add("s", lnrow, reads=["psl1", "psl2"],
                                  writes=["mrow", "irow"], tag=f"lnr{sgi}")
                            P.add("v", lambda v: v.scalar_tensor_tensor(
                                srow[:, 0:512], mrow[:, :], 1.0, mrow[:, :],
                                ALU.mult, ALU.mult),
                                reads=["mrow"], writes=["srow"], tag=f"lnva{sgi}")
                            P.add("v", lambda v: v.tensor_sub(irow[:, :], irow[:, :],
                                                              srow[:, 0:512]),
                                  reads=["srow", "irow"], writes=["irow"], tag=f"lnvb{sgi}")
                            P.add("s", lambda s: s.activation(irow[:, :], irow[:, :],
                                                              AF.Sqrt, bias=epst[:, 1:2]),
                                  reads=["irow", "epst"], writes=["irow"], tag=f"lns{sgi}")
                            P.add("v", lambda v: v.reciprocal(irow[:, :], irow[:, :]),
                                  reads=["irow"], writes=["irow"], tag=f"lnca{sgi}")
                            P.add("v", lambda v: v.tensor_mul(mrow[:, :], mrow[:, :],
                                                              irow[:, :]),
                                  reads=["irow", "mrow"], writes=["mrow"], tag=f"lncb{sgi}")
                            P.add("t", lambda t: t.matmul(psr1[:, :], ones_rf[:, :],
                                                          irow[:, :], start=True, stop=True),
                                  reads=["irow", "ones_rf"], writes=["psr1"],
                                  tag=f"lnR1{sgi}")
                            P.add("t", lambda t: t.matmul(psr2[:, :], ones_rf[:, :],
                                                          mrow[:, :], start=True, stop=True),
                                  reads=["mrow", "ones_rf"], writes=["psr2"],
                                  tag=f"lnR2{sgi}")
                            for kc in range(16):
                                P.add("v", (lambda kc_: lambda v: v.tensor_mul(
                                    v1b[:, kc_, :], v1b[:, kc_, :], psr1[:, :]))(kc),
                                    reads=[("v1b", kc), "psr1"], writes=[("v1b", kc)],
                                    tag=f"y1a{sgi}_{kc}")
                                P.add("v", (lambda kc_: lambda v: v.tensor_sub(
                                    v1b[:, kc_, :], v1b[:, kc_, :], psr2[:, :]))(kc),
                                    reads=[("v1b", kc), "psr2"], writes=[("v1b", kc)],
                                    tag=f"y1b{sgi}_{kc}")
                                if kc % 4 == 3:
                                    q = kc // 4
                                    P.add("sy", (lambda sgi_, q_: lambda sy: sy.dma_start(
                                        out=y1_s[4 * q_:4 * q_ + 4, :,
                                                 sgi_ * 512:(sgi_ + 1) * 512]
                                        .transpose([1, 0, 2]),
                                        in_=v1b[:, 4 * q_:4 * q_ + 4, :]))(sgi, q),
                                        reads=[("v1b", 4 * q + j_) for j_ in range(4)],
                                        writes=["y1_s"], dma=True, tag=f"y1sp{sgi}_{q}")

        P.barrier()
        P.barrier()
        p67 = contextlib.ExitStack()
        w1lt = p67.enter_context(nc.sbuf_tensor("w1lt", [128, 64, 512], BF16))
        if "P6" in phases:
            with contextlib.ExitStack() as px:
                PSB = lambda n_, s_, d_: px.enter_context(nc.sbuf_tensor(n_, s_, d_))
                PPS = lambda n_, s_: px.enter_context(nc.psum_tensor(n_, s_, F32))
                w2048 = PSB("w2048", [128, 16, TC], BF16)
                y1bt = PSB("y1bt", [128, 2, TC], BF16)
                suu = PSB("suu", [128, 16], F32)
                sud = PSB("sud", [128, 64], F32)
                hsa = PSB("hsa", [128, 2, 128], BF16)
                gupsl = PSB("gupsl", [128, 2, 16, 128], BF16)
                ggtsl = PSB("ggtsl", [128, 2, 16, 128], BF16)
                silub = PSB("silub", [128, 2, 512], BF16)
                upb = PSB("upb", [128, 2, 512], BF16)
                x1b = PSB("x1b", [128, 2, 512], BF16)
                w1b = PSB("w1b", [128, 2, TC], BF16)
                psu = PPS("psu", [128, 1024])
                psg6 = PPS("psg6", [128, 1024])
                psw = PPS("psw", [128, 1024])

                P.add("v", lambda v: v.tensor_scalar(suu[:, :], cols[:, COL["su_ug"]:
                                                     COL["su_ug"] + 16],
                                                     float(DM) ** -0.5, None, ALU.mult),
                      reads=["cols"], writes=["suu"], tag="suu")
                P.add("v", lambda v: v.tensor_scalar(sud[:, :], cols[:, COL["su_dp"]:
                                                     COL["su_dp"] + 64],
                                                     float(ITR) ** -0.5, None, ALU.mult),
                      reads=["cols"], writes=["sud"], tag="sud")
                for a in range(16):
                    P.add("g", (lambda a_: lambda g: g.dma_start(
                        out=y1bt[:, a_ % 2, :], in_=y1_s[a_]))(a),
                        reads=["y1_s"], writes=[("y1bt", a % 2)], dma=True, tag=f"y1l{a}")
                    P.add("v", (lambda a_: lambda v: v.tensor_scalar(
                        hsa[:, a_ % 2, :], h128b[:, :], suu[:, a_:a_ + 1], None,
                        ALU.mult))(a),
                        reads=["h128b", "suu"], writes=[("hsa", a % 2)], tag=f"hsa{a}")
                    for lb in range(4):
                        P.add("t", (lambda a_, lb_: lambda t: t.matmul(
                            psu[:, (lb_ % 2) * 512:(lb_ % 2 + 1) * 512], hsa[:, a_ % 2, :],
                            y1bt[:, a_ % 2, lb_ * 512:(lb_ + 1) * 512],
                            start=True, stop=True))(a, lb),
                            reads=[("hsa", a % 2), ("y1bt", a % 2)],
                            writes=[("psu", lb % 2)], tag=f"sa{a}_{lb}")
                        P.add("s", (lambda a_, lb_: lambda s: s.activation(
                            w2048[:, a_, lb_ * 512:(lb_ + 1) * 512],
                            psu[:, (lb_ % 2) * 512:(lb_ % 2 + 1) * 512], AF.Copy))(a, lb),
                            reads=[("psu", lb % 2)], writes=[("w2048", a)],
                            tag=f"sae{a}_{lb}")
                if w2k_s is not None:
                    P.add("sy", lambda sy: sy.dma_start(
                        out=w2k_s.transpose([1, 0, 2]), in_=w2048[:, :, :]),
                        reads=[("w2048", a_) for a_ in range(16)],
                        writes=["w2k_s"], dma=True, tag="w2kdump")
                pend6 = []
                for i in range(64):
                    while len(pend6) >= 2:
                        pend6.pop(0)()
                    P.add("g", (lambda i_: lambda g: g.dma_start(
                        out=gupsl[:, i_ % 2, :, :],
                        in_=gup_s[:, :, i_ * 128:(i_ + 1) * 128].transpose([1, 0, 2])))(i),
                        reads=["g_s0"], writes=[("gupsl", i % 2)], dma=True, tag=f"gu{i}")
                    P.add("g", (lambda i_: lambda g: g.dma_start(
                        out=ggtsl[:, i_ % 2, :, :],
                        in_=ggt_s[:, :, i_ * 128:(i_ + 1) * 128].transpose([1, 0, 2])))(i),
                        reads=["g_s1"], writes=[("ggtsl", i % 2)], dma=True, tag=f"gg{i}")
                    P.add("v", (lambda i_: lambda v: v.tensor_scalar(
                        hsa[:, i_ % 2, :], h128b[:, :], sud[:, i_:i_ + 1], None,
                        ALU.mult))(i),
                        reads=["h128b", "sud"], writes=[("hsa", i % 2)], tag=f"hsd{i}")
                    for lb in range(4):
                        def upmm(i_, lb_):
                            def f(t):
                                sl = psu[:, (lb_ % 2) * 512:(lb_ % 2 + 1) * 512]
                                for a in range(16):
                                    ins = t.matmul(sl, gupsl[:, i_ % 2, a, :],
                                                   w2048[:, a, lb_ * 512:(lb_ + 1) * 512],
                                                   start=(a == 0), stop=(a == 15))
                                return ins
                            return f
                        P.add("t", upmm(i, lb),
                              reads=[("gupsl", i % 2)] + [("w2048", a_) for a_ in range(16)],
                              writes=[("psu", lb % 2)], tag=f"up{i}_{lb}")
                        def gtmm(i_, lb_):
                            def f(t):
                                sl = psg6[:, (lb_ % 2) * 512:(lb_ % 2 + 1) * 512]
                                for a in range(16):
                                    ins = t.matmul(sl, ggtsl[:, i_ % 2, a, :],
                                                   w2048[:, a, lb_ * 512:(lb_ + 1) * 512],
                                                   start=(a == 0), stop=(a == 15))
                                return ins
                            return f
                        P.add("t", gtmm(i, lb),
                              reads=[("ggtsl", i % 2)] + [("w2048", a_) for a_ in range(16)],
                              writes=[("psg6", lb % 2)], tag=f"gt{i}_{lb}")
                        if len(pend6) >= 2:
                            pend6.pop(0)()
                        P.add("s", (lambda i_, lb_: lambda s: s.activation(
                            upb[:, lb_ % 2, :], psu[:, (lb_ % 2) * 512:(lb_ % 2 + 1) * 512],
                            AF.Identity, scale=cols[:, COL["wsu"] + i_:COL["wsu"] + i_ + 1]))(i, lb),
                            reads=[("psu", lb % 2), "cols"], writes=[("upb", lb % 2)],
                            tag=f"ue{i}_{lb}")
                        P.add("s", (lambda i_, lb_: lambda s: s.activation(
                            silub[:, lb_ % 2, :], psg6[:, (lb_ % 2) * 512:(lb_ % 2 + 1) * 512],
                            AF.Silu, scale=cols[:, COL["wsg"] + i_:COL["wsg"] + i_ + 1]))(i, lb),
                            reads=[("psg6", lb % 2), "cols"], writes=[("silub", lb % 2)],
                            tag=f"se{i}_{lb}")
                        P.add("v", (lambda lb_: lambda v: v.tensor_mul(
                            x1b[:, lb_ % 2, :], silub[:, lb_ % 2, :], upb[:, lb_ % 2, :]))(lb),
                            reads=[("silub", lb % 2), ("upb", lb % 2)],
                            writes=[("x1b", lb % 2)], tag=f"x1{i}_{lb}")
                        def swd(i_, lb_):
                            def emit():
                                P.add("t", lambda t: t.matmul(
                                    psw[:, (lb_ % 2) * 512:(lb_ % 2 + 1) * 512],
                                    hsa[:, i_ % 2, :], x1b[:, lb_ % 2, :],
                                    start=True, stop=True),
                                    reads=[("hsa", i_ % 2), ("x1b", lb_ % 2)],
                                    writes=[("psw", lb_ % 2)], tag=f"sw{i_}_{lb_}")
                                P.add("s", lambda s: s.activation(
                                    w1b[:, i_ % 2, lb_ * 512:(lb_ + 1) * 512],
                                    psw[:, (lb_ % 2) * 512:(lb_ % 2 + 1) * 512], AF.Copy),
                                    reads=[("psw", lb_ % 2)], writes=[("w1b", i_ % 2, lb_)],
                                    tag=f"we{i_}_{lb_}")
                            return emit
                        if DEFER6:
                            pend6.append(swd(i, lb))
                        else:
                            swd(i, lb)()
                    def w1spd(i_):
                        def emit():
                            P.add("sy", lambda sy: sy.dma_start(
                                out=w1_s[i_], in_=w1b[:, i_ % 2, :]),
                                reads=[("w1b", i_ % 2, lb_) for lb_ in range(4)],
                                writes=["w1_s"], dma=True, tag=f"w1sp{i_}")
                            P.add("g", lambda g: g.dma_start(
                                out=w1lt[:, i_, :], in_=w1b[:, i_ % 2, 0:512]),
                                reads=[("w1b", i_ % 2, 0)], writes=["w1lt"],
                                dma=True, tag=f"w1c{i_}")
                        return emit
                    if DEFER6:
                        pend6.append(w1spd(i))
                    else:
                        w1spd(i)()

                while pend6:
                    pend6.pop(0)()
        P.barrier()
        if "P7" in phases:
            with contextlib.ExitStack() as px:
                PSB = lambda n_, s_, d_: px.enter_context(nc.sbuf_tensor(n_, s_, d_))
                PPS = lambda n_, s_: px.enter_context(nc.psum_tensor(n_, s_, F32))
                gdnsl = PSB("gdnsl", [128, 2, 64, 128], BF16)
                y1sl = PSB("y1sl", [128, 2, 512], F32)
                v2b = PSB("v2b", [128, 16, 512], F32)
                v2c = PSB("v2c", [128, 4, 512], BF16)
                outb = PSB("outb", [128, 4, 2048], F32)
                mrow2 = PSB("mrow2", [1, 512], F32)
                irow2 = PSB("irow2", [1, 512], F32)
                psd = PPS("psd", [128, 1024])
                psl12 = PPS("psl12", [1, 512])
                psl22 = PPS("psl22", [1, 512])
                psr12 = PPS("psr12", [128, 512])
                psr22 = PPS("psr22", [128, 512])
                pstr = PPS("pstr", [128, 1024])

                pend7 = []

                def w1load(lg_):
                    P.add("sy", (lambda l_: lambda sy: sy.dma_start(
                        out=w1lt[:, :, :],
                        in_=w1_s[:, :, l_ * 512:(l_ + 1) * 512]
                        .transpose([1, 0, 2])))(lg_),
                        reads=["w1_s"], writes=["w1lt"], dma=True, tag=f"w1l{lg_}")

                if "P6" not in phases:
                    w1load(0)
                for lg in range(4):
                    for kc in range(16):
                        if len(pend7) >= 2:
                            pend7.pop(0)()
                        P.add("g", (lambda lg_, kc_: lambda g: g.dma_start(
                            out=gdnsl[:, kc_ % 2, :, :],
                            in_=gdn_s[:, :, kc_ * 128:(kc_ + 1) * 128]
                            .transpose([1, 0, 2])))(lg, kc),
                            reads=["g_s2"], writes=[("gdnsl", kc % 2)], dma=True,
                            tag=f"gd{lg}_{kc}")
                        P.add("g", (lambda lg_, kc_: lambda g: g.dma_start(
                            out=y1sl[:, kc_ % 2, :],
                            in_=y1_s[kc_][:, lg_ * 512:(lg_ + 1) * 512]))(lg, kc),
                            reads=["y1_s"], writes=[("y1sl", kc % 2)], dma=True,
                            tag=f"y1r{lg}_{kc}")
                        def dmm(lg_, kc_):
                            def f(t):
                                sl = psd[:, (kc_ % 2) * 512:(kc_ % 2 + 1) * 512]
                                for i in range(64):
                                    ins = t.matmul(sl, gdnsl[:, kc_ % 2, i, :],
                                                   w1lt[:, i, :],
                                                   start=(i == 0), stop=(i == 63))
                                return ins
                            return f
                        P.add("t", dmm(lg, kc), reads=[("gdnsl", kc % 2), "w1lt"],
                              writes=[("psd", kc % 2)], tag=f"d{lg}_{kc}")
                        def dv(lg_, kc_):
                            def f(v):
                                sl = psd[:, (kc_ % 2) * 512:(kc_ % 2 + 1) * 512]
                                return v.scalar_tensor_tensor(
                                    v2b[:, kc_, :], sl,
                                    cols[:, COL["wsd"] + kc_:COL["wsd"] + kc_ + 1],
                                    y1sl[:, kc_ % 2, :], ALU.mult, ALU.add)
                            return f
                        P.add("v", dv(lg, kc), reads=[("psd", kc % 2), ("y1sl", kc % 2),
                                                      "cols"],
                              writes=[("v2b", kc)], tag=f"dv{lg}_{kc}")
                        P.add("s", (lambda kc_: lambda s: s.activation(
                            v2c[:, 2 * (kc_ % 2), :], v2b[:, kc_, :], AF.Copy))(kc),
                            reads=[("v2b", kc)], writes=[("v2c", 2 * (kc % 2))],
                            tag=f"dc{lg}_{kc}")
                        P.add("s", (lambda kc_: lambda s: s.activation(
                            v2c[:, 2 * (kc_ % 2) + 1, :], v2b[:, kc_, :], AF.Square))(kc),
                            reads=[("v2b", kc)], writes=[("v2c", 2 * (kc % 2) + 1)],
                            tag=f"dq{lg}_{kc}")
                        def dsd(lg_, kc_):
                            def emit():
                                P.add("t", lambda t: t.matmul(
                                    psl12[:, :], ones_cb[:, :], v2c[:, 2 * (kc_ % 2), :],
                                    start=(kc_ == 0), stop=(kc_ == 15)),
                                    reads=[("v2c", 2 * (kc_ % 2)), "ones_cb"],
                                    writes=["psl12"], tag=f"ds1_{lg_}_{kc_}")
                                P.add("t", lambda t: t.matmul(
                                    psl22[:, :], ones_cb[:, :],
                                    v2c[:, 2 * (kc_ % 2) + 1, :],
                                    start=(kc_ == 0), stop=(kc_ == 15)),
                                    reads=[("v2c", 2 * (kc_ % 2) + 1), "ones_cb"],
                                    writes=["psl22"], tag=f"ds2_{lg_}_{kc_}")
                            return emit
                        pend7.append(dsd(lg, kc))
                    if lg < 3:
                        w1load(lg + 1)
                    while pend7:
                        pend7.pop(0)()
                    def ln2row(s):
                        s.activation(mrow2[:, :], psl12[:, :], AF.Copy, scale=1.0 / DM)
                        return s.activation(irow2[:, :], psl22[:, :], AF.Copy, scale=1.0 / DM)
                    P.add("s", ln2row, reads=["psl12", "psl22"],
                          writes=["mrow2", "irow2"], tag=f"l2r{lg}")
                    P.add("v", lambda v: v.scalar_tensor_tensor(
                        srow[:, 0:512], mrow2[:, :], 1.0, mrow2[:, :], ALU.mult, ALU.mult),
                        reads=["mrow2"], writes=["srow"], tag=f"l2va{lg}")
                    P.add("v", lambda v: v.tensor_sub(irow2[:, :], irow2[:, :],
                                                      srow[:, 0:512]),
                          reads=["srow", "irow2"], writes=["irow2"], tag=f"l2vb{lg}")
                    P.add("s", lambda s: s.activation(irow2[:, :], irow2[:, :], AF.Sqrt,
                                                      bias=epst[:, 1:2]),
                          reads=["irow2", "epst"], writes=["irow2"], tag=f"l2s{lg}")
                    P.add("v", lambda v: v.reciprocal(irow2[:, :], irow2[:, :]),
                          reads=["irow2"], writes=["irow2"], tag=f"l2ca{lg}")
                    P.add("v", lambda v: v.tensor_mul(mrow2[:, :], mrow2[:, :],
                                                      irow2[:, :]),
                          reads=["irow2", "mrow2"], writes=["mrow2"], tag=f"l2cb{lg}")
                    P.add("t", lambda t: t.matmul(psr12[:, :], ones_rf[:, :], irow2[:, :],
                                                  start=True, stop=True),
                          reads=["irow2", "ones_rf"], writes=["psr12"], tag=f"l2R1{lg}")
                    P.add("t", lambda t: t.matmul(psr22[:, :], ones_rf[:, :], mrow2[:, :],
                                                  start=True, stop=True),
                          reads=["mrow2", "ones_rf"], writes=["psr22"], tag=f"l2R2{lg}")
                    trpend = []
                    for kc in range(16):
                        P.add("v", (lambda kc_: lambda v: v.tensor_mul(
                            v2b[:, kc_, :], v2b[:, kc_, :], psr12[:, :]))(kc),
                            reads=[("v2b", kc), "psr12"], writes=[("v2b", kc)],
                            tag=f"y2a{lg}_{kc}")
                        P.add("v", (lambda kc_: lambda v: v.tensor_sub(
                            v2b[:, kc_, :], v2b[:, kc_, :], psr22[:, :]))(kc),
                            reads=[("v2b", kc), "psr22"], writes=[("v2b", kc)],
                            tag=f"y2b{lg}_{kc}")
                        def trd(lg_, kc_):
                            def emit():
                                for ls_ in range(4):
                                    sl = (kc_ * 4 + ls_) % 2
                                    P.add("t", (lambda ls2, sl2: lambda t: t.transpose(
                                        pstr[:, sl2 * 512:sl2 * 512 + 128],
                                        v2b[:, kc_, ls2 * 128:(ls2 + 1) * 128],
                                        identf[:, :]))(ls_, sl),
                                        reads=[("v2b", kc_), "identf"],
                                        writes=[("pstr", sl)], tag=f"tr{lg_}_{ls_}_{kc_}")
                                    P.add("v", (lambda ls2, sl2: lambda v: v.tensor_copy(
                                        outb[:, ls2, kc_ * 128:(kc_ + 1) * 128],
                                        pstr[:, sl2 * 512:sl2 * 512 + 128]))(ls_, sl),
                                        reads=[("pstr", sl)], writes=[("outb", ls_, kc_)],
                                        tag=f"te{lg_}_{ls_}_{kc_}")
                            return emit
                        trpend.append(trd(lg, kc))
                        if len(trpend) >= 2:
                            trpend.pop(0)()
                    while trpend:
                        trpend.pop(0)()
                    for ls in range(4):
                        P.add("sy", (lambda lg_, ls_: lambda sy: sy.dma_start(
                            out=out_t[lg_ * 512 + ls_ * 128:lg_ * 512 + (ls_ + 1) * 128, :],
                            in_=outb[:, ls_, :]))(lg, ls),
                            reads=[("outb", ls, kc_) for kc_ in range(16)],
                            writes=["out"], dma=True, tag=f"ost{lg}_{ls}")

        p67.close()
        P.add("sy", None, reads=["out"] + (["qt_s", "kt_s", "v_s", "ao_s", "y1_s",
                                           "w1_s", "g_s0", "g_s1", "g_s2"]
                                          if debug else []), tag="final")

        global LASTPROG
        LASTPROG = P
        runner = P.emit(nc, {k: sem_h[k] for k in sem_h})
        with nc.Block("main") as b:
            b.gpsimd(lambda g: runner(g, "g"))
            b.sync(lambda sy: runner(sy, "sy"))
            b.tensor(lambda t: runner(t, "t"))
            b.vector(lambda v: runner(v, "v"))
            b.scalar(lambda s: runner(s, "s"))
    return nc


def prepack(inputs):
    x = np.asarray(inputs["L0_attn_x"])[0]
    xT = np.ascontiguousarray(x.T)
    d = {}
    wq = np.asarray(inputs["L0_attn_wq"])
    wk = np.asarray(inputs["L0_attn_wk"])
    wv = np.asarray(inputs["L0_attn_wv"])
    wo = np.asarray(inputs["L0_attn_wo"])
    d["wq_p"] = np.ascontiguousarray(
        wq.T.reshape(16, 128, 16, 128).transpose(2, 0, 1, 3))
    d["wk_p"] = np.ascontiguousarray(
        wk.T.reshape(16, 128, 16, 128).transpose(2, 0, 1, 3))
    d["wv_p"] = np.ascontiguousarray(wv.T.reshape(16, 128, DM))
    d["wo_p"] = np.ascontiguousarray(
        wo.T.reshape(16, 128, 16, 128).transpose(2, 0, 1, 3))
    wup = np.asarray(inputs["L0_ffn_W_up"])
    wgt = np.asarray(inputs["L0_ffn_W_gate"])
    wdn = np.asarray(inputs["L0_ffn_W_down"])
    d["fup_p"] = np.ascontiguousarray(
        wup.T.reshape(16, 8, 16, ITR).transpose(2, 0, 1, 3).reshape(16, 128, ITR))
    d["fgt_p"] = np.ascontiguousarray(
        wgt.T.reshape(16, 8, 16, ITR).transpose(2, 0, 1, 3).reshape(16, 128, ITR))
    d["fdn_p"] = np.ascontiguousarray(
        wdn.T.reshape(64, 2, 64, DM).transpose(2, 0, 1, 3).reshape(64, 128, DM))
    colpk = np.zeros((128, NCOLS), np.float32)

    def put(name, vec):
        w = vec.reshape(-1, 128).T
        colpk[:, COL[name]:COL[name] + w.shape[1]] = w
    put("bq", np.asarray(inputs["L0_attn_bq"]))
    put("bk", np.asarray(inputs["L0_attn_bk"]))
    put("gq", np.asarray(inputs["L0_attn_gq"]))
    put("gk", np.asarray(inputs["L0_attn_gk"]))
    put("bo", np.asarray(inputs["L0_attn_bo"]))
    put("su_ug", np.asarray(inputs["L0_ffn_SU_ug"]))
    put("su_dp", np.asarray(inputs["L0_ffn_SU_dp"]))
    put("wsu", np.asarray(inputs["L0_ffn_Wscale_ug"])[:ITR])
    put("wsg", np.asarray(inputs["L0_ffn_Wscale_ug"])[ITR:])
    put("wsd", np.asarray(inputs["L0_ffn_Wscale_dp"]))
    d["colpk"] = colpk
    d["bv_row"] = np.ascontiguousarray(np.asarray(inputs["L0_attn_bv"])[None, :])
    d["wmix_in"] = np.ascontiguousarray(np.asarray(inputs["L0_attn_w_mix"]))
    d.update(make_consts())
    maps = []
    for c in range(NCORE):
        idx = (np.arange(EXTP) + c * TC - HALO_B) % L
        m = dict(d)
        m["xT"] = np.ascontiguousarray(xT[:, idx])
        maps.append(m)
    return maps


LAST = None
LASTPROG = None


def kernel(**inputs):
    global LAST
    maps = prepack(inputs)
    nc = build(debug=DEBUG, phases=PHASES)
    res = run_bass_kernel_spmd(nc, maps, core_ids=list(range(NCORE)))
    LAST = res
    out = np.concatenate([res.results[c]["out"] for c in range(NCORE)], axis=0)
    return out.reshape(1, L, DM)
